# version 09
# baseline (speedup 1.0000x reference)
from contextlib import ExitStack

import numpy as np
import concourse.bass as bass
import concourse.mybir as mybir
from concourse.bass_utils import run_bass_kernel_spmd

F32 = mybir.dt.float32
BF16 = mybir.dt.bfloat16
I32 = mybir.dt.int32
AL = mybir.AluOpType
ACT = mybir.ActivationFunctionType
AXX = mybir.AxisListType.X

B, T, D, DS = 4, 4096, 1024, 256
TLOC = 2048
DCH = D // 128
SCH = DS // 128
C1 = 8
NCH1 = TLOC // C1
C2 = 16
NCH2 = NCH1 // C2
SLAB = 256
NSLAB = TLOC // SLAB
NGRP = TLOC // 128
EPS = 1e-5
DEBUG = False


class Sem:
    def __init__(self, handle):
        self.h = handle
        self.v = 0

    def inc(self, inst, n=1):
        inst.then_inc(self.h, n)
        self.v += n
        return inst


def build_nc():
    nc = bass.Bass("TRN2", target_bir_lowering=False,
                   detect_race_conditions=False)

    def din(name, shape, dt=F32):
        return nc.dram_tensor(name, list(shape), dt, kind="ExternalInput").ap()

    xf = din("xf", (D, TLOC))
    xt = din("xt", (TLOC, D))
    pf = din("pf", (D, TLOC))
    pt = din("pt", (TLOC, D))
    pmask = din("pmask", (128, 1))
    w_in = din("w_in", (D, DS))
    w_gate = din("w_gate", (D, D))
    w_out = din("w_out", (D, D))
    ct_d = din("ct", (DS, D))
    dt_d = din("dtw", (DS, D))
    a_n = din("a_n", (DS, DS))
    a_t = din("a_t", (DS, DS))
    bm_d = din("bm", (DS, DS))
    bm_t = din("bm_t", (DS, DS))
    id128 = din("id128", (128, 128))
    ternt = din("ternt", (D, D), I32)
    b_in_d = din("b_in", (128, SCH))
    b_gate_d = din("b_gate", (128, DCH))
    b_out_d = din("b_out", (128, DCH))
    b_ffn_d = din("b_ffn", (128, DCH))
    scal_d = din("scal", (1, 1))

    out_d = nc.dram_tensor("out", [D, TLOC], F32, kind="ExternalOutput").ap()
    dbg = {}
    if DEBUG:
        for name, shape in [("d_u", (128, SCH * TLOC)), ("d_up", (128, SCH * TLOC)),
                            ("d_e1", (128, SCH * NCH1)), ("d_i2", (128, SCH * NCH1)),
                            ("d_ball", (128, SCH * NCH1)),
                            ("d_m", (128, 2 * NGRP)), ("d_r", (128, 2 * NGRP)),
                            ("d_hst", (128, SCH)), ("d_e2", (128, SCH * NCH2)),
                            ("d_b2", (128, SCH * (NCH2 + 1))),
                            ("d_hs", (128, SCH * SLAB)), ("d_g", (128, DCH * SLAB)),
                            ("d_x2", (128, DCH * SLAB)), ("d_p8", (128, 512)),
                            ("d_q3", (128, 512)), ("d_r16", (128, 512)),
                            ("d_xq", (128, DCH * SLAB)),
                            ("d_st0", (128, 4 * NGRP)), ("d_st1", (128, 4 * NGRP)),
                            ("d_rc0", (128, 2 * NGRP)), ("d_rc1", (128, 2 * NGRP)),
                            ("d_sd0", (128, 2 * NGRP)), ("d_sd1", (128, 2 * NGRP))]:
            dbg[name] = nc.dram_tensor(name, list(shape), F32,
                                       kind="ExternalOutput").ap()

    def m_lhsT(tile, kc, mc):
        return tile[:, 256 * kc + 128 * mc: 256 * kc + 128 * mc + 128]

    def m_rhs(tile, kc):
        return tile[:, 256 * kc: 256 * kc + 256]

    def w_lhsT(tile, dout, kc, mc):
        return tile[:, dout * kc + 128 * mc: dout * kc + 128 * mc + 128]

    _es = ExitStack()
    with _es:
        t_wghi = _es.enter_context(nc.sbuf_tensor("t_wghi", [128, DCH * D], BF16))
        t_wglo = _es.enter_context(nc.sbuf_tensor("t_wglo", [128, DCH * D], BF16))
        t_wffn = _es.enter_context(nc.sbuf_tensor("t_wffn", [128, DCH * D], BF16))
        t_onesbf = _es.enter_context(nc.sbuf_tensor("t_onesbf", [128, 1], BF16))
        t_u = _es.enter_context(nc.sbuf_tensor("t_u", [128, SCH * TLOC], F32))
        t_bm = _es.enter_context(nc.sbuf_tensor("t_bm", [128, 512], F32))
        t_id128 = _es.enter_context(nc.sbuf_tensor("t_id128", [128, 128], F32))
        t_p = _es.enter_context(nc.sbuf_tensor("t_p", [128, 8 * 512], F32))
        t_q = _es.enter_context(nc.sbuf_tensor("t_q", [128, 7 * 512], F32))
        t_ball = _es.enter_context(nc.sbuf_tensor("t_ball", [128, SCH * NCH1], F32))
        t_hst = _es.enter_context(nc.sbuf_tensor("t_hst", [128, SCH], F32))
        t_mcols = _es.enter_context(nc.sbuf_tensor("t_mcols", [128, 2 * NGRP], F32))
        t_rcols = _es.enter_context(nc.sbuf_tensor("t_rcols", [128, 2 * NGRP], F32))
        t_scol = _es.enter_context(nc.sbuf_tensor("t_scol", [128, 1], F32))
        t_rsinv = _es.enter_context(nc.sbuf_tensor("t_rsinv", [128, 1], F32))
        t_pmask = _es.enter_context(nc.sbuf_tensor("t_pmask", [128, 1], F32))
        t_cwin = _es.enter_context(nc.sbuf_tensor("t_cwin", [128, SCH], F32))
        t_cwg = _es.enter_context(nc.sbuf_tensor("t_cwg", [128, DCH], F32))
        t_bin = _es.enter_context(nc.sbuf_tensor("t_bin", [128, SCH], F32))
        t_bgate = _es.enter_context(nc.sbuf_tensor("t_bgate", [128, DCH], F32))
        t_bout = _es.enter_context(nc.sbuf_tensor("t_bout", [128, DCH], F32))
        t_bffn = _es.enter_context(nc.sbuf_tensor("t_bffn", [128, DCH], F32))
        t_ones = _es.enter_context(nc.sbuf_tensor("t_ones", [128, 128], F32))
        t_onescol = _es.enter_context(nc.sbuf_tensor("t_onescol", [128, 1], F32))
        t_xf0 = _es.enter_context(nc.sbuf_tensor("t_xf0", [128, DCH * SLAB], F32))
        t_xf1 = _es.enter_context(nc.sbuf_tensor("t_xf1", [128, DCH * SLAB], F32))
        t_mbc0 = _es.enter_context(nc.sbuf_tensor("t_mbc0", [128, SLAB], F32))
        t_rbc0 = _es.enter_context(nc.sbuf_tensor("t_rbc0", [128, SLAB], F32))
        t_mbc1 = _es.enter_context(nc.sbuf_tensor("t_mbc1", [128, SLAB], F32))
        t_rbc1 = _es.enter_context(nc.sbuf_tensor("t_rbc1", [128, SLAB], F32))
        t_bcc = _es.enter_context(nc.sbuf_tensor("t_bcc", [128, 512], F32))
        t_rs2 = _es.enter_context(nc.sbuf_tensor("t_rs2", [128, 256], F32))
        t_mrs2 = _es.enter_context(nc.sbuf_tensor("t_mrs2", [128, 256], F32))
        t_qi = _es.enter_context(nc.sbuf_tensor("t_qi", [128, DCH * SLAB], mybir.dt.int8))
        t_fa = _es.enter_context(nc.sbuf_tensor("t_fa", [128, SLAB], F32))
        t_fb = _es.enter_context(nc.sbuf_tensor("t_fb", [128, SLAB], F32))
        pA = _es.enter_context(nc.psum_tensor("pA", [128, 512], F32))
        pB = _es.enter_context(nc.psum_tensor("pB", [128, 512], F32))
        pHS = _es.enter_context(nc.psum_tensor("pHS", [128, 512], F32))
        pST = _es.enter_context(nc.psum_tensor("pST", [128, 512], F32))
        pBC = _es.enter_context(nc.psum_tensor("pBC", [128, 512], F32))
        pE = _es.enter_context(nc.psum_tensor("pE", [128, 512], F32))
        pEb = _es.enter_context(nc.psum_tensor("pEb", [128, 512], F32))
        pSP = _es.enter_context(nc.psum_tensor("pSP", [128, 512], F32))
        _qS = _es.enter_context(nc.semaphore("qS"))
        _qG = _es.enter_context(nc.semaphore("qG"))
        _sT = _es.enter_context(nc.semaphore("sT"))
        _sV = _es.enter_context(nc.semaphore("sV"))
        _sC = _es.enter_context(nc.semaphore("sC"))
        _sO = _es.enter_context(nc.semaphore("sO"))
        _sD = _es.enter_context(nc.semaphore("sD"))
        qS, qG = Sem(_qS), Sem(_qG)
        sT, sV, sC, sO = Sem(_sT), Sem(_sV), Sem(_sC), Sem(_sO)
        sD = Sem(_sD)

        def vdrain(vector, inst):
            sD.inc(inst, 1)
            vector.wait_ge(sD.h, sD.v)
        xf_ring = [t_xf0, t_xf1]
        mbc_ring = [(t_mbc0, t_rbc0), (t_mbc1, t_rbc1)]

        def P(l):
            return t_p[:, (l - 1) * 512: l * 512]

        def Q(l):
            return t_bm if l == 0 else t_q[:, (l - 1) * 512: l * 512]

        def dma(eng, q, out, in_):
            q.inc(eng.dma_start(out=out, in_=in_), 16)

        with nc.sbuf_tensor("t_uprior", [128, SCH * TLOC], F32) as t_uprior:
            with (nc.sbuf_tensor("t_win", [128, DCH * DS], F32) as t_win,
                  nc.sbuf_tensor("t_tok0", [128, D], F32) as t_tok0,
                  nc.sbuf_tensor("t_tok1", [128, D], F32) as t_tok1,
                  nc.sbuf_tensor("t_tok2", [128, D], F32) as t_tok2,
                  nc.sbuf_tensor("t_tok3", [128, D], F32) as t_tok3,
                  nc.sbuf_tensor("t_toksq", [128, D], F32) as t_toksq,
                  nc.sbuf_tensor("t_st", [128, 4 * NGRP], F32) as t_st,
                  nc.sbuf_tensor("t_sa", [128, 256], F32) as t_sa,
                  nc.sbuf_tensor("t_sb", [128, 256], F32) as t_sb):
                tok_ring = [t_tok0, t_tok1, t_tok2, t_tok3]
                with nc.sbuf_tensor("t_tern", [128, 4 * D], I32) as t_tern, \
                        nc.sbuf_tensor("t_wgtmp", [128, 2 * D], F32) as t_wgtmp, \
                        nc.Block() as blk:
                    S0, G0, T0, V0, C0 = qS.v, qG.v, sT.v, sV.v, sC.v

                    @blk.sync
                    def _(sync):
                        for kc in range(DCH):
                            dma(sync, qS, t_win[:, kc * DS:(kc + 1) * DS],
                                w_in[kc * 128:(kc + 1) * 128, :])
                        dma(sync, qS, t_bm[:, 0:256], bm_d[0:128, :])
                        dma(sync, qS, t_bm[:, 256:512], bm_d[128:256, :])
                        dma(sync, qS, t_id128[:], id128[:])
                        dma(sync, qS, t_bin[:], b_in_d[:])
                        dma(sync, qS, t_bgate[:], b_gate_d[:])
                        dma(sync, qS, t_bout[:], b_out_d[:])
                        dma(sync, qS, t_bffn[:], b_ffn_d[:])
                        dma(sync, qS, t_pmask[:], pmask[:])
                        dma(sync, qS, t_scol[0:1, 0:1], scal_d[:])
                        for kc in range(DCH):
                            if kc >= 2:
                                sync.wait_ge(sV.h, V0 + 1 + kc)
                            dma(sync, qS,
                                t_wgtmp[:, (kc % 2) * D:(kc % 2 + 1) * D],
                                w_gate[kc * 128:(kc + 1) * 128, :])

                    @blk.gpsimd
                    def _(gpsimd):
                        for kc in range(4):
                            dma(gpsimd, qG, t_tern[:, kc * D:(kc + 1) * D],
                                ternt[kc * 128:(kc + 1) * 128, :])
                        gpsimd.wait_ge(sV.h, V0 + 14)
                        for kc in range(4):
                            dma(gpsimd, qG, t_tern[:, kc * D:(kc + 1) * D],
                                ternt[(4 + kc) * 128:(4 + kc + 1) * 128, :])

                    @blk.vector
                    def _(vector):
                        vector.memset(t_ones[:], 1.0 / 1024.0)
                        sV.inc(vector.memset(t_onescol[:], 1.0), 1)
                        vector.memset(t_onesbf[:], 1.0)
                        sV.inc(vector.memset(t_bcc[0:1, 0:128], 1.0), 1)
                        for kc in range(DCH):
                            vector.wait_ge(qS.h, S0 + 16 * (18 + kc))
                            wsrc = t_wgtmp[:, (kc % 2) * D:(kc % 2 + 1) * D]
                            hi = t_wghi[:, kc * D:(kc + 1) * D]
                            inst = vector.tensor_copy(hi, wsrc)
                            vdrain(vector, inst)
                            sV.inc(vector.tensor_tensor(
                                t_wglo[:, kc * D:(kc + 1) * D], wsrc, hi,
                                AL.subtract), 1)
                        vector.wait_ge(qG.h, G0 + 64)
                        for kc in range(4):
                            sV.inc(vector.tensor_scalar_add(
                                t_wffn[:, kc * D:(kc + 1) * D],
                                t_tern[:, kc * D:(kc + 1) * D], -1.0), 1)
                        vector.wait_ge(qG.h, G0 + 128)
                        for kc in range(4):
                            sV.inc(vector.tensor_scalar_add(
                                t_wffn[:, (4 + kc) * D:(5 + kc) * D],
                                t_tern[:, kc * D:(kc + 1) * D], -1.0), 1)
                        vector.wait_ge(sT.h, T0 + 3)
                        inst = vector.tensor_scalar(
                            t_scol[:, 0:1], pSP[:, 2:3], EPS, None, AL.max)
                        sV.inc(inst, 1)
                        vector.wait_ge(sV.h, sV.v)
                        sV.inc(vector.reciprocal(
                            t_rsinv[:, 0:1], t_scol[:, 0:1]), 1)
                        vector.wait_ge(sT.h, T0 + 3)
                        vector.tensor_copy(t_cwin[:, :], pSP[:, 4:4 + SCH])
                        sV.inc(vector.tensor_copy(
                            t_cwg[:, :], pSP[:, 8:8 + DCH]), 1)

                    @blk.tensor
                    def _(tensor):
                        tensor.wait_ge(qS.h, S0 + 16 * 17)
                        tensor.wait_ge(sV.h, V0 + 2)
                        sT.inc(tensor.matmul(pSP[:, 2:3], t_bcc[0:1, 0:128],
                                             t_scol[0:1, 0:1],
                                             start=True, stop=True), 1)
                        for mc in range(SCH):
                            for kc in range(DCH):
                                inst = tensor.matmul(
                                    pSP[:, 4 + mc:5 + mc],
                                    w_lhsT(t_win, DS, kc, mc),
                                    t_onescol[:, 0:1],
                                    start=(kc == 0), stop=(kc == DCH - 1))
                        sT.inc(inst, 1)
                        tensor.wait_ge(sV.h, V0 + 10)
                        for mc in range(DCH):
                            for kc in range(DCH):
                                tensor.matmul(
                                    pSP[:, 8 + mc:9 + mc],
                                    w_lhsT(t_wghi, D, kc, mc),
                                    t_onesbf[:, 0:1],
                                    start=(kc == 0), stop=False)
                            for kc in range(DCH):
                                inst = tensor.matmul(
                                    pSP[:, 8 + mc:9 + mc],
                                    w_lhsT(t_wglo, D, kc, mc),
                                    t_onesbf[:, 0:1],
                                    start=False, stop=(kc == DCH - 1))
                        sT.inc(inst, 1)

                for src in range(2):
                    xtok = pt if src == 0 else xt
                    xfeat = pf if src == 0 else xf
                    u_t = t_uprior if src == 0 else t_u
                    cbase = NGRP if src == 0 else 0

                    with nc.Block() as blk:
                        G0, S0st, C0, V0 = qG.v, qS.v, sC.v, sV.v

                        def _tok_wait(g):
                            return (max(C0, C0 + g - 3),
                                    max(V0, V0 + g - 3))

                        @blk.gpsimd
                        def _(gpsimd):
                            for g in range(0, NGRP, 2):
                                wc, wv = _tok_wait(g)
                                gpsimd.wait_ge(sC.h, wc)
                                gpsimd.wait_ge(sV.h, wv)
                                dma(gpsimd, qG, tok_ring[g % 4][:],
                                    xtok[g * 128:(g + 1) * 128, :])

                        @blk.sync
                        def _(sync):
                            for g in range(1, NGRP, 2):
                                wc, wv = _tok_wait(g)
                                sync.wait_ge(sC.h, wc)
                                sync.wait_ge(sV.h, wv)
                                dma(sync, qS, tok_ring[g % 4][:],
                                    xtok[g * 128:(g + 1) * 128, :])

                        def _tok_dma_wait(eng, g):
                            if g % 2 == 0:
                                eng.wait_ge(qG.h, G0 + 16 * (g // 2 + 1))
                            else:
                                eng.wait_ge(qS.h, S0st + 16 * ((g + 1) // 2))

                        @blk.scalar
                        def _(scalar):
                            scalar.wait_ge(sV.h, V0)
                            for g in range(NGRP):
                                _tok_dma_wait(scalar, g)
                                sC.inc(scalar.activation(
                                    t_toksq[:], tok_ring[g % 4][:], ACT.Square,
                                    accum_out=t_st[:, NGRP + g:NGRP + g + 1]), 1)

                        @blk.vector
                        def _(vector):
                            for g in range(NGRP):
                                _tok_dma_wait(vector, g)
                                sV.inc(vector.reduce_sum(
                                    t_st[:, g:g + 1], tok_ring[g % 4][:],
                                    AXX), 1)

                    with nc.Block() as blk:
                        V0, C0 = sV.v, sC.v
                        G0d = qG.v

                        if DEBUG:
                            @blk.gpsimd
                            def _(gpsimd):
                                gpsimd.wait_ge(sC.h, C0)
                                gpsimd.wait_ge(sV.h, V0)
                                dma(gpsimd, qG, dbg["d_st%d" % src][:, 0:2 * NGRP],
                                    t_st[:, 0:2 * NGRP])

                        @blk.vector
                        def _(vector):
                            vector.wait_ge(sC.h, C0)
                            vector.wait_ge(sT.h, sT.v)
                            vector.wait_ge(qS.h, qS.v)
                            vector.wait_ge(qG.h, qG.v)
                            if DEBUG:
                                vector.wait_ge(qG.h, G0d + 16)
                            mC = t_mcols[:, cbase:cbase + NGRP]
                            tA = t_st[:, 2 * NGRP:3 * NGRP]
                            tB = t_st[:, 3 * NGRP:4 * NGRP]
                            sV.inc(vector.tensor_scalar_mul(
                                mC, t_st[:, 0:NGRP], 1.0 / D), 1)
                            sV.inc(vector.tensor_scalar_mul(
                                tA, t_st[:, NGRP:2 * NGRP], 1.0 / D), 1)
                            vector.wait_ge(sV.h, sV.v)
                            sV.inc(vector.tensor_mul(tB, mC, mC), 1)
                            vector.wait_ge(sV.h, sV.v)
                            sV.inc(vector.scalar_tensor_tensor(
                                tB, tA, EPS, tB, AL.add, AL.subtract), 1)

                        @blk.scalar
                        def _(scalar):
                            scalar.wait_ge(sV.h, V0 + 4)
                            sC.inc(scalar.activation(
                                t_st[:, 0:NGRP], t_st[:, 3 * NGRP:4 * NGRP],
                                ACT.Sqrt), 1)

                    with nc.Block() as blk:
                        C0 = sC.v
                        V0p2 = sV.v
                        G0d2 = qG.v

                        if DEBUG:
                            @blk.gpsimd
                            def _(gpsimd):
                                gpsimd.wait_ge(sC.h, C0)
                                dma(gpsimd, qG, dbg["d_sd%d" % src][:, 0:NGRP],
                                    t_st[:, 0:NGRP])
                                dma(gpsimd, qG, dbg["d_sd%d" % src][:, NGRP:2 * NGRP],
                                    t_st[:, 3 * NGRP:4 * NGRP])

                        @blk.vector
                        def _(vector):
                            vector.wait_ge(sC.h, C0)
                            if DEBUG:
                                vector.wait_ge(qG.h, G0d2 + 32)
                            rC = t_rcols[:, cbase:cbase + NGRP]
                            sd = t_st[:, 0:NGRP]
                            ve = t_st[:, 3 * NGRP:4 * NGRP]
                            w_ = t_st[:, NGRP:2 * NGRP]
                            sV.inc(vector.reciprocal(rC, sd), 1)
                            vector.wait_ge(sV.h, sV.v)
                            sV.inc(vector.tensor_mul(w_, rC, rC), 1)
                            vector.wait_ge(sV.h, sV.v)
                            sV.inc(vector.scalar_tensor_tensor(
                                w_, ve, -0.5, w_, AL.mult, AL.mult), 1)
                            vector.wait_ge(sV.h, sV.v)
                            sV.inc(vector.scalar_tensor_tensor(
                                rC, w_, 1.5, rC, AL.add, AL.mult), 1)
                            vector.wait_ge(sV.h, sV.v)
                            sV.inc(vector.tensor_scalar_mul(rC, rC, -1.0), 1)

                        if DEBUG:
                            @blk.gpsimd
                            def _(gpsimd):
                                gpsimd.wait_ge(sV.h, V0p2 + 5)
                                dma(gpsimd, qG,
                                    dbg["d_rc%d" % src][:, cbase:cbase + NGRP],
                                    t_rcols[:, cbase:cbase + NGRP])

                    for s in range(NSLAB):
                        with nc.Block() as blk:
                            S0, T0, V0, C0 = qS.v, sT.v, sV.v, sC.v

                            @blk.sync
                            def _(sync):
                                sync.wait_ge(sT.h, max(0, T0 - 6))
                                src_ap = xfeat.rearrange(
                                    "(kc p) t -> p kc t", p=128)[
                                    :, :, s * SLAB:(s + 1) * SLAB]
                                dma(sync, qS, xf_ring[s % 2][:], src_ap)

                            @blk.vector
                            def _(vector):
                                vector.wait_ge(sT.h, max(0, T0 - 2))
                                for gg in range(2):
                                    gabs = cbase + 2 * s + gg
                                    vector.tensor_copy(
                                        t_bcc[:, gg * 128:(gg + 1) * 128],
                                        t_mcols[:, gabs:gabs + 1].broadcast_to(
                                            [128, 128]))
                                    inst = vector.tensor_copy(
                                        t_bcc[:, 256 + gg * 128:
                                              256 + (gg + 1) * 128],
                                        t_rcols[:, gabs:gabs + 1].broadcast_to(
                                            [128, 128]))
                                sV.inc(inst, 1)
                                vector.wait_ge(sT.h, T0 + 1)
                                mb, rb = mbc_ring[s % 2]
                                vector.tensor_copy(mb[:], pBC[:, 0:256])
                                sV.inc(vector.tensor_copy(
                                    rb[:], pBC[:, 256:512]), 1)
                                for mc in range(SCH):
                                    pb = pA if mc == 0 else pB
                                    vector.wait_ge(sT.h, T0 + 2 + mc)
                                    sc = t_sa if mc == 0 else t_sb
                                    vector.wait_ge(sC.h, max(0, C0 - 1 + mc))
                                    vector.scalar_tensor_tensor(
                                        sc[:], mb[:], t_cwin[:, mc:mc + 1],
                                        pb[:, 0:SLAB], AL.mult, AL.subtract)
                                    sV.inc(vector.tensor_mul(
                                        sc[:], sc[:], rb[:]), 1)

                            @blk.tensor
                            def _(tensor):
                                tensor.wait_ge(sV.h, V0 + 1)
                                for gg in range(2):
                                    tensor.matmul(
                                        pBC[:, gg * 128:(gg + 1) * 128],
                                        t_bcc[:, gg * 128:(gg + 1) * 128],
                                        t_id128[:], start=True, stop=True)
                                    inst = tensor.matmul(
                                        pBC[:, 256 + gg * 128:
                                            256 + (gg + 1) * 128],
                                        t_bcc[:, 256 + gg * 128:
                                              256 + (gg + 1) * 128],
                                        t_id128[:], start=True, stop=True)
                                sT.inc(inst, 1)
                                tensor.wait_ge(qS.h, S0 + 16)
                                tensor.wait_ge(sV.h, V0 + 2)
                                for mc in range(SCH):
                                    pb = pA if mc == 0 else pB
                                    if mc >= 1:
                                        tensor.wait_ge(sV.h, V0 + 2 + mc)
                                    for kc in range(DCH):
                                        inst = tensor.matmul(
                                            pb[:, 0:SLAB],
                                            w_lhsT(t_win, DS, kc, mc),
                                            xf_ring[s % 2][
                                                :, kc * SLAB:(kc + 1) * SLAB],
                                            start=(kc == 0),
                                            stop=(kc == DCH - 1))
                                    sT.inc(inst, 1)

                            @blk.scalar
                            def _(scalar):
                                for mc in range(SCH):
                                    scalar.wait_ge(sV.h, V0 + 3 + mc)
                                    sc = t_sa if mc == 0 else t_sb
                                    sC.inc(scalar.activation(
                                        u_t[:, TLOC * mc + s * SLAB:
                                            TLOC * mc + (s + 1) * SLAB],
                                        sc[:], ACT.Identity,
                                        bias=t_bin[:, mc:mc + 1]), 1)

            with (nc.sbuf_tensor("t_r", [128, 15 * 512], F32) as t_r,
                  nc.sbuf_tensor("t_rt0", [128, 512], F32) as t_rt0,
                  nc.sbuf_tensor("t_rt1", [128, 512], F32) as t_rt1,
                  nc.sbuf_tensor("t_at", [128, 512], F32) as t_at,
                  nc.sbuf_tensor("t_pt8", [128, 512], F32) as t_pt8,
                  nc.sbuf_tensor("t_bmt", [128, 512], F32) as t_bmt,
                  nc.sbuf_tensor("t_e1", [128, SCH * NCH1], F32) as t_e1,
                  nc.sbuf_tensor("t_i2", [128, SCH * NCH1], F32) as t_i2,
                  nc.sbuf_tensor("t_e2", [128, SCH * NCH2], F32) as t_e2,
                  nc.sbuf_tensor("t_b2", [128, SCH * (NCH2 + 1)], F32) as t_b2):

                def R(i):
                    return P(8) if i == 1 else t_r[:, (i - 2) * 512:(i - 1) * 512]

                jobs = []
                _prod = {}

                def J(dst, dk, lh, lk, rh, rk):
                    dep = max(_prod.get(lk, -1), _prod.get(rk, -1),
                              len(jobs) - 2)
                    jobs.append((dst, lh, rh, dep))
                    _prod[dk] = len(jobs) - 1
                for l in range(1, 8):
                    J(P(l + 1), f"P{l+1}", t_at, None, P(l), f"P{l}")
                    J(Q(l), f"Q{l}", t_bmt, None, P(l), f"P{l}")
                J(t_pt8, "pt8", P(7), "P7", t_at, None)
                J(R(2), "R2", t_pt8, "pt8", R(1), "P8")
                J(t_rt0, "rt0", R(1), "P8", t_pt8, "pt8")
                J(R(3), "R3", t_rt0, "rt0", R(1), "P8")
                J(R(4), "R4", t_rt0, "rt0", R(2), "R2")
                J(t_rt1, "rt1", R(2), "R2", t_rt0, "rt0")
                J(R(5), "R5", t_rt1, "rt1", R(1), "P8")
                J(R(6), "R6", t_rt1, "rt1", R(2), "R2")
                J(R(7), "R7", t_rt1, "rt1", R(3), "R3")
                J(R(8), "R8", t_rt1, "rt1", R(4), "R4")
                J(t_rt0, "rt0", R(4), "R4", t_rt1, "rt1")
                for i in range(1, 9):
                    J(R(8 + i), f"R{8+i}", t_rt0, "rt0",
                      R(i), "P8" if i == 1 else f"R{i}")

                with nc.Block() as blk:
                    S0, T0, V0 = qS.v, sT.v, sV.v

                    @blk.sync
                    def _(sync):
                        dma(sync, qS, P(1)[:, 0:256], a_n[0:128, :])
                        dma(sync, qS, P(1)[:, 256:512], a_n[128:256, :])
                        dma(sync, qS, t_at[:, 0:256], a_t[0:128, :])
                        dma(sync, qS, t_at[:, 256:512], a_t[128:256, :])
                        dma(sync, qS, t_bmt[:, 0:256], bm_t[0:128, :])
                        dma(sync, qS, t_bmt[:, 256:512], bm_t[128:256, :])

                    @blk.tensor
                    def _(tensor):
                        tensor.wait_ge(qS.h, S0 + 96)
                        for idx, (dst, lh, rh, dep) in enumerate(jobs):
                            bank = pE if idx % 2 == 0 else pEb
                            if dep >= 0:
                                tensor.wait_ge(sV.h, V0 + dep + 1)
                            for mc in range(SCH):
                                for kc in range(SCH):
                                    inst = tensor.matmul(
                                        bank[:, mc * 256:(mc + 1) * 256],
                                        m_lhsT(lh, kc, mc), m_rhs(rh, kc),
                                        start=(kc == 0), stop=(kc == SCH - 1))
                            sT.inc(inst, 1)

                    @blk.vector
                    def _(vector):
                        for idx, (dst, lh, rh, dep) in enumerate(jobs):
                            bank = pE if idx % 2 == 0 else pEb
                            vector.wait_ge(sT.h, T0 + idx + 1)
                            sV.inc(vector.tensor_copy(dst[:], bank[:]), 1)

                for src in range(2):
                    u_t = t_uprior if src == 0 else t_u

                    with nc.Block() as blk:
                        T0, V0 = sT.v, sV.v

                        @blk.tensor
                        def _(tensor):
                            tensor.wait_ge(sV.h, V0)
                            tensor.wait_ge(sC.h, sC.v)
                            for nco in range(SCH):
                                for i in range(C1):
                                    for sc in range(SCH):
                                        uv = u_t[:, TLOC * sc:TLOC * (sc + 1)]
                                        uv = uv.rearrange(
                                            "p (K j c) -> p j K c",
                                            K=NCH2, j=C2, c=C1)[:, :, :, C1 - 1 - i]
                                        inst = tensor.matmul(
                                            pE[:, nco * 256:(nco + 1) * 256],
                                            m_lhsT(Q(i), sc, nco), uv,
                                            start=(nco == 0 and i == 0 and sc == 0),
                                            stop=(nco == SCH - 1 and i == C1 - 1
                                                  and sc == SCH - 1))
                            sT.inc(inst, 1)
                            tensor.wait_ge(sV.h, V0 + 2)
                            for j in range(2, C2):
                                for nco in range(SCH):
                                    for kc in range(SCH):
                                        n_el = (C2 - j) * NCH2
                                        ev = t_e1[:, kc * 256:kc * 256 + n_el]
                                        ov = pEb[:, nco * 256 + j * NCH2:
                                                 (nco + 1) * 256]
                                        inst = tensor.matmul(
                                            ov, m_lhsT(R(j - 1), kc, nco), ev,
                                            start=False, stop=(kc == SCH - 1),
                                            skip_group_check=True)
                            sT.inc(inst, 1)
                            tensor.wait_ge(sV.h, V0 + 4)
                            for nco in range(SCH):
                                for kc in range(SCH):
                                    iv = t_i2[:, kc * 256 + (C2 - 1) * NCH2:
                                              kc * 256 + 256]
                                    inst = tensor.matmul(
                                        pSP[:, nco * 256:nco * 256 + NCH2],
                                        m_lhsT(R(1), kc, nco), iv,
                                        start=(kc == 0), stop=(kc == SCH - 1))
                            sT.inc(inst, 1)
                            for K in range(NCH2):
                                tensor.wait_ge(sV.h, V0 + 6 + K)
                                for nco in range(SCH):
                                    for kc in range(SCH):
                                        inst = tensor.matmul(
                                            pSP[:, nco * 256 + 32:nco * 256 + 33],
                                            m_lhsT(R(16), kc, nco),
                                            t_b2[:, kc * (NCH2 + 1) + K:
                                                 kc * (NCH2 + 1) + K + 1],
                                            start=(kc == 0), stop=(kc == SCH - 1))
                                sT.inc(inst, 1)
                            if src == 1:
                                tensor.wait_ge(sV.h, V0 + 6 + NCH2)
                                for j in range(1, C2):
                                    for nco in range(SCH):
                                        for kc in range(SCH):
                                            ov = pE[:, nco * 256 + j * NCH2:
                                                    nco * 256 + (j + 1) * NCH2]
                                            inst = tensor.matmul(
                                                ov, m_lhsT(R(j), kc, nco),
                                                t_b2[:, kc * (NCH2 + 1):
                                                     kc * (NCH2 + 1) + NCH2],
                                                start=False, stop=(kc == SCH - 1),
                                                skip_group_check=True)
                                sT.inc(inst, 1)

                        @blk.vector
                        def _(vector):
                            vector.wait_ge(sT.h, T0 + 1)
                            sV.inc(vector.tensor_copy(t_e1[:], pE[:]), 1)
                            sV.inc(vector.memset(pEb[:], 0.0), 1)
                            vector.wait_ge(sT.h, T0 + 2)
                            vector.tensor_copy(t_i2[:], pEb[:])
                            for nco in range(SCH):
                                ov = t_i2[:, nco * 256 + NCH2:(nco + 1) * 256]
                                ev = t_e1[:, nco * 256:
                                          nco * 256 + (C2 - 1) * NCH2]
                                inst = vector.tensor_tensor(ov, ov, ev, AL.add)
                            sV.inc(inst, 1)
                            sV.inc(vector.memset(pE[:], 0.0), 1)
                            vector.wait_ge(sT.h, T0 + 3)
                            for nco in range(SCH):
                                ev = t_e1[:, nco * 256 + (C2 - 1) * NCH2:
                                          nco * 256 + 256]
                                inst = vector.tensor_tensor(
                                    t_e2[:, nco * NCH2:(nco + 1) * NCH2],
                                    pSP[:, nco * 256:nco * 256 + NCH2],
                                    ev, AL.add)
                            sV.inc(inst, 1)
                            if src == 0:
                                for sc in range(SCH):
                                    inst = vector.memset(
                                        t_b2[:, sc * (NCH2 + 1):
                                             sc * (NCH2 + 1) + 1], 0.0)
                            else:
                                for sc in range(SCH):
                                    inst = vector.tensor_copy(
                                        t_b2[:, sc * (NCH2 + 1):
                                             sc * (NCH2 + 1) + 1],
                                        t_hst[:, sc:sc + 1])
                            sV.inc(inst, 1)
                            for K in range(NCH2):
                                vector.wait_ge(sT.h, T0 + 4 + K)
                                for nco in range(SCH):
                                    inst = vector.tensor_tensor(
                                        t_b2[:, nco * (NCH2 + 1) + K + 1:
                                             nco * (NCH2 + 1) + K + 2],
                                        pSP[:, nco * 256 + 32:nco * 256 + 33],
                                        t_e2[:, nco * NCH2 + K:
                                             nco * NCH2 + K + 1], AL.add)
                                sV.inc(inst, 1)
                            if src == 0:
                                for sc in range(SCH):
                                    inst = vector.tensor_scalar_mul(
                                        t_hst[:, sc:sc + 1],
                                        t_b2[:, sc * (NCH2 + 1) + NCH2:
                                             sc * (NCH2 + 1) + NCH2 + 1],
                                        t_pmask[:, 0:1])
                                sV.inc(inst, 1)
                            else:
                                vector.wait_ge(sT.h, T0 + 4 + NCH2)
                                inst = vector.tensor_tensor(
                                    t_ball[:], pE[:], t_i2[:], AL.add)
                                for nco in range(SCH):
                                    bv = t_ball[:, nco * 256:nco * 256 + NCH2]
                                    inst = vector.tensor_tensor(
                                        bv, bv,
                                        t_b2[:, nco * (NCH2 + 1):
                                             nco * (NCH2 + 1) + NCH2], AL.add)
                                sV.inc(inst, 1)

                if DEBUG:
                    with nc.Block() as blk:
                        @blk.gpsimd
                        def _(gpsimd):
                            gpsimd.wait_ge(sV.h, sV.v)
                            gpsimd.wait_ge(sC.h, sC.v)
                            for name, tile in [("d_u", t_u), ("d_up", t_uprior),
                                               ("d_e1", t_e1), ("d_i2", t_i2),
                                               ("d_ball", t_ball),
                                               ("d_m", t_mcols), ("d_r", t_rcols),
                                               ("d_hst", t_hst), ("d_e2", t_e2),
                                               ("d_b2", t_b2), ("d_p8", P(8)),
                                               ("d_q3", Q(3)), ("d_r16", R(16))]:
                                dma(gpsimd, qG, dbg[name][:], tile[:])

        with (nc.sbuf_tensor("t_wohi", [128, DCH * D], BF16) as t_wohi,
              nc.sbuf_tensor("t_wolo", [128, DCH * D], BF16) as t_wolo,
              nc.sbuf_tensor("t_bf", [128, DCH * SLAB], BF16) as t_bf,
              nc.sbuf_tensor("t_lo", [128, DCH * SLAB], BF16) as t_lo,
              nc.sbuf_tensor("t_ct", [128, SCH * D], F32) as t_ct,
              nc.sbuf_tensor("t_dt", [128, SCH * D], F32) as t_dt,
              nc.sbuf_tensor("t_gy", [128, DCH * SLAB], F32) as t_gy,
              nc.sbuf_tensor("t_hs", [128, SCH * SLAB], F32) as t_hs,
              nc.sbuf_tensor("t_x2", [128, DCH * SLAB], F32) as t_x2,
              nc.sbuf_tensor("t_xq", [128, DCH * SLAB], BF16) as t_xq):

            with nc.Block() as blk:
                S0, V0 = qS.v, sV.v

                @blk.sync
                def _(sync):
                    for kc in range(SCH):
                        dma(sync, qS, t_ct[:, kc * D:(kc + 1) * D],
                            ct_d[kc * 128:(kc + 1) * 128, :])
                    for kc in range(SCH):
                        dma(sync, qS, t_dt[:, kc * D:(kc + 1) * D],
                            dt_d[kc * 128:(kc + 1) * 128, :])
                    for kc in range(DCH):
                        if kc >= 2:
                            sync.wait_ge(sV.h, V0 + kc - 1)
                        dma(sync, qS,
                            t_x2[:, (kc % 2) * D:(kc % 2 + 1) * D],
                            w_out[kc * 128:(kc + 1) * 128, :])

                @blk.vector
                def _(vector):
                    for kc in range(DCH):
                        vector.wait_ge(qS.h, S0 + 16 * (5 + kc))
                        wsrc = t_x2[:, (kc % 2) * D:(kc % 2 + 1) * D]
                        hi = t_wohi[:, kc * D:(kc + 1) * D]
                        inst = vector.tensor_copy(hi, wsrc)
                        vdrain(vector, inst)
                        sV.inc(vector.tensor_tensor(
                            t_wolo[:, kc * D:(kc + 1) * D], wsrc, hi,
                            AL.subtract), 1)

            with nc.Block() as blk:
                P0V, P0T = sV.v, sT.v

                @blk.vector
                def _(vector):
                    vector.wait_ge(sT.h, P0T)
                    for gg in range(2):
                        vector.tensor_copy(
                            t_bcc[:, gg * 128:(gg + 1) * 128],
                            t_mcols[:, gg:gg + 1].broadcast_to([128, 128]))
                        inst = vector.tensor_copy(
                            t_bcc[:, 256 + gg * 128:256 + (gg + 1) * 128],
                            t_rcols[:, gg:gg + 1].broadcast_to([128, 128]))
                    sV.inc(inst, 1)
                    vector.wait_ge(sT.h, P0T + 1)
                    mb, rb = mbc_ring[0]
                    vector.tensor_copy(mb[:], pBC[:, 0:256])
                    sV.inc(vector.tensor_copy(rb[:], pBC[:, 256:512]), 1)

                @blk.tensor
                def _(tensor):
                    tensor.wait_ge(sV.h, P0V + 1)
                    for gg in range(2):
                        tensor.matmul(
                            pBC[:, gg * 128:(gg + 1) * 128],
                            t_bcc[:, gg * 128:(gg + 1) * 128],
                            t_id128[:], start=True, stop=True)
                        inst = tensor.matmul(
                            pBC[:, 256 + gg * 128:256 + (gg + 1) * 128],
                            t_bcc[:, 256 + gg * 128:256 + (gg + 1) * 128],
                            t_id128[:], start=True, stop=True)
                    sT.inc(inst, 1)

            pre_bcast_cnt = sT.v
            pre_mbc_cnt = sV.v
            rot_banks = [pA, pB, pE, pEb, pSP]
            bank_hist = [("sV", sV.v)] * 5
            gbank = [5]
            prev = None
            for s in range(NSLAB + 1):
                with nc.Block() as blk:
                    S0, G0, T0, V0, C0, O0 = qS.v, qG.v, sT.v, sV.v, sC.v, sO.v
                    last = s == NSLAB
                    pv = prev
                    ev = {}

                    t = T0
                    if not last:
                        for mc in range(DCH):
                            t += 1
                            ev[("t_gate", mc)] = t
                        if s + 1 < NSLAB:
                            t += 1
                            ev["t_bcast"] = t
                    if pv is not None:
                        for mc in range(DCH):
                            t += 1
                            ev[("t_ffn", mc)] = t
                    if not last:
                        t += 1
                        ev["t_lag"] = t
                        for mc in range(DCH):
                            t += 1
                            ev[("t_cd", mc)] = t
                        for mc in range(DCH):
                            t += 1
                            ev[("t_wout", mc)] = t
                        t += 1
                        ev["t_stats"] = t
                    ev["t_end"] = t

                    v = V0
                    _gq = list(range(DCH)) if not last else []

                    def _pop(n, v):
                        for _ in range(n):
                            if _gq:
                                v += 1
                                ev[("v_gate", _gq.pop(0))] = v
                        return v
                    if not last:
                        v += 1
                        ev["v_xsplit"] = v
                    if not last and s + 1 < NSLAB:
                        v += 1
                        ev["v_bcc"] = v
                    v = _pop(2, v)
                    if pv is not None:
                        v += 1
                        ev["v_ln2a"] = v
                        v += 1
                        ev["v_ln2b"] = v
                        v += 1
                        ev["v_qgy"] = v
                        v = _pop(DCH, v)
                        v += 1
                        ev["v_quant"] = v
                    v = _pop(DCH, v)
                    if pv is not None:
                        for mc in range(DCH):
                            v += 1
                            ev[("v_fres", mc)] = v
                    if not last:
                        v += 1
                        ev["v_hs"] = v
                        for mc in range(DCH):
                            v += 1
                            ev[("v_y", mc)] = v
                        v += 1
                        ev["v_gysplit"] = v
                        for mc in range(DCH):
                            v += 1
                            ev[("v_x2", mc)] = v
                        if s + 1 < NSLAB:
                            v += 1
                            ev["v_mbc"] = v
                    ev["v_end"] = v

                    c = C0
                    if not last:
                        c += 1
                        ev["c_xcast"] = c
                    if pv is not None:
                        c += 1
                        ev["c_sqrt"] = c
                    if not last:
                        for mc in range(DCH):
                            c += 1
                            ev[("c_sig", mc)] = c
                    if pv is not None:
                        for mc in range(DCH):
                            c += 1
                            ev[("c_fevac", mc)] = c
                    if not last:
                        c += 1
                        ev["c_gycast"] = c
                        for mc in range(DCH):
                            c += 1
                            ev[("c_sq", mc)] = c
                    ev["c_end"] = c

                    banks = {}
                    bank_wait = {}
                    plan = []
                    if not last:
                        plan += [("gate", mc, ("sV", ev[("v_gate", mc)]))
                                 for mc in range(DCH)]
                    if pv is not None:
                        gi = len([p for p in plan])
                        plan = (plan[:DCH if not last else 0]
                                if False else plan)
                    plan2 = []
                    if not last:
                        plan2 += [("gate", mc, ("sC", ev[("c_sig", mc)]))
                                  for mc in range(DCH)]
                    if pv is not None:
                        plan2 += [("ffn", mc, ("sC", ev[("c_fevac", mc)]))
                                  for mc in range(DCH)]
                    if not last:
                        plan2 += [("cd", mc, ("sV", ev[("v_y", mc)]))
                                  for mc in range(DCH)]
                        plan2 += [("wout", mc, ("sV", ev[("v_x2", mc)]))
                                  for mc in range(DCH)]
                    for stage, mc, consumer in plan2:
                        i = gbank[0]
                        banks[(stage, mc)] = rot_banks[i % 5]
                        bank_wait[(stage, mc)] = (
                            bank_hist[i - 5] if i >= 5 else None)
                        bank_hist.append(consumer)
                        gbank[0] = i + 1

                    def bwait(eng, key):
                        w = bank_wait[key]
                        if w is not None:
                            nm, cnt = w
                            eng.wait_ge(sV.h if nm == "sV" else sC.h, cnt)

                    @blk.sync
                    def _(sync):
                        if not last:
                            if pv is not None and pv.get("prev_x2") is not None:
                                sync.wait_ge(sV.h, pv["prev_x2"])
                            src_ap = xf.rearrange(
                                "(kc p) t -> p kc t", p=128)[
                                :, :, s * SLAB:(s + 1) * SLAB]
                            dma(sync, qS, xf_ring[s % 2][:], src_ap)

                    @blk.vector
                    def _(vector):
                        if not last:
                            vector.wait_ge(sC.h, ev["c_xcast"])
                            sV.inc(vector.tensor_tensor(
                                t_lo[:], xf_ring[s % 2][:], t_bf[:],
                                AL.subtract), 1)
                        if not last and s + 1 < NSLAB:
                            vector.wait_ge(sT.h, pv["t_bcast_prev_cnt"]
                                           if s >= 1 else pre_bcast_cnt)
                            for gg in range(2):
                                gabs = 2 * (s + 1) + gg
                                vector.tensor_copy(
                                    t_bcc[:, gg * 128:(gg + 1) * 128],
                                    t_mcols[:, gabs:gabs + 1].broadcast_to(
                                        [128, 128]))
                                inst = vector.tensor_copy(
                                    t_bcc[:, 256 + gg * 128:256 + (gg + 1) * 128],
                                    t_rcols[:, gabs:gabs + 1].broadcast_to(
                                        [128, 128]))
                            sV.inc(inst, 1)

                        gate_q = list(range(DCH)) if not last else []
                        mb, rb = mbc_ring[s % 2]

                        def emit_gate_evac(mc):
                            pb = banks[("gate", mc)]
                            vector.wait_ge(sT.h, ev[("t_gate", mc)])
                            vector.scalar_tensor_tensor(
                                pb[:, SLAB:2 * SLAB], mb[:],
                                t_cwg[:, mc:mc + 1],
                                pb[:, 0:SLAB], AL.mult, AL.subtract)
                            sV.inc(vector.tensor_mul(
                                pb[:, SLAB:2 * SLAB], pb[:, SLAB:2 * SLAB],
                                rb[:]), 1)

                        def g1(n=1):
                            for _i in range(n):
                                if gate_q:
                                    emit_gate_evac(gate_q.pop(0))

                        g1(2)
                        if pv is not None:
                            vector.wait_ge(sT.h, pv["t_stats"])
                            vector.tensor_copy(t_fa[:], pST[:, 0:256])
                            inst = vector.tensor_copy(t_mrs2[:],
                                                      pST[:, 256:512])
                            vdrain(vector, inst)
                            inst = vector.tensor_mul(t_fb[:], t_fa[:], t_fa[:])
                            vdrain(vector, inst)
                            sV.inc(vector.scalar_tensor_tensor(
                                t_fb[:], t_mrs2[:], EPS, t_fb[:],
                                AL.add, AL.subtract), 1)
                            vector.wait_ge(sC.h, ev["c_sqrt"])
                            inst = vector.reciprocal(t_rs2[:], t_mrs2[:])
                            vdrain(vector, inst)
                            inst = vector.tensor_mul(t_mrs2[:], t_rs2[:],
                                                     t_rs2[:])
                            vdrain(vector, inst)
                            inst = vector.scalar_tensor_tensor(
                                t_mrs2[:], t_fb[:], -0.5, t_mrs2[:],
                                AL.mult, AL.mult)
                            vdrain(vector, inst)
                            inst = vector.scalar_tensor_tensor(
                                t_rs2[:], t_mrs2[:], 1.5, t_rs2[:],
                                AL.add, AL.mult)
                            vdrain(vector, inst)
                            inst = vector.tensor_scalar_mul(
                                t_rs2[:], t_rs2[:], t_rsinv[:, 0:1])
                            vdrain(vector, inst)
                            sV.inc(vector.tensor_tensor(
                                t_mrs2[:], t_fa[:], t_rs2[:], AL.mult), 1)
                            rsb = t_rs2[:, 0:SLAB].unsqueeze(1).broadcast_to(
                                [128, DCH, SLAB])
                            mrsb = t_mrs2[:, 0:SLAB].unsqueeze(1).broadcast_to(
                                [128, DCH, SLAB])
                            x2v = t_x2[:].rearrange("p (m c) -> p m c", c=SLAB)
                            gyv = t_gy[:].rearrange("p (m c) -> p m c", c=SLAB)
                            inst = vector.tensor_tensor(gyv, x2v, rsb, AL.mult)
                            vdrain(vector, inst)
                            inst = vector.tensor_tensor(gyv, gyv, mrsb,
                                                        AL.subtract)
                            vdrain(vector, inst)
                            inst = vector.tensor_scalar(t_gy[:], t_gy[:],
                                                        127.0, -127.0,
                                                        AL.min, AL.max)
                            vdrain(vector, inst)
                            inst = vector.tensor_copy(t_qi[:], t_gy[:])
                            vdrain(vector, inst)
                            sV.inc(vector.tensor_tensor(
                                t_xq[:], t_gy[:], t_qi[:],
                                AL.subtract), 1)
                            vector.wait_ge(sD.h, sD.v)
                            g1(DCH)
                            inst = vector.scalar_tensor_tensor(
                                t_qi[:], t_xq[:], 0.5, t_qi[:],
                                AL.is_gt, AL.add)
                            vdrain(vector, inst)
                            inst = vector.scalar_tensor_tensor(
                                t_qi[:], t_xq[:], -0.5, t_qi[:],
                                AL.is_lt, AL.subtract)
                            vdrain(vector, inst)
                            sV.inc(vector.tensor_scalar_mul(
                                t_xq[:], t_qi[:], -1.0), 1)
                        g1(DCH)
                        if pv is not None:
                            for mc in range(DCH):
                                vector.wait_ge(sC.h, ev[("c_fevac", mc)])
                                fb = t_fa if mc % 2 == 0 else t_fb
                                sV.inc(vector.tensor_tensor(
                                    t_x2[:, mc * SLAB:(mc + 1) * SLAB],
                                    fb[:],
                                    t_x2[:, mc * SLAB:(mc + 1) * SLAB],
                                    AL.add), 1)
                        if last:
                            return
                        vector.wait_ge(sT.h, ev["t_lag"])
                        sV.inc(vector.tensor_copy(t_hs[:], pHS[:]), 1)
                        vector.wait_ge(sC.h, ev[("c_sig", DCH - 1)])
                        if DEBUG and s == 1:
                            vector.wait_ge(qG.h, G0 + 32)
                        for mc in range(DCH):
                            pb = banks[("cd", mc)]
                            vector.wait_ge(sT.h, ev[("t_cd", mc)])
                            sV.inc(vector.tensor_tensor(
                                t_gy[:, mc * SLAB:(mc + 1) * SLAB],
                                pb[:, 0:SLAB],
                                t_gy[:, mc * SLAB:(mc + 1) * SLAB],
                                AL.mult), 1)
                        vector.wait_ge(sC.h, ev["c_gycast"])
                        sV.inc(vector.tensor_tensor(
                            t_lo[:], t_gy[:], t_bf[:],
                            AL.subtract), 1)
                        if pv is not None:
                            vector.wait_ge(sO.h, O0 + 16)
                        for mc in range(DCH):
                            pb = banks[("wout", mc)]
                            vector.wait_ge(sT.h, ev[("t_wout", mc)])
                            sV.inc(vector.scalar_tensor_tensor(
                                t_x2[:, mc * SLAB:(mc + 1) * SLAB],
                                pb[:, 0:SLAB], t_bout[:, mc:mc + 1],
                                xf_ring[s % 2][:, mc * SLAB:(mc + 1) * SLAB],
                                AL.add, AL.add), 1)
                        if s + 1 < NSLAB:
                            vector.wait_ge(sT.h, ev["t_bcast"])
                            mb, rb = mbc_ring[(s + 1) % 2]
                            vector.tensor_copy(mb[:], pBC[:, 0:256])
                            sV.inc(vector.tensor_copy(rb[:], pBC[:, 256:512]), 1)

                    @blk.scalar
                    def _(scalar):
                        if not last:
                            if pv is not None:
                                scalar.wait_ge(sT.h, pv[("t_wout", DCH - 1)])
                            scalar.wait_ge(qS.h, S0 + 16)
                            sC.inc(scalar.activation(
                                t_bf[:], xf_ring[s % 2][:], ACT.Copy), 1)
                        if pv is not None:
                            scalar.wait_ge(sV.h, ev["v_ln2a"])
                            sC.inc(scalar.activation(t_mrs2[:], t_fb[:],
                                                     ACT.Sqrt), 1)
                        if not last:
                            if pv is not None:
                                scalar.wait_ge(sV.h, ev["v_qgy"])
                            for mc in range(DCH):
                                scalar.wait_ge(sV.h, ev[("v_gate", mc)])
                                pb = banks[("gate", mc)]
                                sC.inc(scalar.activation(
                                    t_gy[:, mc * SLAB:(mc + 1) * SLAB],
                                    pb[:, SLAB:2 * SLAB],
                                    ACT.Sigmoid, bias=t_bgate[:, mc:mc + 1]), 1)
                        if pv is not None:
                            for mc in range(DCH):
                                pb = banks[("ffn", mc)]
                                fb = t_fa if mc % 2 == 0 else t_fb
                                scalar.wait_ge(sT.h, ev[("t_ffn", mc)])
                                if mc >= 2:
                                    scalar.wait_ge(sV.h, ev[("v_fres", mc - 2)])
                                sC.inc(scalar.activation(
                                    fb[:],
                                    pb[:, 0:SLAB], ACT.Identity,
                                    bias=t_bffn[:, mc:mc + 1],
                                    scale=t_scol[:, 0:1]), 1)
                        if last:
                            return
                        scalar.wait_ge(sV.h, ev[("v_y", DCH - 1)])
                        sC.inc(scalar.activation(
                            t_bf[:], t_gy[:], ACT.Copy), 1)
                        scalar.wait_ge(sT.h, ev[("t_wout", DCH - 1)])
                        for mc in range(DCH):
                            scalar.wait_ge(sV.h, ev[("v_x2", mc)])
                            sC.inc(scalar.activation(
                                t_gy[:, mc * SLAB:(mc + 1) * SLAB],
                                t_x2[:, mc * SLAB:(mc + 1) * SLAB],
                                ACT.Square), 1)

                    @blk.tensor
                    def _(tensor):
                        if not last:
                            tensor.wait_ge(qS.h, S0 + 16)
                            tensor.wait_ge(sV.h, ev["v_xsplit"])
                            for mc in range(DCH):
                                pb = banks[("gate", mc)]
                                bwait(tensor, ("gate", mc))
                                for kc in range(DCH):
                                    tensor.matmul(
                                        pb[:, 0:SLAB],
                                        w_lhsT(t_wghi, D, kc, mc),
                                        t_bf[:, kc * SLAB:(kc + 1) * SLAB],
                                        start=(kc == 0), stop=False)
                                for kc in range(DCH):
                                    tensor.matmul(
                                        pb[:, 0:SLAB],
                                        w_lhsT(t_wglo, D, kc, mc),
                                        t_bf[:, kc * SLAB:(kc + 1) * SLAB],
                                        start=False, stop=False)
                                for kc in range(DCH):
                                    inst = tensor.matmul(
                                        pb[:, 0:SLAB],
                                        w_lhsT(t_wghi, D, kc, mc),
                                        t_lo[:, kc * SLAB:(kc + 1) * SLAB],
                                        start=False, stop=(kc == DCH - 1))
                                sT.inc(inst, 1)
                        if not last and s + 1 < NSLAB:
                            tensor.wait_ge(sV.h, ev["v_bcc"])
                            tensor.wait_ge(sV.h, pv["v_mbc"]
                                           if pv is not None else pre_mbc_cnt)
                            for gg in range(2):
                                tensor.matmul(
                                    pBC[:, gg * 128:(gg + 1) * 128],
                                    t_bcc[:, gg * 128:(gg + 1) * 128],
                                    t_id128[:], start=True, stop=True)
                                inst = tensor.matmul(
                                    pBC[:, 256 + gg * 128:256 + (gg + 1) * 128],
                                    t_bcc[:, 256 + gg * 128:
                                          256 + (gg + 1) * 128],
                                    t_id128[:], start=True, stop=True)
                            sT.inc(inst, 1)
                        if pv is not None:
                            tensor.wait_ge(sV.h, ev["v_quant"])
                            for mc in range(DCH):
                                pb = banks[("ffn", mc)]
                                bwait(tensor, ("ffn", mc))
                                for kc in range(DCH):
                                    inst = tensor.matmul(
                                        pb[:, 0:SLAB],
                                        w_lhsT(t_wffn, D, kc, mc),
                                        t_xq[:, kc * SLAB:(kc + 1) * SLAB],
                                        start=(kc == 0), stop=(kc == DCH - 1))
                                sT.inc(inst, 1)
                        if last:
                            return
                        if pv is not None:
                            tensor.wait_ge(sV.h, pv["v_hs"])
                        nch_s = SLAB // C1
                        for nco in range(SCH):
                            for l in range(C1):
                                for sc in range(SCH):
                                    uv = t_u[:, TLOC * sc + s * SLAB:
                                             TLOC * sc + (s + 1) * SLAB]
                                    uv = uv.rearrange("p (ch c) -> p c ch",
                                                      c=C1)[:, 0:C1 - l, :]
                                    ov = pHS[:, nco * 256 + l * nch_s:
                                             (nco + 1) * 256]
                                    st = (nco == 0 and l == 0 and sc == 0)
                                    inst = tensor.matmul(
                                        ov, m_lhsT(Q(l), sc, nco), uv,
                                        start=st, stop=False)
                            for j in range(C1):
                                for sc in range(SCH):
                                    ov = pHS[:, nco * 256 + j * nch_s:
                                             nco * 256 + (j + 1) * nch_s]
                                    bv = t_ball[:, sc * 256:(sc + 1) * 256]
                                    bv = bv.rearrange("p (j K) -> p K j",
                                                      K=NCH2)[:, 2 * s:2 * s + 2, :]
                                    inst = tensor.matmul(
                                        ov, m_lhsT(P(j + 1), sc, nco), bv,
                                        start=False,
                                        stop=(nco == SCH - 1 and j == C1 - 1
                                              and sc == SCH - 1))
                        sT.inc(inst, 1)
                        tensor.wait_ge(sV.h, ev["v_hs"])
                        for mc in range(DCH):
                            pb = banks[("cd", mc)]
                            bwait(tensor, ("cd", mc))
                            for sc in range(SCH):
                                hv = t_hs[:, sc * SLAB:(sc + 1) * SLAB]
                                hv = hv.rearrange("p (c ch) -> p ch c", c=C1)
                                tensor.matmul(
                                    pb[:, 0:SLAB], w_lhsT(t_ct, D, sc, mc),
                                    hv, start=(sc == 0), stop=False)
                            for sc in range(SCH):
                                inst = tensor.matmul(
                                    pb[:, 0:SLAB], w_lhsT(t_dt, D, sc, mc),
                                    t_u[:, TLOC * sc + s * SLAB:
                                        TLOC * sc + (s + 1) * SLAB],
                                    start=False, stop=(sc == SCH - 1))
                            sT.inc(inst, 1)
                        tensor.wait_ge(sV.h, ev["v_gysplit"])
                        for mc in range(DCH):
                            pb = banks[("wout", mc)]
                            bwait(tensor, ("wout", mc))
                            for kc in range(DCH):
                                tensor.matmul(
                                    pb[:, 0:SLAB], w_lhsT(t_wohi, D, kc, mc),
                                    t_bf[:, kc * SLAB:(kc + 1) * SLAB],
                                    start=(kc == 0), stop=False)
                            for kc in range(DCH):
                                tensor.matmul(
                                    pb[:, 0:SLAB], w_lhsT(t_wolo, D, kc, mc),
                                    t_bf[:, kc * SLAB:(kc + 1) * SLAB],
                                    start=False, stop=False)
                            for kc in range(DCH):
                                inst = tensor.matmul(
                                    pb[:, 0:SLAB], w_lhsT(t_wohi, D, kc, mc),
                                    t_lo[:, kc * SLAB:(kc + 1) * SLAB],
                                    start=False, stop=(kc == DCH - 1))
                            sT.inc(inst, 1)
                        if pv is not None:
                            tensor.wait_ge(sV.h, ev["v_ln2a"])
                        tensor.wait_ge(sV.h, ev[("v_x2", DCH - 1)])
                        tensor.wait_ge(sC.h, ev[("c_sq", DCH - 1)])
                        for kc in range(DCH):
                            tensor.matmul(
                                pST[:, 0:256], t_ones[:],
                                t_x2[:, kc * SLAB:(kc + 1) * SLAB],
                                start=(kc == 0), stop=(kc == DCH - 1))
                        for kc in range(DCH):
                            inst = tensor.matmul(
                                pST[:, 256:512], t_ones[:],
                                t_gy[:, kc * SLAB:(kc + 1) * SLAB],
                                start=(kc == 0), stop=(kc == DCH - 1))
                        sT.inc(inst, 1)

                    @blk.gpsimd
                    def _(gpsimd):
                        if pv is not None:
                            gpsimd.wait_ge(sV.h, ev[("v_fres", DCH - 1)])
                            dst = out_d.rearrange("(kc p) t -> p kc t", p=128)[
                                :, :, (s - 1) * SLAB:s * SLAB]
                            dma(gpsimd, sO, dst, t_x2[:])
                        if DEBUG and s == 1:
                            gpsimd.wait_ge(sC.h, ev[("c_sig", DCH - 1)])
                            dma(gpsimd, qG, dbg["d_g"][:], t_gy[:])
                            gpsimd.wait_ge(sV.h, ev["v_hs"])
                            dma(gpsimd, qG, dbg["d_hs"][:], t_hs[:])
                            gpsimd.wait_ge(sV.h, ev[("v_x2", DCH - 1)])
                            dma(gpsimd, qG, dbg["d_x2"][:], t_x2[:])

                    assert sT.v == ev["t_end"], (s, sT.v, ev["t_end"])
                    assert sV.v == ev["v_end"], (s, sV.v, ev["v_end"])
                    assert sC.v == ev["c_end"], (s, sC.v, ev["c_end"])

                    nxt = dict(ev)
                    nxt["t_bcast_prev_cnt"] = ev.get("t_bcast", sT.v)
                    nxt["prev_x2"] = (pv or {}).get("x2_cnt_for_ring", None)
                    nxt["x2_cnt_for_ring"] = ev.get(("v_x2", DCH - 1))
                    prev = nxt

            with nc.Block() as blk:
                @blk.gpsimd
                def _(gpsimd):
                    gpsimd.wait_ge(sO.h, sO.v)

    return nc


_NC_CACHE = None


def kernel(**inputs):
    global _NC_CACHE
    x = np.asarray(inputs["L0_attn_x"], dtype=np.float32)
    w_in = np.ascontiguousarray(inputs["L0_attn_W_in"], dtype=np.float32)
    b_in = np.asarray(inputs["L0_attn_b_in"], dtype=np.float32)
    w_gate = np.ascontiguousarray(inputs["L0_attn_W_gate"], dtype=np.float32)
    b_gate = np.asarray(inputs["L0_attn_b_gate"], dtype=np.float32)
    w_out = np.ascontiguousarray(inputs["L0_attn_W_out"], dtype=np.float32)
    b_out = np.asarray(inputs["L0_attn_b_out"], dtype=np.float32)
    a_m = np.asarray(inputs["L0_attn_A"], dtype=np.float32)
    bm_m = np.asarray(inputs["L0_attn_Bm"], dtype=np.float32)
    c_m = np.asarray(inputs["L0_attn_C"], dtype=np.float32)
    d_m = np.asarray(inputs["L0_attn_D"], dtype=np.float32)
    tern = np.asarray(inputs["L0_ffn_ternary_weight"], dtype=np.int32)
    b_ffn = np.asarray(inputs["L0_ffn_bias"], dtype=np.float32)
    act_scale = np.asarray(inputs["L0_ffn_act_scale"], dtype=np.float32)

    if _NC_CACHE is None:
        _NC_CACHE = build_nc()
    nc = _NC_CACHE

    shared = {
        "w_in": w_in,
        "w_gate": w_gate,
        "w_out": w_out,
        "ct": np.ascontiguousarray(c_m.T),
        "dtw": np.ascontiguousarray(d_m.T),
        "a_n": np.ascontiguousarray(a_m),
        "a_t": np.ascontiguousarray(a_m.T),
        "bm": np.ascontiguousarray(bm_m),
        "bm_t": np.ascontiguousarray(bm_m.T),
        "id128": np.eye(128, dtype=np.float32),
        "ternt": np.ascontiguousarray(tern.T),
        "b_in": np.ascontiguousarray(b_in.reshape(SCH, 128).T),
        "b_gate": np.ascontiguousarray(b_gate.reshape(DCH, 128).T),
        "b_out": np.ascontiguousarray(b_out.reshape(DCH, 128).T),
        "b_ffn": np.ascontiguousarray(b_ffn.reshape(DCH, 128).T),
        "scal": act_scale.reshape(1, 1),
    }
    zeros_f = np.zeros((D, TLOC), np.float32)
    zeros_t = np.zeros((TLOC, D), np.float32)
    in_maps = []
    for core in range(8):
        b = core // 2
        half = core % 2
        xs = x[b, half * TLOC:(half + 1) * TLOC, :]
        m = dict(shared)
        m["xf"] = np.ascontiguousarray(xs.T)
        m["xt"] = np.ascontiguousarray(xs)
        if half == 1:
            ps = x[b, 0:TLOC, :]
            m["pf"] = np.ascontiguousarray(ps.T)
            m["pt"] = np.ascontiguousarray(ps)
            m["pmask"] = np.ones((128, 1), np.float32)
        else:
            m["pf"] = zeros_f
            m["pt"] = zeros_t
            m["pmask"] = np.zeros((128, 1), np.float32)
        in_maps.append(m)

    res = run_bass_kernel_spmd(nc, in_maps, core_ids=list(range(8)))
    out = np.empty((B, T, D), np.float32)
    for core in range(8):
        b = core // 2
        half = core % 2
        out[b, half * TLOC:(half + 1) * TLOC, :] = res.results[core]["out"].T
    if DEBUG:
        kernel.last_results = res
    return out


# version 10
# speedup vs baseline: 1.0122x; 1.0122x over previous
from contextlib import ExitStack

import numpy as np
import concourse.bass as bass
import concourse.mybir as mybir
from concourse.bass_utils import run_bass_kernel_spmd

F32 = mybir.dt.float32
BF16 = mybir.dt.bfloat16
I32 = mybir.dt.int32
AL = mybir.AluOpType
ACT = mybir.ActivationFunctionType
AXX = mybir.AxisListType.X

B, T, D, DS = 4, 4096, 1024, 256
TLOC = 2048
DCH = D // 128
SCH = DS // 128
C1 = 8
NCH1 = TLOC // C1
C2 = 16
NCH2 = NCH1 // C2
SLAB = 256
NSLAB = TLOC // SLAB
NGRP = TLOC // 128
EPS = 1e-5
DEBUG = False


class Sem:
    def __init__(self, handle):
        self.h = handle
        self.v = 0

    def inc(self, inst, n=1):
        inst.then_inc(self.h, n)
        self.v += n
        return inst


def build_nc():
    nc = bass.Bass("TRN2", target_bir_lowering=False,
                   detect_race_conditions=False)

    def din(name, shape, dt=F32):
        return nc.dram_tensor(name, list(shape), dt, kind="ExternalInput").ap()

    xf = din("xf", (D, TLOC))
    xt = din("xt", (TLOC, D))
    pf = din("pf", (D, TLOC))
    pt = din("pt", (TLOC, D))
    pmask = din("pmask", (128, 1))
    w_in = din("w_in", (D, DS))
    w_gate = din("w_gate", (D, D))
    w_out = din("w_out", (D, D))
    ct_d = din("ct", (DS, D))
    dt_d = din("dtw", (DS, D))
    a_n = din("a_n", (DS, DS))
    a_t = din("a_t", (DS, DS))
    bm_d = din("bm", (DS, DS))
    bm_t = din("bm_t", (DS, DS))
    id128 = din("id128", (128, 128))
    ternt = din("ternt", (D, D), I32)
    b_in_d = din("b_in", (128, SCH))
    b_gate_d = din("b_gate", (128, DCH))
    b_out_d = din("b_out", (128, DCH))
    b_ffn_d = din("b_ffn", (128, DCH))
    scal_d = din("scal", (1, 1))

    out_d = nc.dram_tensor("out", [D, TLOC], F32, kind="ExternalOutput").ap()
    dbg = {}
    if DEBUG:
        for name, shape in [("d_u", (128, SCH * TLOC)), ("d_up", (128, SCH * TLOC)),
                            ("d_e1", (128, SCH * NCH1)), ("d_i2", (128, SCH * NCH1)),
                            ("d_ball", (128, SCH * NCH1)),
                            ("d_m", (128, 2 * NGRP)), ("d_r", (128, 2 * NGRP)),
                            ("d_hst", (128, SCH)), ("d_e2", (128, SCH * NCH2)),
                            ("d_b2", (128, SCH * (NCH2 + 1))),
                            ("d_hs", (128, SCH * SLAB)), ("d_g", (128, DCH * SLAB)),
                            ("d_x2", (128, DCH * SLAB)), ("d_p8", (128, 512)),
                            ("d_q3", (128, 512)), ("d_r16", (128, 512)),
                            ("d_xq", (128, DCH * SLAB)),
                            ("d_st0", (128, 4 * NGRP)), ("d_st1", (128, 4 * NGRP)),
                            ("d_rc0", (128, 2 * NGRP)), ("d_rc1", (128, 2 * NGRP)),
                            ("d_sd0", (128, 2 * NGRP)), ("d_sd1", (128, 2 * NGRP))]:
            dbg[name] = nc.dram_tensor(name, list(shape), F32,
                                       kind="ExternalOutput").ap()

    def m_lhsT(tile, kc, mc):
        return tile[:, 256 * kc + 128 * mc: 256 * kc + 128 * mc + 128]

    def m_rhs(tile, kc):
        return tile[:, 256 * kc: 256 * kc + 256]

    def w_lhsT(tile, dout, kc, mc):
        return tile[:, dout * kc + 128 * mc: dout * kc + 128 * mc + 128]

    _es = ExitStack()
    with _es:
        t_wghi = _es.enter_context(nc.sbuf_tensor("t_wghi", [128, DCH * D], BF16))
        t_wglo = _es.enter_context(nc.sbuf_tensor("t_wglo", [128, DCH * D], BF16))
        t_wffn = _es.enter_context(nc.sbuf_tensor("t_wffn", [128, DCH * D], BF16))
        t_onesbf = _es.enter_context(nc.sbuf_tensor("t_onesbf", [128, 1], BF16))
        t_u = _es.enter_context(nc.sbuf_tensor("t_u", [128, SCH * TLOC], F32))
        t_bm = _es.enter_context(nc.sbuf_tensor("t_bm", [128, 512], F32))
        t_id128 = _es.enter_context(nc.sbuf_tensor("t_id128", [128, 128], F32))
        t_p = _es.enter_context(nc.sbuf_tensor("t_p", [128, 8 * 512], F32))
        t_q = _es.enter_context(nc.sbuf_tensor("t_q", [128, 7 * 512], F32))
        t_ball = _es.enter_context(nc.sbuf_tensor("t_ball", [128, SCH * NCH1], F32))
        t_hst = _es.enter_context(nc.sbuf_tensor("t_hst", [128, SCH], F32))
        t_mcols = _es.enter_context(nc.sbuf_tensor("t_mcols", [128, 2 * NGRP], F32))
        t_rcols = _es.enter_context(nc.sbuf_tensor("t_rcols", [128, 2 * NGRP], F32))
        t_scol = _es.enter_context(nc.sbuf_tensor("t_scol", [128, 1], F32))
        t_rsinv = _es.enter_context(nc.sbuf_tensor("t_rsinv", [128, 1], F32))
        t_pmask = _es.enter_context(nc.sbuf_tensor("t_pmask", [128, 1], F32))
        t_cwin = _es.enter_context(nc.sbuf_tensor("t_cwin", [128, SCH], F32))
        t_cwg = _es.enter_context(nc.sbuf_tensor("t_cwg", [128, DCH], F32))
        t_bin = _es.enter_context(nc.sbuf_tensor("t_bin", [128, SCH], F32))
        t_bgate = _es.enter_context(nc.sbuf_tensor("t_bgate", [128, DCH], F32))
        t_bout = _es.enter_context(nc.sbuf_tensor("t_bout", [128, DCH], F32))
        t_bffn = _es.enter_context(nc.sbuf_tensor("t_bffn", [128, DCH], F32))
        t_ones = _es.enter_context(nc.sbuf_tensor("t_ones", [128, 128], F32))
        t_onescol = _es.enter_context(nc.sbuf_tensor("t_onescol", [128, 1], F32))
        t_xf0 = _es.enter_context(nc.sbuf_tensor("t_xf0", [128, DCH * SLAB], F32))
        t_xf1 = _es.enter_context(nc.sbuf_tensor("t_xf1", [128, DCH * SLAB], F32))
        t_mbc0 = _es.enter_context(nc.sbuf_tensor("t_mbc0", [128, SLAB], F32))
        t_rbc0 = _es.enter_context(nc.sbuf_tensor("t_rbc0", [128, SLAB], F32))
        t_mbc1 = _es.enter_context(nc.sbuf_tensor("t_mbc1", [128, SLAB], F32))
        t_rbc1 = _es.enter_context(nc.sbuf_tensor("t_rbc1", [128, SLAB], F32))
        t_bcc = _es.enter_context(nc.sbuf_tensor("t_bcc", [128, 512], F32))
        t_rs2 = _es.enter_context(nc.sbuf_tensor("t_rs2", [128, 256], F32))
        t_mrs2 = _es.enter_context(nc.sbuf_tensor("t_mrs2", [128, 256], F32))
        t_qi = _es.enter_context(nc.sbuf_tensor("t_qi", [128, DCH * SLAB], mybir.dt.int8))
        t_fa = _es.enter_context(nc.sbuf_tensor("t_fa", [128, SLAB], F32))
        t_fb = _es.enter_context(nc.sbuf_tensor("t_fb", [128, SLAB], F32))
        pA = _es.enter_context(nc.psum_tensor("pA", [128, 512], F32))
        pB = _es.enter_context(nc.psum_tensor("pB", [128, 512], F32))
        pHS = _es.enter_context(nc.psum_tensor("pHS", [128, 512], F32))
        pST = _es.enter_context(nc.psum_tensor("pST", [128, 512], F32))
        pBC = _es.enter_context(nc.psum_tensor("pBC", [128, 512], F32))
        pE = _es.enter_context(nc.psum_tensor("pE", [128, 512], F32))
        pEb = _es.enter_context(nc.psum_tensor("pEb", [128, 512], F32))
        pSP = _es.enter_context(nc.psum_tensor("pSP", [128, 512], F32))
        _qS = _es.enter_context(nc.semaphore("qS"))
        _qG = _es.enter_context(nc.semaphore("qG"))
        _sT = _es.enter_context(nc.semaphore("sT"))
        _sV = _es.enter_context(nc.semaphore("sV"))
        _sC = _es.enter_context(nc.semaphore("sC"))
        _sO = _es.enter_context(nc.semaphore("sO"))
        _sD = _es.enter_context(nc.semaphore("sD"))
        qS, qG = Sem(_qS), Sem(_qG)
        sT, sV, sC, sO = Sem(_sT), Sem(_sV), Sem(_sC), Sem(_sO)
        sD = Sem(_sD)

        def vdrain(vector, inst):
            sD.inc(inst, 1)
            vector.wait_ge(sD.h, sD.v)
        xf_ring = [t_xf0, t_xf1]
        mbc_ring = [(t_mbc0, t_rbc0), (t_mbc1, t_rbc1)]

        def P(l):
            return t_p[:, (l - 1) * 512: l * 512]

        def Q(l):
            return t_bm if l == 0 else t_q[:, (l - 1) * 512: l * 512]

        def dma(eng, q, out, in_):
            q.inc(eng.dma_start(out=out, in_=in_), 16)

        with nc.sbuf_tensor("t_uprior", [128, SCH * TLOC], F32) as t_uprior:
            with (nc.sbuf_tensor("t_win", [128, DCH * DS], F32) as t_win,
                  nc.sbuf_tensor("t_tok0", [128, D], F32) as t_tok0,
                  nc.sbuf_tensor("t_tok1", [128, D], F32) as t_tok1,
                  nc.sbuf_tensor("t_tok2", [128, D], F32) as t_tok2,
                  nc.sbuf_tensor("t_tok3", [128, D], F32) as t_tok3,
                  nc.sbuf_tensor("t_toksq", [128, D], F32) as t_toksq,
                  nc.sbuf_tensor("t_st", [128, 4 * NGRP], F32) as t_st,
                  nc.sbuf_tensor("t_sa", [128, 256], F32) as t_sa,
                  nc.sbuf_tensor("t_sb", [128, 256], F32) as t_sb):
                tok_ring = [t_tok0, t_tok1, t_tok2, t_tok3]
                with nc.sbuf_tensor("t_tern", [128, 4 * D], I32) as t_tern, \
                        nc.sbuf_tensor("t_wgtmp", [128, 2 * D], F32) as t_wgtmp, \
                        nc.Block() as blk:
                    S0, G0, T0, V0, C0 = qS.v, qG.v, sT.v, sV.v, sC.v

                    @blk.sync
                    def _(sync):
                        for kc in range(DCH):
                            dma(sync, qS, t_win[:, kc * DS:(kc + 1) * DS],
                                w_in[kc * 128:(kc + 1) * 128, :])
                        dma(sync, qS, t_bm[:, 0:256], bm_d[0:128, :])
                        dma(sync, qS, t_bm[:, 256:512], bm_d[128:256, :])
                        dma(sync, qS, t_id128[:], id128[:])
                        dma(sync, qS, t_bin[:], b_in_d[:])
                        dma(sync, qS, t_bgate[:], b_gate_d[:])
                        dma(sync, qS, t_bout[:], b_out_d[:])
                        dma(sync, qS, t_bffn[:], b_ffn_d[:])
                        dma(sync, qS, t_pmask[:], pmask[:])
                        dma(sync, qS, t_scol[0:1, 0:1], scal_d[:])
                        for kc in range(DCH):
                            if kc >= 2:
                                sync.wait_ge(sV.h, V0 + 1 + kc)
                            dma(sync, qS,
                                t_wgtmp[:, (kc % 2) * D:(kc % 2 + 1) * D],
                                w_gate[kc * 128:(kc + 1) * 128, :])

                    @blk.gpsimd
                    def _(gpsimd):
                        for kc in range(4):
                            dma(gpsimd, qG, t_tern[:, kc * D:(kc + 1) * D],
                                ternt[kc * 128:(kc + 1) * 128, :])
                        gpsimd.wait_ge(sV.h, V0 + 14)
                        for kc in range(4):
                            dma(gpsimd, qG, t_tern[:, kc * D:(kc + 1) * D],
                                ternt[(4 + kc) * 128:(4 + kc + 1) * 128, :])

                    @blk.vector
                    def _(vector):
                        vector.memset(t_ones[:], 1.0 / 1024.0)
                        sV.inc(vector.memset(t_onescol[:], 1.0), 1)
                        vector.memset(t_onesbf[:], 1.0)
                        sV.inc(vector.memset(t_bcc[0:1, 0:128], 1.0), 1)
                        for kc in range(DCH):
                            vector.wait_ge(qS.h, S0 + 16 * (18 + kc))
                            wsrc = t_wgtmp[:, (kc % 2) * D:(kc % 2 + 1) * D]
                            hi = t_wghi[:, kc * D:(kc + 1) * D]
                            inst = vector.tensor_copy(hi, wsrc)
                            vdrain(vector, inst)
                            sV.inc(vector.tensor_tensor(
                                t_wglo[:, kc * D:(kc + 1) * D], wsrc, hi,
                                AL.subtract), 1)
                        vector.wait_ge(qG.h, G0 + 64)
                        for kc in range(4):
                            sV.inc(vector.tensor_scalar_add(
                                t_wffn[:, kc * D:(kc + 1) * D],
                                t_tern[:, kc * D:(kc + 1) * D], -1.0), 1)
                        vector.wait_ge(qG.h, G0 + 128)
                        for kc in range(4):
                            sV.inc(vector.tensor_scalar_add(
                                t_wffn[:, (4 + kc) * D:(5 + kc) * D],
                                t_tern[:, kc * D:(kc + 1) * D], -1.0), 1)
                        vector.wait_ge(sT.h, T0 + 2)
                        inst = vector.tensor_scalar(
                            t_scol[:, 0:1], pSP[:, 2:3], EPS, None, AL.max)
                        sV.inc(inst, 1)
                        vector.wait_ge(sV.h, sV.v)
                        sV.inc(vector.reciprocal(
                            t_rsinv[:, 0:1], t_scol[:, 0:1]), 1)
                        vector.wait_ge(sT.h, T0 + 2)
                        sV.inc(vector.tensor_copy(
                            t_cwin[:, :], pSP[:, 4:4 + SCH]), 1)

                    @blk.tensor
                    def _(tensor):
                        tensor.wait_ge(qS.h, S0 + 16 * 17)
                        tensor.wait_ge(sV.h, V0 + 2)
                        sT.inc(tensor.matmul(pSP[:, 2:3], t_bcc[0:1, 0:128],
                                             t_scol[0:1, 0:1],
                                             start=True, stop=True), 1)
                        for mc in range(SCH):
                            for kc in range(DCH):
                                inst = tensor.matmul(
                                    pSP[:, 4 + mc:5 + mc],
                                    w_lhsT(t_win, DS, kc, mc),
                                    t_onescol[:, 0:1],
                                    start=(kc == 0), stop=(kc == DCH - 1))
                        sT.inc(inst, 1)

                wg_split_done = V0 + 10

                for src in range(2):
                    xtok = pt if src == 0 else xt
                    xfeat = pf if src == 0 else xf
                    u_t = t_uprior if src == 0 else t_u
                    cbase = NGRP if src == 0 else 0

                    with nc.Block() as blk:
                        G0, S0st, C0, V0 = qG.v, qS.v, sC.v, sV.v

                        def _tok_wait(g):
                            return (max(C0, C0 + g - 3),
                                    max(V0, V0 + g - 3))

                        @blk.gpsimd
                        def _(gpsimd):
                            for g in range(0, NGRP, 2):
                                wc, wv = _tok_wait(g)
                                gpsimd.wait_ge(sC.h, wc)
                                gpsimd.wait_ge(sV.h, wv)
                                dma(gpsimd, qG, tok_ring[g % 4][:],
                                    xtok[g * 128:(g + 1) * 128, :])

                        @blk.sync
                        def _(sync):
                            for g in range(1, NGRP, 2):
                                wc, wv = _tok_wait(g)
                                sync.wait_ge(sC.h, wc)
                                sync.wait_ge(sV.h, wv)
                                dma(sync, qS, tok_ring[g % 4][:],
                                    xtok[g * 128:(g + 1) * 128, :])

                        def _tok_dma_wait(eng, g):
                            if g % 2 == 0:
                                eng.wait_ge(qG.h, G0 + 16 * (g // 2 + 1))
                            else:
                                eng.wait_ge(qS.h, S0st + 16 * ((g + 1) // 2))

                        @blk.scalar
                        def _(scalar):
                            scalar.wait_ge(sV.h, V0)
                            for g in range(NGRP):
                                _tok_dma_wait(scalar, g)
                                sC.inc(scalar.activation(
                                    t_toksq[:], tok_ring[g % 4][:], ACT.Square,
                                    accum_out=t_st[:, NGRP + g:NGRP + g + 1]), 1)

                        @blk.vector
                        def _(vector):
                            for g in range(NGRP):
                                _tok_dma_wait(vector, g)
                                sV.inc(vector.reduce_sum(
                                    t_st[:, g:g + 1], tok_ring[g % 4][:],
                                    AXX), 1)

                    with nc.Block() as blk:
                        V0, C0 = sV.v, sC.v
                        G0d = qG.v

                        if DEBUG:
                            @blk.gpsimd
                            def _(gpsimd):
                                gpsimd.wait_ge(sC.h, C0)
                                gpsimd.wait_ge(sV.h, V0)
                                dma(gpsimd, qG, dbg["d_st%d" % src][:, 0:2 * NGRP],
                                    t_st[:, 0:2 * NGRP])

                        @blk.vector
                        def _(vector):
                            vector.wait_ge(sC.h, C0)
                            vector.wait_ge(sT.h, sT.v)
                            vector.wait_ge(qS.h, qS.v)
                            vector.wait_ge(qG.h, qG.v)
                            if DEBUG:
                                vector.wait_ge(qG.h, G0d + 16)
                            mC = t_mcols[:, cbase:cbase + NGRP]
                            tA = t_st[:, 2 * NGRP:3 * NGRP]
                            tB = t_st[:, 3 * NGRP:4 * NGRP]
                            sV.inc(vector.tensor_scalar_mul(
                                mC, t_st[:, 0:NGRP], 1.0 / D), 1)
                            sV.inc(vector.tensor_scalar_mul(
                                tA, t_st[:, NGRP:2 * NGRP], 1.0 / D), 1)
                            vector.wait_ge(sV.h, sV.v)
                            sV.inc(vector.tensor_mul(tB, mC, mC), 1)
                            vector.wait_ge(sV.h, sV.v)
                            sV.inc(vector.scalar_tensor_tensor(
                                tB, tA, EPS, tB, AL.add, AL.subtract), 1)

                        @blk.scalar
                        def _(scalar):
                            scalar.wait_ge(sV.h, V0 + 4)
                            sC.inc(scalar.activation(
                                t_st[:, 0:NGRP], t_st[:, 3 * NGRP:4 * NGRP],
                                ACT.Sqrt), 1)

                    with nc.Block() as blk:
                        C0 = sC.v
                        V0p2 = sV.v
                        G0d2 = qG.v

                        if DEBUG:
                            @blk.gpsimd
                            def _(gpsimd):
                                gpsimd.wait_ge(sC.h, C0)
                                dma(gpsimd, qG, dbg["d_sd%d" % src][:, 0:NGRP],
                                    t_st[:, 0:NGRP])
                                dma(gpsimd, qG, dbg["d_sd%d" % src][:, NGRP:2 * NGRP],
                                    t_st[:, 3 * NGRP:4 * NGRP])

                        @blk.vector
                        def _(vector):
                            vector.wait_ge(sC.h, C0)
                            if DEBUG:
                                vector.wait_ge(qG.h, G0d2 + 32)
                            rC = t_rcols[:, cbase:cbase + NGRP]
                            sd = t_st[:, 0:NGRP]
                            ve = t_st[:, 3 * NGRP:4 * NGRP]
                            w_ = t_st[:, NGRP:2 * NGRP]
                            sV.inc(vector.reciprocal(rC, sd), 1)
                            vector.wait_ge(sV.h, sV.v)
                            sV.inc(vector.tensor_mul(w_, rC, rC), 1)
                            vector.wait_ge(sV.h, sV.v)
                            sV.inc(vector.scalar_tensor_tensor(
                                w_, ve, -0.5, w_, AL.mult, AL.mult), 1)
                            vector.wait_ge(sV.h, sV.v)
                            sV.inc(vector.scalar_tensor_tensor(
                                rC, w_, 1.5, rC, AL.add, AL.mult), 1)
                            vector.wait_ge(sV.h, sV.v)
                            sV.inc(vector.tensor_scalar_mul(rC, rC, -1.0), 1)

                        if DEBUG:
                            @blk.gpsimd
                            def _(gpsimd):
                                gpsimd.wait_ge(sV.h, V0p2 + 5)
                                dma(gpsimd, qG,
                                    dbg["d_rc%d" % src][:, cbase:cbase + NGRP],
                                    t_rcols[:, cbase:cbase + NGRP])

                    for s in range(NSLAB):
                        with nc.Block() as blk:
                            S0, T0, V0, C0 = qS.v, sT.v, sV.v, sC.v

                            @blk.sync
                            def _(sync):
                                sync.wait_ge(sT.h, max(0, T0 - 6))
                                src_ap = xfeat.rearrange(
                                    "(kc p) t -> p kc t", p=128)[
                                    :, :, s * SLAB:(s + 1) * SLAB]
                                dma(sync, qS, xf_ring[s % 2][:], src_ap)

                            @blk.vector
                            def _(vector):
                                vector.wait_ge(sT.h, max(0, T0 - 2))
                                for gg in range(2):
                                    gabs = cbase + 2 * s + gg
                                    vector.tensor_copy(
                                        t_bcc[:, gg * 128:(gg + 1) * 128],
                                        t_mcols[:, gabs:gabs + 1].broadcast_to(
                                            [128, 128]))
                                    inst = vector.tensor_copy(
                                        t_bcc[:, 256 + gg * 128:
                                              256 + (gg + 1) * 128],
                                        t_rcols[:, gabs:gabs + 1].broadcast_to(
                                            [128, 128]))
                                sV.inc(inst, 1)
                                vector.wait_ge(sT.h, T0 + 1)
                                mb, rb = mbc_ring[s % 2]
                                vector.tensor_copy(mb[:], pBC[:, 0:256])
                                sV.inc(vector.tensor_copy(
                                    rb[:], pBC[:, 256:512]), 1)
                                for mc in range(SCH):
                                    pb = pA if mc == 0 else pB
                                    vector.wait_ge(sT.h, T0 + 2 + mc)
                                    sc = t_sa if mc == 0 else t_sb
                                    vector.wait_ge(sC.h, max(0, C0 - 1 + mc))
                                    vector.scalar_tensor_tensor(
                                        sc[:], mb[:], t_cwin[:, mc:mc + 1],
                                        pb[:, 0:SLAB], AL.mult, AL.subtract)
                                    sV.inc(vector.tensor_mul(
                                        sc[:], sc[:], rb[:]), 1)

                            @blk.tensor
                            def _(tensor):
                                tensor.wait_ge(sV.h, V0 + 1)
                                for gg in range(2):
                                    tensor.matmul(
                                        pBC[:, gg * 128:(gg + 1) * 128],
                                        t_bcc[:, gg * 128:(gg + 1) * 128],
                                        t_id128[:], start=True, stop=True)
                                    inst = tensor.matmul(
                                        pBC[:, 256 + gg * 128:
                                            256 + (gg + 1) * 128],
                                        t_bcc[:, 256 + gg * 128:
                                              256 + (gg + 1) * 128],
                                        t_id128[:], start=True, stop=True)
                                sT.inc(inst, 1)
                                tensor.wait_ge(qS.h, S0 + 16)
                                tensor.wait_ge(sV.h, V0 + 2)
                                for mc in range(SCH):
                                    pb = pA if mc == 0 else pB
                                    if mc >= 1:
                                        tensor.wait_ge(sV.h, V0 + 2 + mc)
                                    for kc in range(DCH):
                                        inst = tensor.matmul(
                                            pb[:, 0:SLAB],
                                            w_lhsT(t_win, DS, kc, mc),
                                            xf_ring[s % 2][
                                                :, kc * SLAB:(kc + 1) * SLAB],
                                            start=(kc == 0),
                                            stop=(kc == DCH - 1))
                                    sT.inc(inst, 1)

                            @blk.scalar
                            def _(scalar):
                                for mc in range(SCH):
                                    scalar.wait_ge(sV.h, V0 + 3 + mc)
                                    sc = t_sa if mc == 0 else t_sb
                                    sC.inc(scalar.activation(
                                        u_t[:, TLOC * mc + s * SLAB:
                                            TLOC * mc + (s + 1) * SLAB],
                                        sc[:], ACT.Identity,
                                        bias=t_bin[:, mc:mc + 1]), 1)

            with (nc.sbuf_tensor("t_r", [128, 15 * 512], F32) as t_r,
                  nc.sbuf_tensor("t_rt0", [128, 512], F32) as t_rt0,
                  nc.sbuf_tensor("t_rt1", [128, 512], F32) as t_rt1,
                  nc.sbuf_tensor("t_at", [128, 512], F32) as t_at,
                  nc.sbuf_tensor("t_pt8", [128, 512], F32) as t_pt8,
                  nc.sbuf_tensor("t_bmt", [128, 512], F32) as t_bmt,
                  nc.sbuf_tensor("t_e1", [128, SCH * NCH1], F32) as t_e1,
                  nc.sbuf_tensor("t_i2", [128, SCH * NCH1], F32) as t_i2,
                  nc.sbuf_tensor("t_e2", [128, SCH * NCH2], F32) as t_e2,
                  nc.sbuf_tensor("t_b2", [128, SCH * (NCH2 + 1)], F32) as t_b2):

                def R(i):
                    return P(8) if i == 1 else t_r[:, (i - 2) * 512:(i - 1) * 512]

                jobs = []
                _prod = {}

                def J(dst, dk, lh, lk, rh, rk):
                    dep = max(_prod.get(lk, -1), _prod.get(rk, -1),
                              len(jobs) - 2)
                    jobs.append((dst, lh, rh, dep))
                    _prod[dk] = len(jobs) - 1
                for l in range(1, 8):
                    J(P(l + 1), f"P{l+1}", t_at, None, P(l), f"P{l}")
                    J(Q(l), f"Q{l}", t_bmt, None, P(l), f"P{l}")
                J(t_pt8, "pt8", P(7), "P7", t_at, None)
                J(R(2), "R2", t_pt8, "pt8", R(1), "P8")
                J(t_rt0, "rt0", R(1), "P8", t_pt8, "pt8")
                J(R(3), "R3", t_rt0, "rt0", R(1), "P8")
                J(R(4), "R4", t_rt0, "rt0", R(2), "R2")
                J(t_rt1, "rt1", R(2), "R2", t_rt0, "rt0")
                J(R(5), "R5", t_rt1, "rt1", R(1), "P8")
                J(R(6), "R6", t_rt1, "rt1", R(2), "R2")
                J(R(7), "R7", t_rt1, "rt1", R(3), "R3")
                J(R(8), "R8", t_rt1, "rt1", R(4), "R4")
                J(t_rt0, "rt0", R(4), "R4", t_rt1, "rt1")
                for i in range(1, 9):
                    J(R(8 + i), f"R{8+i}", t_rt0, "rt0",
                      R(i), "P8" if i == 1 else f"R{i}")

                with nc.Block() as blk:
                    S0, T0, V0 = qS.v, sT.v, sV.v

                    @blk.sync
                    def _(sync):
                        dma(sync, qS, P(1)[:, 0:256], a_n[0:128, :])
                        dma(sync, qS, P(1)[:, 256:512], a_n[128:256, :])
                        dma(sync, qS, t_at[:, 0:256], a_t[0:128, :])
                        dma(sync, qS, t_at[:, 256:512], a_t[128:256, :])
                        dma(sync, qS, t_bmt[:, 0:256], bm_t[0:128, :])
                        dma(sync, qS, t_bmt[:, 256:512], bm_t[128:256, :])

                    @blk.tensor
                    def _(tensor):
                        tensor.wait_ge(qS.h, S0 + 96)
                        for idx, (dst, lh, rh, dep) in enumerate(jobs):
                            bank = pE if idx % 2 == 0 else pEb
                            if dep >= 0:
                                tensor.wait_ge(sV.h, V0 + dep + 1)
                            for mc in range(SCH):
                                for kc in range(SCH):
                                    inst = tensor.matmul(
                                        bank[:, mc * 256:(mc + 1) * 256],
                                        m_lhsT(lh, kc, mc), m_rhs(rh, kc),
                                        start=(kc == 0), stop=(kc == SCH - 1))
                            sT.inc(inst, 1)

                    @blk.vector
                    def _(vector):
                        for idx, (dst, lh, rh, dep) in enumerate(jobs):
                            bank = pE if idx % 2 == 0 else pEb
                            vector.wait_ge(sT.h, T0 + idx + 1)
                            sV.inc(vector.tensor_copy(dst[:], bank[:]), 1)

                with nc.Block() as blk:
                    T0c = sT.v

                    @blk.tensor
                    def _(tensor):
                        tensor.wait_ge(sV.h, wg_split_done)
                        for mc in range(DCH):
                            for kc in range(DCH):
                                tensor.matmul(
                                    pSP[:, 8 + mc:9 + mc],
                                    w_lhsT(t_wghi, D, kc, mc),
                                    t_onesbf[:, 0:1],
                                    start=(kc == 0), stop=False)
                            for kc in range(DCH):
                                inst = tensor.matmul(
                                    pSP[:, 8 + mc:9 + mc],
                                    w_lhsT(t_wglo, D, kc, mc),
                                    t_onesbf[:, 0:1],
                                    start=False, stop=(kc == DCH - 1))
                        sT.inc(inst, 1)

                    @blk.vector
                    def _(vector):
                        vector.wait_ge(sT.h, T0c + 1)
                        sV.inc(vector.tensor_copy(
                            t_cwg[:, :], pSP[:, 8:8 + DCH]), 1)

                for src in range(2):
                    u_t = t_uprior if src == 0 else t_u

                    with nc.Block() as blk:
                        T0, V0 = sT.v, sV.v

                        @blk.tensor
                        def _(tensor):
                            tensor.wait_ge(sV.h, V0)
                            tensor.wait_ge(sC.h, sC.v)
                            for nco in range(SCH):
                                for i in range(C1):
                                    for sc in range(SCH):
                                        uv = u_t[:, TLOC * sc:TLOC * (sc + 1)]
                                        uv = uv.rearrange(
                                            "p (K j c) -> p j K c",
                                            K=NCH2, j=C2, c=C1)[:, :, :, C1 - 1 - i]
                                        inst = tensor.matmul(
                                            pE[:, nco * 256:(nco + 1) * 256],
                                            m_lhsT(Q(i), sc, nco), uv,
                                            start=(nco == 0 and i == 0 and sc == 0),
                                            stop=(nco == SCH - 1 and i == C1 - 1
                                                  and sc == SCH - 1))
                            sT.inc(inst, 1)
                            tensor.wait_ge(sV.h, V0 + 2)
                            for j in range(2, C2):
                                for nco in range(SCH):
                                    for kc in range(SCH):
                                        n_el = (C2 - j) * NCH2
                                        ev = t_e1[:, kc * 256:kc * 256 + n_el]
                                        ov = pEb[:, nco * 256 + j * NCH2:
                                                 (nco + 1) * 256]
                                        inst = tensor.matmul(
                                            ov, m_lhsT(R(j - 1), kc, nco), ev,
                                            start=False, stop=(kc == SCH - 1),
                                            skip_group_check=True)
                            sT.inc(inst, 1)
                            tensor.wait_ge(sV.h, V0 + 4)
                            for nco in range(SCH):
                                for kc in range(SCH):
                                    iv = t_i2[:, kc * 256 + (C2 - 1) * NCH2:
                                              kc * 256 + 256]
                                    inst = tensor.matmul(
                                        pSP[:, nco * 256:nco * 256 + NCH2],
                                        m_lhsT(R(1), kc, nco), iv,
                                        start=(kc == 0), stop=(kc == SCH - 1))
                            sT.inc(inst, 1)
                            for K in range(NCH2):
                                tensor.wait_ge(sV.h, V0 + 6 + K)
                                for nco in range(SCH):
                                    for kc in range(SCH):
                                        inst = tensor.matmul(
                                            pSP[:, nco * 256 + 32:nco * 256 + 33],
                                            m_lhsT(R(16), kc, nco),
                                            t_b2[:, kc * (NCH2 + 1) + K:
                                                 kc * (NCH2 + 1) + K + 1],
                                            start=(kc == 0), stop=(kc == SCH - 1))
                                sT.inc(inst, 1)
                            if src == 1:
                                tensor.wait_ge(sV.h, V0 + 6 + NCH2)
                                for j in range(1, C2):
                                    for nco in range(SCH):
                                        for kc in range(SCH):
                                            ov = pE[:, nco * 256 + j * NCH2:
                                                    nco * 256 + (j + 1) * NCH2]
                                            inst = tensor.matmul(
                                                ov, m_lhsT(R(j), kc, nco),
                                                t_b2[:, kc * (NCH2 + 1):
                                                     kc * (NCH2 + 1) + NCH2],
                                                start=False, stop=(kc == SCH - 1),
                                                skip_group_check=True)
                                sT.inc(inst, 1)

                        @blk.vector
                        def _(vector):
                            vector.wait_ge(sT.h, T0 + 1)
                            sV.inc(vector.tensor_copy(t_e1[:], pE[:]), 1)
                            sV.inc(vector.memset(pEb[:], 0.0), 1)
                            vector.wait_ge(sT.h, T0 + 2)
                            vector.tensor_copy(t_i2[:], pEb[:])
                            for nco in range(SCH):
                                ov = t_i2[:, nco * 256 + NCH2:(nco + 1) * 256]
                                ev = t_e1[:, nco * 256:
                                          nco * 256 + (C2 - 1) * NCH2]
                                inst = vector.tensor_tensor(ov, ov, ev, AL.add)
                            sV.inc(inst, 1)
                            sV.inc(vector.memset(pE[:], 0.0), 1)
                            vector.wait_ge(sT.h, T0 + 3)
                            for nco in range(SCH):
                                ev = t_e1[:, nco * 256 + (C2 - 1) * NCH2:
                                          nco * 256 + 256]
                                inst = vector.tensor_tensor(
                                    t_e2[:, nco * NCH2:(nco + 1) * NCH2],
                                    pSP[:, nco * 256:nco * 256 + NCH2],
                                    ev, AL.add)
                            sV.inc(inst, 1)
                            if src == 0:
                                for sc in range(SCH):
                                    inst = vector.memset(
                                        t_b2[:, sc * (NCH2 + 1):
                                             sc * (NCH2 + 1) + 1], 0.0)
                            else:
                                for sc in range(SCH):
                                    inst = vector.tensor_copy(
                                        t_b2[:, sc * (NCH2 + 1):
                                             sc * (NCH2 + 1) + 1],
                                        t_hst[:, sc:sc + 1])
                            sV.inc(inst, 1)
                            for K in range(NCH2):
                                vector.wait_ge(sT.h, T0 + 4 + K)
                                for nco in range(SCH):
                                    inst = vector.tensor_tensor(
                                        t_b2[:, nco * (NCH2 + 1) + K + 1:
                                             nco * (NCH2 + 1) + K + 2],
                                        pSP[:, nco * 256 + 32:nco * 256 + 33],
                                        t_e2[:, nco * NCH2 + K:
                                             nco * NCH2 + K + 1], AL.add)
                                sV.inc(inst, 1)
                            if src == 0:
                                for sc in range(SCH):
                                    inst = vector.tensor_scalar_mul(
                                        t_hst[:, sc:sc + 1],
                                        t_b2[:, sc * (NCH2 + 1) + NCH2:
                                             sc * (NCH2 + 1) + NCH2 + 1],
                                        t_pmask[:, 0:1])
                                sV.inc(inst, 1)
                            else:
                                vector.wait_ge(sT.h, T0 + 4 + NCH2)
                                inst = vector.tensor_tensor(
                                    t_ball[:], pE[:], t_i2[:], AL.add)
                                for nco in range(SCH):
                                    bv = t_ball[:, nco * 256:nco * 256 + NCH2]
                                    inst = vector.tensor_tensor(
                                        bv, bv,
                                        t_b2[:, nco * (NCH2 + 1):
                                             nco * (NCH2 + 1) + NCH2], AL.add)
                                sV.inc(inst, 1)

                if DEBUG:
                    with nc.Block() as blk:
                        @blk.gpsimd
                        def _(gpsimd):
                            gpsimd.wait_ge(sV.h, sV.v)
                            gpsimd.wait_ge(sC.h, sC.v)
                            for name, tile in [("d_u", t_u), ("d_up", t_uprior),
                                               ("d_e1", t_e1), ("d_i2", t_i2),
                                               ("d_ball", t_ball),
                                               ("d_m", t_mcols), ("d_r", t_rcols),
                                               ("d_hst", t_hst), ("d_e2", t_e2),
                                               ("d_b2", t_b2), ("d_p8", P(8)),
                                               ("d_q3", Q(3)), ("d_r16", R(16))]:
                                dma(gpsimd, qG, dbg[name][:], tile[:])

        with (nc.sbuf_tensor("t_wohi", [128, DCH * D], BF16) as t_wohi,
              nc.sbuf_tensor("t_wolo", [128, DCH * D], BF16) as t_wolo,
              nc.sbuf_tensor("t_bf", [128, DCH * SLAB], BF16) as t_bf,
              nc.sbuf_tensor("t_lo", [128, DCH * SLAB], BF16) as t_lo,
              nc.sbuf_tensor("t_ct", [128, SCH * D], F32) as t_ct,
              nc.sbuf_tensor("t_dt", [128, SCH * D], F32) as t_dt,
              nc.sbuf_tensor("t_gy", [128, DCH * SLAB], F32) as t_gy,
              nc.sbuf_tensor("t_hs", [128, SCH * SLAB], F32) as t_hs,
              nc.sbuf_tensor("t_x2", [128, DCH * SLAB], F32) as t_x2,
              nc.sbuf_tensor("t_xq", [128, DCH * SLAB], BF16) as t_xq):

            with nc.Block() as blk:
                S0, V0 = qS.v, sV.v

                @blk.sync
                def _(sync):
                    for kc in range(SCH):
                        dma(sync, qS, t_ct[:, kc * D:(kc + 1) * D],
                            ct_d[kc * 128:(kc + 1) * 128, :])
                    for kc in range(SCH):
                        dma(sync, qS, t_dt[:, kc * D:(kc + 1) * D],
                            dt_d[kc * 128:(kc + 1) * 128, :])
                    for kc in range(DCH):
                        if kc >= 2:
                            sync.wait_ge(sV.h, V0 + kc - 1)
                        dma(sync, qS,
                            t_x2[:, (kc % 2) * D:(kc % 2 + 1) * D],
                            w_out[kc * 128:(kc + 1) * 128, :])

                @blk.vector
                def _(vector):
                    for kc in range(DCH):
                        vector.wait_ge(qS.h, S0 + 16 * (5 + kc))
                        wsrc = t_x2[:, (kc % 2) * D:(kc % 2 + 1) * D]
                        hi = t_wohi[:, kc * D:(kc + 1) * D]
                        inst = vector.tensor_copy(hi, wsrc)
                        vdrain(vector, inst)
                        sV.inc(vector.tensor_tensor(
                            t_wolo[:, kc * D:(kc + 1) * D], wsrc, hi,
                            AL.subtract), 1)

            with nc.Block() as blk:
                P0V, P0T = sV.v, sT.v

                @blk.vector
                def _(vector):
                    vector.wait_ge(sT.h, P0T)
                    for gg in range(2):
                        vector.tensor_copy(
                            t_bcc[:, gg * 128:(gg + 1) * 128],
                            t_mcols[:, gg:gg + 1].broadcast_to([128, 128]))
                        inst = vector.tensor_copy(
                            t_bcc[:, 256 + gg * 128:256 + (gg + 1) * 128],
                            t_rcols[:, gg:gg + 1].broadcast_to([128, 128]))
                    sV.inc(inst, 1)
                    vector.wait_ge(sT.h, P0T + 1)
                    mb, rb = mbc_ring[0]
                    vector.tensor_copy(mb[:], pBC[:, 0:256])
                    sV.inc(vector.tensor_copy(rb[:], pBC[:, 256:512]), 1)

                @blk.tensor
                def _(tensor):
                    tensor.wait_ge(sV.h, P0V + 1)
                    for gg in range(2):
                        tensor.matmul(
                            pBC[:, gg * 128:(gg + 1) * 128],
                            t_bcc[:, gg * 128:(gg + 1) * 128],
                            t_id128[:], start=True, stop=True)
                        inst = tensor.matmul(
                            pBC[:, 256 + gg * 128:256 + (gg + 1) * 128],
                            t_bcc[:, 256 + gg * 128:256 + (gg + 1) * 128],
                            t_id128[:], start=True, stop=True)
                    sT.inc(inst, 1)

            pre_bcast_cnt = sT.v
            pre_mbc_cnt = sV.v
            rot_banks = [pA, pB, pE, pEb, pSP]
            bank_hist = [("sV", sV.v)] * 5
            gbank = [5]
            prev = None
            for s in range(NSLAB + 1):
                with nc.Block() as blk:
                    S0, G0, T0, V0, C0, O0 = qS.v, qG.v, sT.v, sV.v, sC.v, sO.v
                    last = s == NSLAB
                    pv = prev
                    ev = {}

                    t = T0
                    if not last:
                        for mc in range(DCH):
                            t += 1
                            ev[("t_gate", mc)] = t
                        if s + 1 < NSLAB:
                            t += 1
                            ev["t_bcast"] = t
                    if pv is not None:
                        for mc in range(DCH):
                            t += 1
                            ev[("t_ffn", mc)] = t
                    if not last:
                        t += 1
                        ev["t_lag"] = t
                        for mc in range(DCH):
                            t += 1
                            ev[("t_cd", mc)] = t
                        for mc in range(DCH):
                            t += 1
                            ev[("t_wout", mc)] = t
                        t += 1
                        ev["t_stats"] = t
                    ev["t_end"] = t

                    v = V0
                    _gq = list(range(DCH)) if not last else []

                    def _pop(n, v):
                        for _ in range(n):
                            if _gq:
                                v += 1
                                ev[("v_gate", _gq.pop(0))] = v
                        return v
                    if not last:
                        v += 1
                        ev["v_xsplit"] = v
                    if not last and s + 1 < NSLAB:
                        v += 1
                        ev["v_bcc"] = v
                    v = _pop(2, v)
                    if pv is not None:
                        v += 1
                        ev["v_ln2a"] = v
                        v += 1
                        ev["v_ln2b"] = v
                        v += 1
                        ev["v_qgy"] = v
                        v = _pop(DCH, v)
                        v += 1
                        ev["v_quant"] = v
                    v = _pop(DCH, v)
                    if pv is not None:
                        for mc in range(DCH):
                            v += 1
                            ev[("v_fres", mc)] = v
                    if not last:
                        v += 1
                        ev["v_hs"] = v
                        for mc in range(DCH):
                            v += 1
                            ev[("v_y", mc)] = v
                        v += 1
                        ev["v_gysplit"] = v
                        for mc in range(DCH):
                            v += 1
                            ev[("v_x2", mc)] = v
                        if s + 1 < NSLAB:
                            v += 1
                            ev["v_mbc"] = v
                    ev["v_end"] = v

                    c = C0
                    if not last:
                        c += 1
                        ev["c_xcast"] = c
                    if pv is not None:
                        c += 1
                        ev["c_sqrt"] = c
                    if not last:
                        for mc in range(DCH):
                            c += 1
                            ev[("c_sig", mc)] = c
                    if pv is not None:
                        for mc in range(DCH):
                            c += 1
                            ev[("c_fevac", mc)] = c
                    if not last:
                        c += 1
                        ev["c_gycast"] = c
                        for mc in range(DCH):
                            c += 1
                            ev[("c_sq", mc)] = c
                    ev["c_end"] = c

                    banks = {}
                    bank_wait = {}
                    plan = []
                    if not last:
                        plan += [("gate", mc, ("sV", ev[("v_gate", mc)]))
                                 for mc in range(DCH)]
                    if pv is not None:
                        gi = len([p for p in plan])
                        plan = (plan[:DCH if not last else 0]
                                if False else plan)
                    plan2 = []
                    if not last:
                        plan2 += [("gate", mc, ("sC", ev[("c_sig", mc)]))
                                  for mc in range(DCH)]
                    if pv is not None:
                        plan2 += [("ffn", mc, ("sC", ev[("c_fevac", mc)]))
                                  for mc in range(DCH)]
                    if not last:
                        plan2 += [("cd", mc, ("sV", ev[("v_y", mc)]))
                                  for mc in range(DCH)]
                        plan2 += [("wout", mc, ("sV", ev[("v_x2", mc)]))
                                  for mc in range(DCH)]
                    for stage, mc, consumer in plan2:
                        i = gbank[0]
                        banks[(stage, mc)] = rot_banks[i % 5]
                        bank_wait[(stage, mc)] = (
                            bank_hist[i - 5] if i >= 5 else None)
                        bank_hist.append(consumer)
                        gbank[0] = i + 1

                    def bwait(eng, key):
                        w = bank_wait[key]
                        if w is not None:
                            nm, cnt = w
                            eng.wait_ge(sV.h if nm == "sV" else sC.h, cnt)

                    @blk.sync
                    def _(sync):
                        if not last:
                            if pv is not None and pv.get("prev_x2") is not None:
                                sync.wait_ge(sV.h, pv["prev_x2"])
                            src_ap = xf.rearrange(
                                "(kc p) t -> p kc t", p=128)[
                                :, :, s * SLAB:(s + 1) * SLAB]
                            dma(sync, qS, xf_ring[s % 2][:], src_ap)

                    @blk.vector
                    def _(vector):
                        if not last:
                            vector.wait_ge(sC.h, ev["c_xcast"])
                            sV.inc(vector.tensor_tensor(
                                t_lo[:], xf_ring[s % 2][:], t_bf[:],
                                AL.subtract), 1)
                        if not last and s + 1 < NSLAB:
                            vector.wait_ge(sT.h, pv["t_bcast_prev_cnt"]
                                           if s >= 1 else pre_bcast_cnt)
                            for gg in range(2):
                                gabs = 2 * (s + 1) + gg
                                vector.tensor_copy(
                                    t_bcc[:, gg * 128:(gg + 1) * 128],
                                    t_mcols[:, gabs:gabs + 1].broadcast_to(
                                        [128, 128]))
                                inst = vector.tensor_copy(
                                    t_bcc[:, 256 + gg * 128:256 + (gg + 1) * 128],
                                    t_rcols[:, gabs:gabs + 1].broadcast_to(
                                        [128, 128]))
                            sV.inc(inst, 1)

                        gate_q = list(range(DCH)) if not last else []
                        mb, rb = mbc_ring[s % 2]

                        def emit_gate_evac(mc):
                            pb = banks[("gate", mc)]
                            vector.wait_ge(sT.h, ev[("t_gate", mc)])
                            vector.scalar_tensor_tensor(
                                pb[:, SLAB:2 * SLAB], mb[:],
                                t_cwg[:, mc:mc + 1],
                                pb[:, 0:SLAB], AL.mult, AL.subtract)
                            sV.inc(vector.tensor_mul(
                                pb[:, SLAB:2 * SLAB], pb[:, SLAB:2 * SLAB],
                                rb[:]), 1)

                        def g1(n=1):
                            for _i in range(n):
                                if gate_q:
                                    emit_gate_evac(gate_q.pop(0))

                        g1(2)
                        if pv is not None:
                            vector.wait_ge(sT.h, pv["t_stats"])
                            vector.tensor_copy(t_fa[:], pST[:, 0:256])
                            inst = vector.tensor_copy(t_mrs2[:],
                                                      pST[:, 256:512])
                            vdrain(vector, inst)
                            inst = vector.tensor_mul(t_fb[:], t_fa[:], t_fa[:])
                            vdrain(vector, inst)
                            sV.inc(vector.scalar_tensor_tensor(
                                t_fb[:], t_mrs2[:], EPS, t_fb[:],
                                AL.add, AL.subtract), 1)
                            vector.wait_ge(sC.h, ev["c_sqrt"])
                            inst = vector.reciprocal(t_rs2[:], t_mrs2[:])
                            vdrain(vector, inst)
                            inst = vector.tensor_mul(t_mrs2[:], t_rs2[:],
                                                     t_rs2[:])
                            vdrain(vector, inst)
                            inst = vector.scalar_tensor_tensor(
                                t_mrs2[:], t_fb[:], -0.5, t_mrs2[:],
                                AL.mult, AL.mult)
                            vdrain(vector, inst)
                            inst = vector.scalar_tensor_tensor(
                                t_rs2[:], t_mrs2[:], 1.5, t_rs2[:],
                                AL.add, AL.mult)
                            vdrain(vector, inst)
                            inst = vector.tensor_scalar_mul(
                                t_rs2[:], t_rs2[:], t_rsinv[:, 0:1])
                            vdrain(vector, inst)
                            sV.inc(vector.tensor_tensor(
                                t_mrs2[:], t_fa[:], t_rs2[:], AL.mult), 1)
                            rsb = t_rs2[:, 0:SLAB].unsqueeze(1).broadcast_to(
                                [128, DCH, SLAB])
                            mrsb = t_mrs2[:, 0:SLAB].unsqueeze(1).broadcast_to(
                                [128, DCH, SLAB])
                            x2v = t_x2[:].rearrange("p (m c) -> p m c", c=SLAB)
                            gyv = t_gy[:].rearrange("p (m c) -> p m c", c=SLAB)
                            inst = vector.tensor_tensor(gyv, x2v, rsb, AL.mult)
                            vdrain(vector, inst)
                            inst = vector.tensor_tensor(gyv, gyv, mrsb,
                                                        AL.subtract)
                            vdrain(vector, inst)
                            inst = vector.tensor_scalar(t_gy[:], t_gy[:],
                                                        127.0, -127.0,
                                                        AL.min, AL.max)
                            vdrain(vector, inst)
                            inst = vector.tensor_copy(t_qi[:], t_gy[:])
                            vdrain(vector, inst)
                            sV.inc(vector.tensor_tensor(
                                t_xq[:], t_gy[:], t_qi[:],
                                AL.subtract), 1)
                            vector.wait_ge(sD.h, sD.v)
                            g1(DCH)
                            inst = vector.scalar_tensor_tensor(
                                t_qi[:], t_xq[:], 0.5, t_qi[:],
                                AL.is_gt, AL.add)
                            vdrain(vector, inst)
                            inst = vector.scalar_tensor_tensor(
                                t_qi[:], t_xq[:], -0.5, t_qi[:],
                                AL.is_lt, AL.subtract)
                            vdrain(vector, inst)
                            sV.inc(vector.tensor_scalar_mul(
                                t_xq[:], t_qi[:], -1.0), 1)
                        g1(DCH)
                        if pv is not None:
                            for mc in range(DCH):
                                vector.wait_ge(sC.h, ev[("c_fevac", mc)])
                                fb = t_fa if mc % 2 == 0 else t_fb
                                sV.inc(vector.tensor_tensor(
                                    t_x2[:, mc * SLAB:(mc + 1) * SLAB],
                                    fb[:],
                                    t_x2[:, mc * SLAB:(mc + 1) * SLAB],
                                    AL.add), 1)
                        if last:
                            return
                        vector.wait_ge(sT.h, ev["t_lag"])
                        sV.inc(vector.tensor_copy(t_hs[:], pHS[:]), 1)
                        vector.wait_ge(sC.h, ev[("c_sig", DCH - 1)])
                        if DEBUG and s == 1:
                            vector.wait_ge(qG.h, G0 + 32)
                        for mc in range(DCH):
                            pb = banks[("cd", mc)]
                            vector.wait_ge(sT.h, ev[("t_cd", mc)])
                            sV.inc(vector.tensor_tensor(
                                t_gy[:, mc * SLAB:(mc + 1) * SLAB],
                                pb[:, 0:SLAB],
                                t_gy[:, mc * SLAB:(mc + 1) * SLAB],
                                AL.mult), 1)
                        vector.wait_ge(sC.h, ev["c_gycast"])
                        sV.inc(vector.tensor_tensor(
                            t_lo[:], t_gy[:], t_bf[:],
                            AL.subtract), 1)
                        if pv is not None:
                            vector.wait_ge(sO.h, O0 + 16)
                        for mc in range(DCH):
                            pb = banks[("wout", mc)]
                            vector.wait_ge(sT.h, ev[("t_wout", mc)])
                            sV.inc(vector.scalar_tensor_tensor(
                                t_x2[:, mc * SLAB:(mc + 1) * SLAB],
                                pb[:, 0:SLAB], t_bout[:, mc:mc + 1],
                                xf_ring[s % 2][:, mc * SLAB:(mc + 1) * SLAB],
                                AL.add, AL.add), 1)
                        if s + 1 < NSLAB:
                            vector.wait_ge(sT.h, ev["t_bcast"])
                            mb, rb = mbc_ring[(s + 1) % 2]
                            vector.tensor_copy(mb[:], pBC[:, 0:256])
                            sV.inc(vector.tensor_copy(rb[:], pBC[:, 256:512]), 1)

                    @blk.scalar
                    def _(scalar):
                        if not last:
                            if pv is not None:
                                scalar.wait_ge(sT.h, pv[("t_wout", DCH - 1)])
                            scalar.wait_ge(qS.h, S0 + 16)
                            sC.inc(scalar.activation(
                                t_bf[:], xf_ring[s % 2][:], ACT.Copy), 1)
                        if pv is not None:
                            scalar.wait_ge(sV.h, ev["v_ln2a"])
                            sC.inc(scalar.activation(t_mrs2[:], t_fb[:],
                                                     ACT.Sqrt), 1)
                        if not last:
                            if pv is not None:
                                scalar.wait_ge(sV.h, ev["v_qgy"])
                            for mc in range(DCH):
                                scalar.wait_ge(sV.h, ev[("v_gate", mc)])
                                pb = banks[("gate", mc)]
                                sC.inc(scalar.activation(
                                    t_gy[:, mc * SLAB:(mc + 1) * SLAB],
                                    pb[:, SLAB:2 * SLAB],
                                    ACT.Sigmoid, bias=t_bgate[:, mc:mc + 1]), 1)
                        if pv is not None:
                            for mc in range(DCH):
                                pb = banks[("ffn", mc)]
                                fb = t_fa if mc % 2 == 0 else t_fb
                                scalar.wait_ge(sT.h, ev[("t_ffn", mc)])
                                if mc >= 2:
                                    scalar.wait_ge(sV.h, ev[("v_fres", mc - 2)])
                                sC.inc(scalar.activation(
                                    fb[:],
                                    pb[:, 0:SLAB], ACT.Identity,
                                    bias=t_bffn[:, mc:mc + 1],
                                    scale=t_scol[:, 0:1]), 1)
                        if last:
                            return
                        scalar.wait_ge(sV.h, ev[("v_y", DCH - 1)])
                        sC.inc(scalar.activation(
                            t_bf[:], t_gy[:], ACT.Copy), 1)
                        scalar.wait_ge(sT.h, ev[("t_wout", DCH - 1)])
                        for mc in range(DCH):
                            scalar.wait_ge(sV.h, ev[("v_x2", mc)])
                            sC.inc(scalar.activation(
                                t_gy[:, mc * SLAB:(mc + 1) * SLAB],
                                t_x2[:, mc * SLAB:(mc + 1) * SLAB],
                                ACT.Square), 1)

                    @blk.tensor
                    def _(tensor):
                        if not last:
                            tensor.wait_ge(qS.h, S0 + 16)
                            tensor.wait_ge(sV.h, ev["v_xsplit"])
                            for mc in range(DCH):
                                pb = banks[("gate", mc)]
                                bwait(tensor, ("gate", mc))
                                for kc in range(DCH):
                                    tensor.matmul(
                                        pb[:, 0:SLAB],
                                        w_lhsT(t_wghi, D, kc, mc),
                                        t_bf[:, kc * SLAB:(kc + 1) * SLAB],
                                        start=(kc == 0), stop=False)
                                for kc in range(DCH):
                                    tensor.matmul(
                                        pb[:, 0:SLAB],
                                        w_lhsT(t_wglo, D, kc, mc),
                                        t_bf[:, kc * SLAB:(kc + 1) * SLAB],
                                        start=False, stop=False)
                                for kc in range(DCH):
                                    inst = tensor.matmul(
                                        pb[:, 0:SLAB],
                                        w_lhsT(t_wghi, D, kc, mc),
                                        t_lo[:, kc * SLAB:(kc + 1) * SLAB],
                                        start=False, stop=(kc == DCH - 1))
                                sT.inc(inst, 1)
                        if not last and s + 1 < NSLAB:
                            tensor.wait_ge(sV.h, ev["v_bcc"])
                            tensor.wait_ge(sV.h, pv["v_mbc"]
                                           if pv is not None else pre_mbc_cnt)
                            for gg in range(2):
                                tensor.matmul(
                                    pBC[:, gg * 128:(gg + 1) * 128],
                                    t_bcc[:, gg * 128:(gg + 1) * 128],
                                    t_id128[:], start=True, stop=True)
                                inst = tensor.matmul(
                                    pBC[:, 256 + gg * 128:256 + (gg + 1) * 128],
                                    t_bcc[:, 256 + gg * 128:
                                          256 + (gg + 1) * 128],
                                    t_id128[:], start=True, stop=True)
                            sT.inc(inst, 1)
                        if pv is not None:
                            tensor.wait_ge(sV.h, ev["v_quant"])
                            for mc in range(DCH):
                                pb = banks[("ffn", mc)]
                                bwait(tensor, ("ffn", mc))
                                for kc in range(DCH):
                                    inst = tensor.matmul(
                                        pb[:, 0:SLAB],
                                        w_lhsT(t_wffn, D, kc, mc),
                                        t_xq[:, kc * SLAB:(kc + 1) * SLAB],
                                        start=(kc == 0), stop=(kc == DCH - 1))
                                sT.inc(inst, 1)
                        if last:
                            return
                        if pv is not None:
                            tensor.wait_ge(sV.h, pv["v_hs"])
                        nch_s = SLAB // C1
                        for nco in range(SCH):
                            for l in range(C1):
                                for sc in range(SCH):
                                    uv = t_u[:, TLOC * sc + s * SLAB:
                                             TLOC * sc + (s + 1) * SLAB]
                                    uv = uv.rearrange("p (ch c) -> p c ch",
                                                      c=C1)[:, 0:C1 - l, :]
                                    ov = pHS[:, nco * 256 + l * nch_s:
                                             (nco + 1) * 256]
                                    st = (nco == 0 and l == 0 and sc == 0)
                                    inst = tensor.matmul(
                                        ov, m_lhsT(Q(l), sc, nco), uv,
                                        start=st, stop=False)
                            for j in range(C1):
                                for sc in range(SCH):
                                    ov = pHS[:, nco * 256 + j * nch_s:
                                             nco * 256 + (j + 1) * nch_s]
                                    bv = t_ball[:, sc * 256:(sc + 1) * 256]
                                    bv = bv.rearrange("p (j K) -> p K j",
                                                      K=NCH2)[:, 2 * s:2 * s + 2, :]
                                    inst = tensor.matmul(
                                        ov, m_lhsT(P(j + 1), sc, nco), bv,
                                        start=False,
                                        stop=(nco == SCH - 1 and j == C1 - 1
                                              and sc == SCH - 1))
                        sT.inc(inst, 1)
                        tensor.wait_ge(sV.h, ev["v_hs"])
                        for mc in range(DCH):
                            pb = banks[("cd", mc)]
                            bwait(tensor, ("cd", mc))
                            for sc in range(SCH):
                                hv = t_hs[:, sc * SLAB:(sc + 1) * SLAB]
                                hv = hv.rearrange("p (c ch) -> p ch c", c=C1)
                                tensor.matmul(
                                    pb[:, 0:SLAB], w_lhsT(t_ct, D, sc, mc),
                                    hv, start=(sc == 0), stop=False)
                            for sc in range(SCH):
                                inst = tensor.matmul(
                                    pb[:, 0:SLAB], w_lhsT(t_dt, D, sc, mc),
                                    t_u[:, TLOC * sc + s * SLAB:
                                        TLOC * sc + (s + 1) * SLAB],
                                    start=False, stop=(sc == SCH - 1))
                            sT.inc(inst, 1)
                        tensor.wait_ge(sV.h, ev["v_gysplit"])
                        for mc in range(DCH):
                            pb = banks[("wout", mc)]
                            bwait(tensor, ("wout", mc))
                            for kc in range(DCH):
                                tensor.matmul(
                                    pb[:, 0:SLAB], w_lhsT(t_wohi, D, kc, mc),
                                    t_bf[:, kc * SLAB:(kc + 1) * SLAB],
                                    start=(kc == 0), stop=False)
                            for kc in range(DCH):
                                tensor.matmul(
                                    pb[:, 0:SLAB], w_lhsT(t_wolo, D, kc, mc),
                                    t_bf[:, kc * SLAB:(kc + 1) * SLAB],
                                    start=False, stop=False)
                            for kc in range(DCH):
                                inst = tensor.matmul(
                                    pb[:, 0:SLAB], w_lhsT(t_wohi, D, kc, mc),
                                    t_lo[:, kc * SLAB:(kc + 1) * SLAB],
                                    start=False, stop=(kc == DCH - 1))
                            sT.inc(inst, 1)
                        if pv is not None:
                            tensor.wait_ge(sV.h, ev["v_ln2a"])
                        tensor.wait_ge(sV.h, ev[("v_x2", DCH - 1)])
                        tensor.wait_ge(sC.h, ev[("c_sq", DCH - 1)])
                        for kc in range(DCH):
                            tensor.matmul(
                                pST[:, 0:256], t_ones[:],
                                t_x2[:, kc * SLAB:(kc + 1) * SLAB],
                                start=(kc == 0), stop=(kc == DCH - 1))
                        for kc in range(DCH):
                            inst = tensor.matmul(
                                pST[:, 256:512], t_ones[:],
                                t_gy[:, kc * SLAB:(kc + 1) * SLAB],
                                start=(kc == 0), stop=(kc == DCH - 1))
                        sT.inc(inst, 1)

                    @blk.gpsimd
                    def _(gpsimd):
                        if pv is not None:
                            gpsimd.wait_ge(sV.h, ev[("v_fres", DCH - 1)])
                            dst = out_d.rearrange("(kc p) t -> p kc t", p=128)[
                                :, :, (s - 1) * SLAB:s * SLAB]
                            dma(gpsimd, sO, dst, t_x2[:])
                        if DEBUG and s == 1:
                            gpsimd.wait_ge(sC.h, ev[("c_sig", DCH - 1)])
                            dma(gpsimd, qG, dbg["d_g"][:], t_gy[:])
                            gpsimd.wait_ge(sV.h, ev["v_hs"])
                            dma(gpsimd, qG, dbg["d_hs"][:], t_hs[:])
                            gpsimd.wait_ge(sV.h, ev[("v_x2", DCH - 1)])
                            dma(gpsimd, qG, dbg["d_x2"][:], t_x2[:])

                    assert sT.v == ev["t_end"], (s, sT.v, ev["t_end"])
                    assert sV.v == ev["v_end"], (s, sV.v, ev["v_end"])
                    assert sC.v == ev["c_end"], (s, sC.v, ev["c_end"])

                    nxt = dict(ev)
                    nxt["t_bcast_prev_cnt"] = ev.get("t_bcast", sT.v)
                    nxt["prev_x2"] = (pv or {}).get("x2_cnt_for_ring", None)
                    nxt["x2_cnt_for_ring"] = ev.get(("v_x2", DCH - 1))
                    prev = nxt

            with nc.Block() as blk:
                @blk.gpsimd
                def _(gpsimd):
                    gpsimd.wait_ge(sO.h, sO.v)

    return nc


_NC_CACHE = None


def kernel(**inputs):
    global _NC_CACHE
    x = np.asarray(inputs["L0_attn_x"], dtype=np.float32)
    w_in = np.ascontiguousarray(inputs["L0_attn_W_in"], dtype=np.float32)
    b_in = np.asarray(inputs["L0_attn_b_in"], dtype=np.float32)
    w_gate = np.ascontiguousarray(inputs["L0_attn_W_gate"], dtype=np.float32)
    b_gate = np.asarray(inputs["L0_attn_b_gate"], dtype=np.float32)
    w_out = np.ascontiguousarray(inputs["L0_attn_W_out"], dtype=np.float32)
    b_out = np.asarray(inputs["L0_attn_b_out"], dtype=np.float32)
    a_m = np.asarray(inputs["L0_attn_A"], dtype=np.float32)
    bm_m = np.asarray(inputs["L0_attn_Bm"], dtype=np.float32)
    c_m = np.asarray(inputs["L0_attn_C"], dtype=np.float32)
    d_m = np.asarray(inputs["L0_attn_D"], dtype=np.float32)
    tern = np.asarray(inputs["L0_ffn_ternary_weight"], dtype=np.int32)
    b_ffn = np.asarray(inputs["L0_ffn_bias"], dtype=np.float32)
    act_scale = np.asarray(inputs["L0_ffn_act_scale"], dtype=np.float32)

    if _NC_CACHE is None:
        _NC_CACHE = build_nc()
    nc = _NC_CACHE

    shared = {
        "w_in": w_in,
        "w_gate": w_gate,
        "w_out": w_out,
        "ct": np.ascontiguousarray(c_m.T),
        "dtw": np.ascontiguousarray(d_m.T),
        "a_n": np.ascontiguousarray(a_m),
        "a_t": np.ascontiguousarray(a_m.T),
        "bm": np.ascontiguousarray(bm_m),
        "bm_t": np.ascontiguousarray(bm_m.T),
        "id128": np.eye(128, dtype=np.float32),
        "ternt": np.ascontiguousarray(tern.T),
        "b_in": np.ascontiguousarray(b_in.reshape(SCH, 128).T),
        "b_gate": np.ascontiguousarray(b_gate.reshape(DCH, 128).T),
        "b_out": np.ascontiguousarray(b_out.reshape(DCH, 128).T),
        "b_ffn": np.ascontiguousarray(b_ffn.reshape(DCH, 128).T),
        "scal": act_scale.reshape(1, 1),
    }
    zeros_f = np.zeros((D, TLOC), np.float32)
    zeros_t = np.zeros((TLOC, D), np.float32)
    in_maps = []
    for core in range(8):
        b = core // 2
        half = core % 2
        xs = x[b, half * TLOC:(half + 1) * TLOC, :]
        m = dict(shared)
        m["xf"] = np.ascontiguousarray(xs.T)
        m["xt"] = np.ascontiguousarray(xs)
        if half == 1:
            ps = x[b, 0:TLOC, :]
            m["pf"] = np.ascontiguousarray(ps.T)
            m["pt"] = np.ascontiguousarray(ps)
            m["pmask"] = np.ones((128, 1), np.float32)
        else:
            m["pf"] = zeros_f
            m["pt"] = zeros_t
            m["pmask"] = np.zeros((128, 1), np.float32)
        in_maps.append(m)

    res = run_bass_kernel_spmd(nc, in_maps, core_ids=list(range(8)))
    out = np.empty((B, T, D), np.float32)
    for core in range(8):
        b = core // 2
        half = core % 2
        out[b, half * TLOC:(half + 1) * TLOC, :] = res.results[core]["out"].T
    if DEBUG:
        kernel.last_results = res
    return out


# version 11
# speedup vs baseline: 1.0252x; 1.0129x over previous
from contextlib import ExitStack

import numpy as np
import concourse.bass as bass
import concourse.mybir as mybir
from concourse.bass_utils import run_bass_kernel_spmd

F32 = mybir.dt.float32
BF16 = mybir.dt.bfloat16
I32 = mybir.dt.int32
AL = mybir.AluOpType
ACT = mybir.ActivationFunctionType
AXX = mybir.AxisListType.X

B, T, D, DS = 4, 4096, 1024, 256
TLOC = 2048
DCH = D // 128
SCH = DS // 128
C1 = 8
NCH1 = TLOC // C1
C2 = 16
NCH2 = NCH1 // C2
SLAB = 256
NSLAB = TLOC // SLAB
NGRP = TLOC // 128
EPS = 1e-5
DEBUG = False


class Sem:
    def __init__(self, handle):
        self.h = handle
        self.v = 0

    def inc(self, inst, n=1):
        inst.then_inc(self.h, n)
        self.v += n
        return inst


def build_nc():
    nc = bass.Bass("TRN2", target_bir_lowering=False,
                   detect_race_conditions=False)

    def din(name, shape, dt=F32):
        return nc.dram_tensor(name, list(shape), dt, kind="ExternalInput").ap()

    xf = din("xf", (D, TLOC))
    xt = din("xt", (TLOC, D))
    pf = din("pf", (D, TLOC))
    pt = din("pt", (TLOC, D))
    pmask = din("pmask", (128, 1))
    w_in = din("w_in", (D, DS))
    w_gate = din("w_gate", (D, D))
    w_out = din("w_out", (D, D))
    ct_d = din("ct", (DS, D))
    dt_d = din("dtw", (DS, D))
    a_n = din("a_n", (DS, DS))
    a_t = din("a_t", (DS, DS))
    bm_d = din("bm", (DS, DS))
    bm_t = din("bm_t", (DS, DS))
    id128 = din("id128", (128, 128))
    ternt = din("ternt", (D, D), I32)
    b_in_d = din("b_in", (128, SCH))
    b_gate_d = din("b_gate", (128, DCH))
    b_out_d = din("b_out", (128, DCH))
    b_ffn_d = din("b_ffn", (128, DCH))
    scal_d = din("scal", (1, 1))

    out_d = nc.dram_tensor("out", [D, TLOC], F32, kind="ExternalOutput").ap()
    dbg = {}
    if DEBUG:
        for name, shape in [("d_u", (128, SCH * TLOC)), ("d_up", (128, SCH * TLOC)),
                            ("d_e1", (128, SCH * NCH1)), ("d_i2", (128, SCH * NCH1)),
                            ("d_ball", (128, SCH * NCH1)),
                            ("d_m", (128, 2 * NGRP)), ("d_r", (128, 2 * NGRP)),
                            ("d_hst", (128, SCH)), ("d_e2", (128, SCH * NCH2)),
                            ("d_b2", (128, SCH * (NCH2 + 1))),
                            ("d_hs", (128, SCH * SLAB)), ("d_g", (128, DCH * SLAB)),
                            ("d_x2", (128, DCH * SLAB)), ("d_p8", (128, 512)),
                            ("d_q3", (128, 512)), ("d_r16", (128, 512)),
                            ("d_xq", (128, DCH * SLAB)),
                            ("d_st0", (128, 4 * NGRP)), ("d_st1", (128, 4 * NGRP)),
                            ("d_rc0", (128, 2 * NGRP)), ("d_rc1", (128, 2 * NGRP)),
                            ("d_sd0", (128, 2 * NGRP)), ("d_sd1", (128, 2 * NGRP))]:
            dbg[name] = nc.dram_tensor(name, list(shape), F32,
                                       kind="ExternalOutput").ap()

    def m_lhsT(tile, kc, mc):
        return tile[:, 256 * kc + 128 * mc: 256 * kc + 128 * mc + 128]

    def m_rhs(tile, kc):
        return tile[:, 256 * kc: 256 * kc + 256]

    def w_lhsT(tile, dout, kc, mc):
        return tile[:, dout * kc + 128 * mc: dout * kc + 128 * mc + 128]

    _es = ExitStack()
    with _es:
        t_wghi = _es.enter_context(nc.sbuf_tensor("t_wghi", [128, DCH * D], BF16))
        t_wglo = _es.enter_context(nc.sbuf_tensor("t_wglo", [128, DCH * D], BF16))
        t_wffn = _es.enter_context(nc.sbuf_tensor("t_wffn", [128, DCH * D], BF16))
        t_onesbf = _es.enter_context(nc.sbuf_tensor("t_onesbf", [128, 1], BF16))
        t_u = _es.enter_context(nc.sbuf_tensor("t_u", [128, SCH * TLOC], F32))
        t_bm = _es.enter_context(nc.sbuf_tensor("t_bm", [128, 512], F32))
        t_id128 = _es.enter_context(nc.sbuf_tensor("t_id128", [128, 128], F32))
        t_p = _es.enter_context(nc.sbuf_tensor("t_p", [128, 8 * 512], F32))
        t_q = _es.enter_context(nc.sbuf_tensor("t_q", [128, 7 * 512], F32))
        t_ball = _es.enter_context(nc.sbuf_tensor("t_ball", [128, SCH * NCH1], F32))
        t_hst = _es.enter_context(nc.sbuf_tensor("t_hst", [128, SCH], F32))
        t_mcols = _es.enter_context(nc.sbuf_tensor("t_mcols", [128, 2 * NGRP], F32))
        t_rcols = _es.enter_context(nc.sbuf_tensor("t_rcols", [128, 2 * NGRP], F32))
        t_scol = _es.enter_context(nc.sbuf_tensor("t_scol", [128, 1], F32))
        t_rsinv = _es.enter_context(nc.sbuf_tensor("t_rsinv", [128, 1], F32))
        t_pmask = _es.enter_context(nc.sbuf_tensor("t_pmask", [128, 1], F32))
        t_cwin = _es.enter_context(nc.sbuf_tensor("t_cwin", [128, SCH], F32))
        t_cwg = _es.enter_context(nc.sbuf_tensor("t_cwg", [128, DCH], F32))
        t_bin = _es.enter_context(nc.sbuf_tensor("t_bin", [128, SCH], F32))
        t_bgate = _es.enter_context(nc.sbuf_tensor("t_bgate", [128, DCH], F32))
        t_bout = _es.enter_context(nc.sbuf_tensor("t_bout", [128, DCH], F32))
        t_bffn = _es.enter_context(nc.sbuf_tensor("t_bffn", [128, DCH], F32))
        t_ones = _es.enter_context(nc.sbuf_tensor("t_ones", [128, 128], F32))
        t_onescol = _es.enter_context(nc.sbuf_tensor("t_onescol", [128, 1], F32))
        t_xf0 = _es.enter_context(nc.sbuf_tensor("t_xf0", [128, DCH * SLAB], F32))
        t_xf1 = _es.enter_context(nc.sbuf_tensor("t_xf1", [128, DCH * SLAB], F32))
        t_mbc0 = _es.enter_context(nc.sbuf_tensor("t_mbc0", [128, SLAB], F32))
        t_rbc0 = _es.enter_context(nc.sbuf_tensor("t_rbc0", [128, SLAB], F32))
        t_mbc1 = _es.enter_context(nc.sbuf_tensor("t_mbc1", [128, SLAB], F32))
        t_rbc1 = _es.enter_context(nc.sbuf_tensor("t_rbc1", [128, SLAB], F32))
        t_bcc = _es.enter_context(nc.sbuf_tensor("t_bcc", [128, 512], F32))
        t_rs2 = _es.enter_context(nc.sbuf_tensor("t_rs2", [128, 256], F32))
        t_mrs2 = _es.enter_context(nc.sbuf_tensor("t_mrs2", [128, 256], F32))
        t_qi = _es.enter_context(nc.sbuf_tensor("t_qi", [128, DCH * SLAB], mybir.dt.int8))
        t_fa = _es.enter_context(nc.sbuf_tensor("t_fa", [128, SLAB], F32))
        t_fb = _es.enter_context(nc.sbuf_tensor("t_fb", [128, SLAB], F32))
        pA = _es.enter_context(nc.psum_tensor("pA", [128, 512], F32))
        pB = _es.enter_context(nc.psum_tensor("pB", [128, 512], F32))
        pHS = _es.enter_context(nc.psum_tensor("pHS", [128, 512], F32))
        pST = _es.enter_context(nc.psum_tensor("pST", [128, 512], F32))
        pBC = _es.enter_context(nc.psum_tensor("pBC", [128, 512], F32))
        pE = _es.enter_context(nc.psum_tensor("pE", [128, 512], F32))
        pEb = _es.enter_context(nc.psum_tensor("pEb", [128, 512], F32))
        pSP = _es.enter_context(nc.psum_tensor("pSP", [128, 512], F32))
        _qS = _es.enter_context(nc.semaphore("qS"))
        _qG = _es.enter_context(nc.semaphore("qG"))
        _sT = _es.enter_context(nc.semaphore("sT"))
        _sV = _es.enter_context(nc.semaphore("sV"))
        _sC = _es.enter_context(nc.semaphore("sC"))
        _sO = _es.enter_context(nc.semaphore("sO"))
        _sD = _es.enter_context(nc.semaphore("sD"))
        qS, qG = Sem(_qS), Sem(_qG)
        sT, sV, sC, sO = Sem(_sT), Sem(_sV), Sem(_sC), Sem(_sO)
        sD = Sem(_sD)

        def vdrain(vector, inst):
            sD.inc(inst, 1)
            vector.wait_ge(sD.h, sD.v)
        xf_ring = [t_xf0, t_xf1]
        mbc_ring = [(t_mbc0, t_rbc0), (t_mbc1, t_rbc1)]

        def P(l):
            return t_p[:, (l - 1) * 512: l * 512]

        def Q(l):
            return t_bm if l == 0 else t_q[:, (l - 1) * 512: l * 512]

        def dma(eng, q, out, in_):
            q.inc(eng.dma_start(out=out, in_=in_), 16)

        with nc.sbuf_tensor("t_uprior", [128, SCH * TLOC], F32) as t_uprior:
            with (nc.sbuf_tensor("t_win", [128, DCH * DS], F32) as t_win,
                  nc.sbuf_tensor("t_tok0", [128, D], F32) as t_tok0,
                  nc.sbuf_tensor("t_tok1", [128, D], F32) as t_tok1,
                  nc.sbuf_tensor("t_tok2", [128, D], F32) as t_tok2,
                  nc.sbuf_tensor("t_tok3", [128, D], F32) as t_tok3,
                  nc.sbuf_tensor("t_toksq", [128, D], F32) as t_toksq,
                  nc.sbuf_tensor("t_st", [128, 4 * NGRP], F32) as t_st,
                  nc.sbuf_tensor("t_sa", [128, 256], F32) as t_sa,
                  nc.sbuf_tensor("t_sb", [128, 256], F32) as t_sb):
                tok_ring = [t_tok0, t_tok1, t_tok2, t_tok3]
                with nc.sbuf_tensor("t_tern", [128, 4 * D], I32) as t_tern, \
                        nc.sbuf_tensor("t_wgtmp", [128, 2 * D], F32) as t_wgtmp, \
                        nc.Block() as blk:
                    S0, G0, T0, V0, C0 = qS.v, qG.v, sT.v, sV.v, sC.v

                    @blk.sync
                    def _(sync):
                        for kc in range(DCH):
                            dma(sync, qS, t_win[:, kc * DS:(kc + 1) * DS],
                                w_in[kc * 128:(kc + 1) * 128, :])
                        dma(sync, qS, t_bm[:, 0:256], bm_d[0:128, :])
                        dma(sync, qS, t_bm[:, 256:512], bm_d[128:256, :])
                        dma(sync, qS, t_id128[:], id128[:])
                        dma(sync, qS, t_bin[:], b_in_d[:])
                        dma(sync, qS, t_bgate[:], b_gate_d[:])
                        dma(sync, qS, t_bout[:], b_out_d[:])
                        dma(sync, qS, t_bffn[:], b_ffn_d[:])
                        dma(sync, qS, t_pmask[:], pmask[:])
                        dma(sync, qS, t_scol[0:1, 0:1], scal_d[:])
                        for kc in range(DCH):
                            if kc >= 2:
                                sync.wait_ge(sV.h, V0 + 1 + kc)
                            dma(sync, qS,
                                t_wgtmp[:, (kc % 2) * D:(kc % 2 + 1) * D],
                                w_gate[kc * 128:(kc + 1) * 128, :])

                    @blk.gpsimd
                    def _(gpsimd):
                        for kc in range(4):
                            dma(gpsimd, qG, t_tern[:, kc * D:(kc + 1) * D],
                                ternt[kc * 128:(kc + 1) * 128, :])
                        gpsimd.wait_ge(sV.h, V0 + 14)
                        for kc in range(4):
                            dma(gpsimd, qG, t_tern[:, kc * D:(kc + 1) * D],
                                ternt[(4 + kc) * 128:(4 + kc + 1) * 128, :])

                    @blk.vector
                    def _(vector):
                        vector.memset(t_ones[:], 1.0 / 1024.0)
                        sV.inc(vector.memset(t_onescol[:], 1.0), 1)
                        vector.memset(t_onesbf[:], 1.0)
                        sV.inc(vector.memset(t_bcc[0:1, 0:128], 1.0), 1)
                        for kc in range(DCH):
                            vector.wait_ge(qS.h, S0 + 16 * (18 + kc))
                            wsrc = t_wgtmp[:, (kc % 2) * D:(kc % 2 + 1) * D]
                            hi = t_wghi[:, kc * D:(kc + 1) * D]
                            inst = vector.tensor_copy(hi, wsrc)
                            vdrain(vector, inst)
                            sV.inc(vector.tensor_tensor(
                                t_wglo[:, kc * D:(kc + 1) * D], wsrc, hi,
                                AL.subtract), 1)
                        vector.wait_ge(qG.h, G0 + 64)
                        for kc in range(4):
                            sV.inc(vector.tensor_scalar_add(
                                t_wffn[:, kc * D:(kc + 1) * D],
                                t_tern[:, kc * D:(kc + 1) * D], -1.0), 1)
                        vector.wait_ge(qG.h, G0 + 128)
                        for kc in range(4):
                            sV.inc(vector.tensor_scalar_add(
                                t_wffn[:, (4 + kc) * D:(5 + kc) * D],
                                t_tern[:, kc * D:(kc + 1) * D], -1.0), 1)
                        vector.wait_ge(sT.h, T0 + 2)
                        inst = vector.tensor_scalar(
                            t_scol[:, 0:1], pSP[:, 2:3], EPS, None, AL.max)
                        sV.inc(inst, 1)
                        vector.wait_ge(sV.h, sV.v)
                        sV.inc(vector.reciprocal(
                            t_rsinv[:, 0:1], t_scol[:, 0:1]), 1)
                        vector.wait_ge(sT.h, T0 + 2)
                        sV.inc(vector.tensor_copy(
                            t_cwin[:, :], pSP[:, 4:4 + SCH]), 1)

                    @blk.tensor
                    def _(tensor):
                        tensor.wait_ge(qS.h, S0 + 16 * 17)
                        tensor.wait_ge(sV.h, V0 + 2)
                        sT.inc(tensor.matmul(pSP[:, 2:3], t_bcc[0:1, 0:128],
                                             t_scol[0:1, 0:1],
                                             start=True, stop=True), 1)
                        for mc in range(SCH):
                            for kc in range(DCH):
                                inst = tensor.matmul(
                                    pSP[:, 4 + mc:5 + mc],
                                    w_lhsT(t_win, DS, kc, mc),
                                    t_onescol[:, 0:1],
                                    start=(kc == 0), stop=(kc == DCH - 1))
                        sT.inc(inst, 1)

                wg_split_done = V0 + 10

                for src in range(2):
                    xtok = pt if src == 0 else xt
                    xfeat = pf if src == 0 else xf
                    u_t = t_uprior if src == 0 else t_u
                    cbase = NGRP if src == 0 else 0

                    with nc.Block() as blk:
                        G0, S0st, C0, V0 = qG.v, qS.v, sC.v, sV.v

                        def _tok_wait(g):
                            return (max(C0, C0 + g - 3),
                                    max(V0, V0 + g - 3))

                        @blk.gpsimd
                        def _(gpsimd):
                            for g in range(0, NGRP, 2):
                                wc, wv = _tok_wait(g)
                                gpsimd.wait_ge(sC.h, wc)
                                gpsimd.wait_ge(sV.h, wv)
                                dma(gpsimd, qG, tok_ring[g % 4][:],
                                    xtok[g * 128:(g + 1) * 128, :])

                        @blk.sync
                        def _(sync):
                            for g in range(1, NGRP, 2):
                                wc, wv = _tok_wait(g)
                                sync.wait_ge(sC.h, wc)
                                sync.wait_ge(sV.h, wv)
                                dma(sync, qS, tok_ring[g % 4][:],
                                    xtok[g * 128:(g + 1) * 128, :])

                        def _tok_dma_wait(eng, g):
                            if g % 2 == 0:
                                eng.wait_ge(qG.h, G0 + 16 * (g // 2 + 1))
                            else:
                                eng.wait_ge(qS.h, S0st + 16 * ((g + 1) // 2))

                        @blk.scalar
                        def _(scalar):
                            scalar.wait_ge(sV.h, V0)
                            for g in range(NGRP):
                                _tok_dma_wait(scalar, g)
                                sC.inc(scalar.activation(
                                    t_toksq[:], tok_ring[g % 4][:], ACT.Square,
                                    accum_out=t_st[:, NGRP + g:NGRP + g + 1]), 1)

                        @blk.vector
                        def _(vector):
                            for g in range(NGRP):
                                _tok_dma_wait(vector, g)
                                sV.inc(vector.reduce_sum(
                                    t_st[:, g:g + 1], tok_ring[g % 4][:],
                                    AXX), 1)

                    with nc.Block() as blk:
                        V0, C0 = sV.v, sC.v
                        G0d = qG.v

                        if DEBUG:
                            @blk.gpsimd
                            def _(gpsimd):
                                gpsimd.wait_ge(sC.h, C0)
                                gpsimd.wait_ge(sV.h, V0)
                                dma(gpsimd, qG, dbg["d_st%d" % src][:, 0:2 * NGRP],
                                    t_st[:, 0:2 * NGRP])

                        @blk.vector
                        def _(vector):
                            vector.wait_ge(sC.h, C0)
                            vector.wait_ge(sT.h, sT.v)
                            vector.wait_ge(qS.h, qS.v)
                            vector.wait_ge(qG.h, qG.v)
                            if DEBUG:
                                vector.wait_ge(qG.h, G0d + 16)
                            mC = t_mcols[:, cbase:cbase + NGRP]
                            tA = t_st[:, 2 * NGRP:3 * NGRP]
                            tB = t_st[:, 3 * NGRP:4 * NGRP]
                            sV.inc(vector.tensor_scalar_mul(
                                mC, t_st[:, 0:NGRP], 1.0 / D), 1)
                            sV.inc(vector.tensor_scalar_mul(
                                tA, t_st[:, NGRP:2 * NGRP], 1.0 / D), 1)
                            vector.wait_ge(sV.h, sV.v)
                            sV.inc(vector.tensor_mul(tB, mC, mC), 1)
                            vector.wait_ge(sV.h, sV.v)
                            sV.inc(vector.scalar_tensor_tensor(
                                tB, tA, EPS, tB, AL.add, AL.subtract), 1)

                        @blk.scalar
                        def _(scalar):
                            scalar.wait_ge(sV.h, V0 + 4)
                            sC.inc(scalar.activation(
                                t_st[:, 0:NGRP], t_st[:, 3 * NGRP:4 * NGRP],
                                ACT.Sqrt), 1)

                    with nc.Block() as blk:
                        C0 = sC.v
                        V0p2 = sV.v
                        G0d2 = qG.v

                        if DEBUG:
                            @blk.gpsimd
                            def _(gpsimd):
                                gpsimd.wait_ge(sC.h, C0)
                                dma(gpsimd, qG, dbg["d_sd%d" % src][:, 0:NGRP],
                                    t_st[:, 0:NGRP])
                                dma(gpsimd, qG, dbg["d_sd%d" % src][:, NGRP:2 * NGRP],
                                    t_st[:, 3 * NGRP:4 * NGRP])

                        @blk.vector
                        def _(vector):
                            vector.wait_ge(sC.h, C0)
                            if DEBUG:
                                vector.wait_ge(qG.h, G0d2 + 32)
                            rC = t_rcols[:, cbase:cbase + NGRP]
                            sd = t_st[:, 0:NGRP]
                            ve = t_st[:, 3 * NGRP:4 * NGRP]
                            w_ = t_st[:, NGRP:2 * NGRP]
                            sV.inc(vector.reciprocal(rC, sd), 1)
                            vector.wait_ge(sV.h, sV.v)
                            sV.inc(vector.tensor_mul(w_, rC, rC), 1)
                            vector.wait_ge(sV.h, sV.v)
                            sV.inc(vector.scalar_tensor_tensor(
                                w_, ve, -0.5, w_, AL.mult, AL.mult), 1)
                            vector.wait_ge(sV.h, sV.v)
                            sV.inc(vector.scalar_tensor_tensor(
                                rC, w_, 1.5, rC, AL.add, AL.mult), 1)
                            vector.wait_ge(sV.h, sV.v)
                            sV.inc(vector.tensor_scalar_mul(rC, rC, -1.0), 1)

                        if DEBUG:
                            @blk.gpsimd
                            def _(gpsimd):
                                gpsimd.wait_ge(sV.h, V0p2 + 5)
                                dma(gpsimd, qG,
                                    dbg["d_rc%d" % src][:, cbase:cbase + NGRP],
                                    t_rcols[:, cbase:cbase + NGRP])

                    for s in range(NSLAB):
                        with nc.Block() as blk:
                            S0, T0, V0, C0 = qS.v, sT.v, sV.v, sC.v

                            @blk.sync
                            def _(sync):
                                sync.wait_ge(sT.h, max(0, T0 - 6))
                                src_ap = xfeat.rearrange(
                                    "(kc p) t -> p kc t", p=128)[
                                    :, :, s * SLAB:(s + 1) * SLAB]
                                dma(sync, qS, xf_ring[s % 2][:], src_ap)

                            @blk.vector
                            def _(vector):
                                vector.wait_ge(sT.h, max(0, T0 - 2))
                                for gg in range(2):
                                    gabs = cbase + 2 * s + gg
                                    vector.tensor_copy(
                                        t_bcc[:, gg * 128:(gg + 1) * 128],
                                        t_mcols[:, gabs:gabs + 1].broadcast_to(
                                            [128, 128]))
                                    inst = vector.tensor_copy(
                                        t_bcc[:, 256 + gg * 128:
                                              256 + (gg + 1) * 128],
                                        t_rcols[:, gabs:gabs + 1].broadcast_to(
                                            [128, 128]))
                                sV.inc(inst, 1)
                                vector.wait_ge(sT.h, T0 + 1)
                                mb, rb = mbc_ring[s % 2]
                                vector.tensor_copy(mb[:], pBC[:, 0:256])
                                sV.inc(vector.tensor_copy(
                                    rb[:], pBC[:, 256:512]), 1)
                                for mc in range(SCH):
                                    pb = pA if mc == 0 else pB
                                    vector.wait_ge(sT.h, T0 + 2 + mc)
                                    sc = t_sa if mc == 0 else t_sb
                                    vector.wait_ge(sC.h, max(0, C0 - 1 + mc))
                                    vector.scalar_tensor_tensor(
                                        sc[:], mb[:], t_cwin[:, mc:mc + 1],
                                        pb[:, 0:SLAB], AL.mult, AL.subtract)
                                    sV.inc(vector.tensor_mul(
                                        sc[:], sc[:], rb[:]), 1)

                            @blk.tensor
                            def _(tensor):
                                tensor.wait_ge(sV.h, V0 + 1)
                                for gg in range(2):
                                    tensor.matmul(
                                        pBC[:, gg * 128:(gg + 1) * 128],
                                        t_bcc[:, gg * 128:(gg + 1) * 128],
                                        t_id128[:], start=True, stop=True)
                                    inst = tensor.matmul(
                                        pBC[:, 256 + gg * 128:
                                            256 + (gg + 1) * 128],
                                        t_bcc[:, 256 + gg * 128:
                                              256 + (gg + 1) * 128],
                                        t_id128[:], start=True, stop=True)
                                sT.inc(inst, 1)
                                tensor.wait_ge(qS.h, S0 + 16)
                                tensor.wait_ge(sV.h, V0 + 2)
                                for mc in range(SCH):
                                    pb = pA if mc == 0 else pB
                                    if mc >= 1:
                                        tensor.wait_ge(sV.h, V0 + 2 + mc)
                                    for kc in range(DCH):
                                        inst = tensor.matmul(
                                            pb[:, 0:SLAB],
                                            w_lhsT(t_win, DS, kc, mc),
                                            xf_ring[s % 2][
                                                :, kc * SLAB:(kc + 1) * SLAB],
                                            start=(kc == 0),
                                            stop=(kc == DCH - 1))
                                    sT.inc(inst, 1)

                            @blk.scalar
                            def _(scalar):
                                for mc in range(SCH):
                                    scalar.wait_ge(sV.h, V0 + 3 + mc)
                                    sc = t_sa if mc == 0 else t_sb
                                    sC.inc(scalar.activation(
                                        u_t[:, TLOC * mc + s * SLAB:
                                            TLOC * mc + (s + 1) * SLAB],
                                        sc[:], ACT.Identity,
                                        bias=t_bin[:, mc:mc + 1]), 1)

            with (nc.sbuf_tensor("t_r", [128, 15 * 512], F32) as t_r,
                  nc.sbuf_tensor("t_rt0", [128, 512], F32) as t_rt0,
                  nc.sbuf_tensor("t_rt1", [128, 512], F32) as t_rt1,
                  nc.sbuf_tensor("t_at", [128, 512], F32) as t_at,
                  nc.sbuf_tensor("t_pt8", [128, 512], F32) as t_pt8,
                  nc.sbuf_tensor("t_bmt", [128, 512], F32) as t_bmt,
                  nc.sbuf_tensor("t_e1", [128, SCH * NCH1], F32) as t_e1,
                  nc.sbuf_tensor("t_i2", [128, SCH * NCH1], F32) as t_i2,
                  nc.sbuf_tensor("t_e2", [128, SCH * NCH2], F32) as t_e2,
                  nc.sbuf_tensor("t_b2", [128, SCH * (NCH2 + 1)], F32) as t_b2):

                def R(i):
                    return P(8) if i == 1 else t_r[:, (i - 2) * 512:(i - 1) * 512]

                jobs = []
                _prod = {}

                def J(dst, dk, lh, lk, rh, rk):
                    dep = max(_prod.get(lk, -1), _prod.get(rk, -1),
                              len(jobs) - 2)
                    jobs.append((dst, lh, rh, dep))
                    _prod[dk] = len(jobs) - 1
                for l in range(1, 8):
                    J(P(l + 1), f"P{l+1}", t_at, None, P(l), f"P{l}")
                    J(Q(l), f"Q{l}", t_bmt, None, P(l), f"P{l}")
                J(t_pt8, "pt8", P(7), "P7", t_at, None)
                J(R(2), "R2", t_pt8, "pt8", R(1), "P8")
                J(t_rt0, "rt0", R(1), "P8", t_pt8, "pt8")
                J(R(3), "R3", t_rt0, "rt0", R(1), "P8")
                J(R(4), "R4", t_rt0, "rt0", R(2), "R2")
                J(t_rt1, "rt1", R(2), "R2", t_rt0, "rt0")
                J(R(5), "R5", t_rt1, "rt1", R(1), "P8")
                J(R(6), "R6", t_rt1, "rt1", R(2), "R2")
                J(R(7), "R7", t_rt1, "rt1", R(3), "R3")
                J(R(8), "R8", t_rt1, "rt1", R(4), "R4")
                J(t_rt0, "rt0", R(4), "R4", t_rt1, "rt1")
                for i in range(1, 9):
                    J(R(8 + i), f"R{8+i}", t_rt0, "rt0",
                      R(i), "P8" if i == 1 else f"R{i}")

                with nc.Block() as blk:
                    S0, T0, V0 = qS.v, sT.v, sV.v

                    @blk.sync
                    def _(sync):
                        dma(sync, qS, P(1)[:, 0:256], a_n[0:128, :])
                        dma(sync, qS, P(1)[:, 256:512], a_n[128:256, :])
                        dma(sync, qS, t_at[:, 0:256], a_t[0:128, :])
                        dma(sync, qS, t_at[:, 256:512], a_t[128:256, :])
                        dma(sync, qS, t_bmt[:, 0:256], bm_t[0:128, :])
                        dma(sync, qS, t_bmt[:, 256:512], bm_t[128:256, :])

                    @blk.tensor
                    def _(tensor):
                        tensor.wait_ge(qS.h, S0 + 96)
                        for idx, (dst, lh, rh, dep) in enumerate(jobs):
                            bank = pE if idx % 2 == 0 else pEb
                            if dep >= 0:
                                tensor.wait_ge(sV.h, V0 + dep + 1)
                            for mc in range(SCH):
                                for kc in range(SCH):
                                    inst = tensor.matmul(
                                        bank[:, mc * 256:(mc + 1) * 256],
                                        m_lhsT(lh, kc, mc), m_rhs(rh, kc),
                                        start=(kc == 0), stop=(kc == SCH - 1))
                            sT.inc(inst, 1)

                    @blk.vector
                    def _(vector):
                        for idx, (dst, lh, rh, dep) in enumerate(jobs):
                            bank = pE if idx % 2 == 0 else pEb
                            vector.wait_ge(sT.h, T0 + idx + 1)
                            sV.inc(vector.tensor_copy(dst[:], bank[:]), 1)

                with nc.Block() as blk:
                    T0c = sT.v

                    @blk.tensor
                    def _(tensor):
                        tensor.wait_ge(sV.h, wg_split_done)
                        for mc in range(DCH):
                            for kc in range(DCH):
                                tensor.matmul(
                                    pSP[:, 8 + mc:9 + mc],
                                    w_lhsT(t_wghi, D, kc, mc),
                                    t_onesbf[:, 0:1],
                                    start=(kc == 0), stop=False)
                            for kc in range(DCH):
                                inst = tensor.matmul(
                                    pSP[:, 8 + mc:9 + mc],
                                    w_lhsT(t_wglo, D, kc, mc),
                                    t_onesbf[:, 0:1],
                                    start=False, stop=(kc == DCH - 1))
                        sT.inc(inst, 1)

                    @blk.vector
                    def _(vector):
                        vector.wait_ge(sT.h, T0c + 1)
                        sV.inc(vector.tensor_copy(
                            t_cwg[:, :], pSP[:, 8:8 + DCH]), 1)

                for src in range(2):
                    u_t = t_uprior if src == 0 else t_u

                    with nc.Block() as blk:
                        T0, V0 = sT.v, sV.v

                        @blk.tensor
                        def _(tensor):
                            tensor.wait_ge(sV.h, V0)
                            tensor.wait_ge(sC.h, sC.v)
                            for nco in range(SCH):
                                for i in range(C1):
                                    for sc in range(SCH):
                                        uv = u_t[:, TLOC * sc:TLOC * (sc + 1)]
                                        uv = uv.rearrange(
                                            "p (K j c) -> p j K c",
                                            K=NCH2, j=C2, c=C1)[:, :, :, C1 - 1 - i]
                                        inst = tensor.matmul(
                                            pE[:, nco * 256:(nco + 1) * 256],
                                            m_lhsT(Q(i), sc, nco), uv,
                                            start=(nco == 0 and i == 0 and sc == 0),
                                            stop=(nco == SCH - 1 and i == C1 - 1
                                                  and sc == SCH - 1))
                            sT.inc(inst, 1)
                            tensor.wait_ge(sV.h, V0 + 2)
                            for j in range(2, C2):
                                for nco in range(SCH):
                                    for kc in range(SCH):
                                        n_el = (C2 - j) * NCH2
                                        ev = t_e1[:, kc * 256:kc * 256 + n_el]
                                        ov = pEb[:, nco * 256 + j * NCH2:
                                                 (nco + 1) * 256]
                                        inst = tensor.matmul(
                                            ov, m_lhsT(R(j - 1), kc, nco), ev,
                                            start=False, stop=(kc == SCH - 1),
                                            skip_group_check=True)
                            sT.inc(inst, 1)
                            tensor.wait_ge(sV.h, V0 + 4)
                            for nco in range(SCH):
                                for kc in range(SCH):
                                    iv = t_i2[:, kc * 256 + (C2 - 1) * NCH2:
                                              kc * 256 + 256]
                                    inst = tensor.matmul(
                                        pSP[:, nco * 256:nco * 256 + NCH2],
                                        m_lhsT(R(1), kc, nco), iv,
                                        start=(kc == 0), stop=(kc == SCH - 1))
                            sT.inc(inst, 1)
                            for K in range(NCH2):
                                tensor.wait_ge(sV.h, V0 + 6 + K)
                                for nco in range(SCH):
                                    for kc in range(SCH):
                                        inst = tensor.matmul(
                                            pSP[:, nco * 256 + 32:nco * 256 + 33],
                                            m_lhsT(R(16), kc, nco),
                                            t_b2[:, kc * (NCH2 + 1) + K:
                                                 kc * (NCH2 + 1) + K + 1],
                                            start=(kc == 0), stop=(kc == SCH - 1))
                                sT.inc(inst, 1)
                            if src == 1:
                                tensor.wait_ge(sV.h, V0 + 6 + NCH2)
                                for j in range(1, C2):
                                    for nco in range(SCH):
                                        for kc in range(SCH):
                                            ov = pE[:, nco * 256 + j * NCH2:
                                                    nco * 256 + (j + 1) * NCH2]
                                            inst = tensor.matmul(
                                                ov, m_lhsT(R(j), kc, nco),
                                                t_b2[:, kc * (NCH2 + 1):
                                                     kc * (NCH2 + 1) + NCH2],
                                                start=False, stop=(kc == SCH - 1),
                                                skip_group_check=True)
                                sT.inc(inst, 1)

                        @blk.vector
                        def _(vector):
                            vector.wait_ge(sT.h, T0 + 1)
                            sV.inc(vector.tensor_copy(t_e1[:], pE[:]), 1)
                            sV.inc(vector.memset(pEb[:], 0.0), 1)
                            vector.wait_ge(sT.h, T0 + 2)
                            vector.tensor_copy(t_i2[:], pEb[:])
                            for nco in range(SCH):
                                ov = t_i2[:, nco * 256 + NCH2:(nco + 1) * 256]
                                ev = t_e1[:, nco * 256:
                                          nco * 256 + (C2 - 1) * NCH2]
                                inst = vector.tensor_tensor(ov, ov, ev, AL.add)
                            sV.inc(inst, 1)
                            sV.inc(vector.memset(pE[:], 0.0), 1)
                            vector.wait_ge(sT.h, T0 + 3)
                            for nco in range(SCH):
                                ev = t_e1[:, nco * 256 + (C2 - 1) * NCH2:
                                          nco * 256 + 256]
                                inst = vector.tensor_tensor(
                                    t_e2[:, nco * NCH2:(nco + 1) * NCH2],
                                    pSP[:, nco * 256:nco * 256 + NCH2],
                                    ev, AL.add)
                            sV.inc(inst, 1)
                            if src == 0:
                                for sc in range(SCH):
                                    inst = vector.memset(
                                        t_b2[:, sc * (NCH2 + 1):
                                             sc * (NCH2 + 1) + 1], 0.0)
                            else:
                                for sc in range(SCH):
                                    inst = vector.tensor_copy(
                                        t_b2[:, sc * (NCH2 + 1):
                                             sc * (NCH2 + 1) + 1],
                                        t_hst[:, sc:sc + 1])
                            sV.inc(inst, 1)
                            for K in range(NCH2):
                                vector.wait_ge(sT.h, T0 + 4 + K)
                                for nco in range(SCH):
                                    inst = vector.tensor_tensor(
                                        t_b2[:, nco * (NCH2 + 1) + K + 1:
                                             nco * (NCH2 + 1) + K + 2],
                                        pSP[:, nco * 256 + 32:nco * 256 + 33],
                                        t_e2[:, nco * NCH2 + K:
                                             nco * NCH2 + K + 1], AL.add)
                                sV.inc(inst, 1)
                            if src == 0:
                                for sc in range(SCH):
                                    inst = vector.tensor_scalar_mul(
                                        t_hst[:, sc:sc + 1],
                                        t_b2[:, sc * (NCH2 + 1) + NCH2:
                                             sc * (NCH2 + 1) + NCH2 + 1],
                                        t_pmask[:, 0:1])
                                sV.inc(inst, 1)
                            else:
                                vector.wait_ge(sT.h, T0 + 4 + NCH2)
                                inst = vector.tensor_tensor(
                                    t_ball[:], pE[:], t_i2[:], AL.add)
                                for nco in range(SCH):
                                    bv = t_ball[:, nco * 256:nco * 256 + NCH2]
                                    inst = vector.tensor_tensor(
                                        bv, bv,
                                        t_b2[:, nco * (NCH2 + 1):
                                             nco * (NCH2 + 1) + NCH2], AL.add)
                                sV.inc(inst, 1)

                if DEBUG:
                    with nc.Block() as blk:
                        @blk.gpsimd
                        def _(gpsimd):
                            gpsimd.wait_ge(sV.h, sV.v)
                            gpsimd.wait_ge(sC.h, sC.v)
                            for name, tile in [("d_u", t_u), ("d_up", t_uprior),
                                               ("d_e1", t_e1), ("d_i2", t_i2),
                                               ("d_ball", t_ball),
                                               ("d_m", t_mcols), ("d_r", t_rcols),
                                               ("d_hst", t_hst), ("d_e2", t_e2),
                                               ("d_b2", t_b2), ("d_p8", P(8)),
                                               ("d_q3", Q(3)), ("d_r16", R(16))]:
                                dma(gpsimd, qG, dbg[name][:], tile[:])

        with (nc.sbuf_tensor("t_wohi", [128, DCH * D], BF16) as t_wohi,
              nc.sbuf_tensor("t_wolo", [128, DCH * D], BF16) as t_wolo,
              nc.sbuf_tensor("t_bf", [128, DCH * SLAB], BF16) as t_bf,
              nc.sbuf_tensor("t_lo", [128, DCH * SLAB], BF16) as t_lo,
              nc.sbuf_tensor("t_ct", [128, SCH * D], F32) as t_ct,
              nc.sbuf_tensor("t_dt", [128, SCH * D], F32) as t_dt,
              nc.sbuf_tensor("t_gy", [128, DCH * SLAB], F32) as t_gy,
              nc.sbuf_tensor("t_hs", [128, SCH * SLAB], F32) as t_hs,
              nc.sbuf_tensor("t_x2", [128, DCH * SLAB], F32) as t_x2,
              nc.sbuf_tensor("t_xq", [128, DCH * SLAB], BF16) as t_xq):

            with nc.Block() as blk:
                S0, V0 = qS.v, sV.v

                @blk.sync
                def _(sync):
                    for kc in range(SCH):
                        dma(sync, qS, t_ct[:, kc * D:(kc + 1) * D],
                            ct_d[kc * 128:(kc + 1) * 128, :])
                    for kc in range(SCH):
                        dma(sync, qS, t_dt[:, kc * D:(kc + 1) * D],
                            dt_d[kc * 128:(kc + 1) * 128, :])
                    for kc in range(DCH):
                        if kc >= 2:
                            sync.wait_ge(sV.h, V0 + kc - 1)
                        dma(sync, qS,
                            t_x2[:, (kc % 2) * D:(kc % 2 + 1) * D],
                            w_out[kc * 128:(kc + 1) * 128, :])

                @blk.vector
                def _(vector):
                    for kc in range(DCH):
                        vector.wait_ge(qS.h, S0 + 16 * (5 + kc))
                        wsrc = t_x2[:, (kc % 2) * D:(kc % 2 + 1) * D]
                        hi = t_wohi[:, kc * D:(kc + 1) * D]
                        inst = vector.tensor_copy(hi, wsrc)
                        vdrain(vector, inst)
                        sV.inc(vector.tensor_tensor(
                            t_wolo[:, kc * D:(kc + 1) * D], wsrc, hi,
                            AL.subtract), 1)

            with nc.Block() as blk:
                P0V, P0T = sV.v, sT.v

                @blk.vector
                def _(vector):
                    vector.wait_ge(sT.h, P0T)
                    for gg in range(2):
                        vector.tensor_copy(
                            t_bcc[:, gg * 128:(gg + 1) * 128],
                            t_mcols[:, gg:gg + 1].broadcast_to([128, 128]))
                        inst = vector.tensor_copy(
                            t_bcc[:, 256 + gg * 128:256 + (gg + 1) * 128],
                            t_rcols[:, gg:gg + 1].broadcast_to([128, 128]))
                    sV.inc(inst, 1)
                    vector.wait_ge(sT.h, P0T + 1)
                    mb, rb = mbc_ring[0]
                    vector.tensor_copy(mb[:], pBC[:, 0:256])
                    sV.inc(vector.tensor_copy(rb[:], pBC[:, 256:512]), 1)

                @blk.tensor
                def _(tensor):
                    tensor.wait_ge(sV.h, P0V + 1)
                    for gg in range(2):
                        tensor.matmul(
                            pBC[:, gg * 128:(gg + 1) * 128],
                            t_bcc[:, gg * 128:(gg + 1) * 128],
                            t_id128[:], start=True, stop=True)
                        inst = tensor.matmul(
                            pBC[:, 256 + gg * 128:256 + (gg + 1) * 128],
                            t_bcc[:, 256 + gg * 128:256 + (gg + 1) * 128],
                            t_id128[:], start=True, stop=True)
                    sT.inc(inst, 1)

            pre_bcast_cnt = sT.v
            pre_mbc_cnt = sV.v
            rot_banks = [pA, pB, pE, pEb, pSP]
            bank_hist = [("sV", sV.v)] * 5
            gbank = [5]
            prev = None
            for s in range(NSLAB + 1):
                with nc.Block() as blk:
                    S0, G0, T0, V0, C0, O0 = qS.v, qG.v, sT.v, sV.v, sC.v, sO.v
                    last = s == NSLAB
                    pv = prev
                    ev = {}

                    t = T0
                    if not last:
                        for mc in range(DCH):
                            t += 1
                            ev[("t_gate", mc)] = t
                        if s + 1 < NSLAB:
                            t += 1
                            ev["t_bcast"] = t
                    if pv is not None:
                        for mc in range(DCH):
                            t += 1
                            ev[("t_ffn", mc)] = t
                    if not last:
                        t += 1
                        ev["t_lag"] = t
                        for mc in range(DCH):
                            t += 1
                            ev[("t_cd", mc)] = t
                        for mc in range(DCH):
                            t += 1
                            ev[("t_wout", mc)] = t
                        t += 1
                        ev["t_stats"] = t
                    ev["t_end"] = t

                    v = V0
                    _gq = list(range(DCH)) if not last else []

                    def _pop(n, v):
                        for _ in range(n):
                            if _gq:
                                v += 1
                                ev[("v_gate", _gq.pop(0))] = v
                        return v
                    if not last:
                        v += 1
                        ev["v_xsplit"] = v
                    if not last and s + 1 < NSLAB:
                        v += 1
                        ev["v_bcc"] = v
                    v = _pop(2, v)
                    if pv is not None:
                        v += 1
                        ev["v_ln2a"] = v
                        v += 1
                        ev["v_ln2b"] = v
                        v += 1
                        ev["v_qgy"] = v
                        v = _pop(DCH, v)
                        v += 1
                        ev["v_quant"] = v
                    v = _pop(DCH, v)
                    if pv is not None:
                        for mc in range(DCH):
                            v += 1
                            ev[("v_fres", mc)] = v
                    if not last:
                        v += 1
                        ev["v_hs"] = v
                        for mc in range(DCH):
                            v += 1
                            ev[("v_y", mc)] = v
                        v += 1
                        ev["v_gysplit"] = v
                        for mc in range(DCH):
                            v += 1
                            ev[("v_x2", mc)] = v
                        if s + 1 < NSLAB:
                            v += 1
                            ev["v_mbc"] = v
                    ev["v_end"] = v

                    c = C0
                    if not last:
                        c += 1
                        ev["c_xcast"] = c
                    if pv is not None:
                        c += 1
                        ev["c_sqrt"] = c
                    if not last:
                        for mc in range(DCH):
                            c += 1
                            ev[("c_sig", mc)] = c
                    if pv is not None:
                        for mc in range(DCH):
                            c += 1
                            ev[("c_fevac", mc)] = c
                    if not last:
                        c += 1
                        ev["c_gycast"] = c
                        for mc in range(DCH):
                            c += 1
                            ev[("c_sq", mc)] = c
                    ev["c_end"] = c

                    banks = {}
                    bank_wait = {}
                    plan = []
                    if not last:
                        plan += [("gate", mc, ("sV", ev[("v_gate", mc)]))
                                 for mc in range(DCH)]
                    if pv is not None:
                        gi = len([p for p in plan])
                        plan = (plan[:DCH if not last else 0]
                                if False else plan)
                    plan2 = []
                    if not last:
                        plan2 += [("gate", mc, ("sV", ev[("v_gate", mc)]))
                                  for mc in range(DCH)]
                    if pv is not None:
                        plan2 += [("ffn", mc, ("sC", ev[("c_fevac", mc)]))
                                  for mc in range(DCH)]
                    if not last:
                        plan2 += [("cd", mc, ("sV", ev[("v_y", mc)]))
                                  for mc in range(DCH)]
                        plan2 += [("wout", mc, ("sV", ev[("v_x2", mc)]))
                                  for mc in range(DCH)]
                    for stage, mc, consumer in plan2:
                        i = gbank[0]
                        banks[(stage, mc)] = rot_banks[i % 5]
                        bank_wait[(stage, mc)] = (
                            bank_hist[i - 5] if i >= 5 else None)
                        bank_hist.append(consumer)
                        gbank[0] = i + 1

                    def bwait(eng, key):
                        w = bank_wait[key]
                        if w is not None:
                            nm, cnt = w
                            eng.wait_ge(sV.h if nm == "sV" else sC.h, cnt)

                    @blk.sync
                    def _(sync):
                        if not last:
                            if pv is not None and pv.get("prev_x2") is not None:
                                sync.wait_ge(sV.h, pv["prev_x2"])
                            src_ap = xf.rearrange(
                                "(kc p) t -> p kc t", p=128)[
                                :, :, s * SLAB:(s + 1) * SLAB]
                            dma(sync, qS, xf_ring[s % 2][:], src_ap)

                    @blk.vector
                    def _(vector):
                        if not last:
                            vector.wait_ge(sC.h, ev["c_xcast"])
                            sV.inc(vector.tensor_tensor(
                                t_lo[:], xf_ring[s % 2][:], t_bf[:],
                                AL.subtract), 1)
                        if not last and s + 1 < NSLAB:
                            vector.wait_ge(sT.h, pv["t_bcast_prev_cnt"]
                                           if s >= 1 else pre_bcast_cnt)
                            for gg in range(2):
                                gabs = 2 * (s + 1) + gg
                                vector.tensor_copy(
                                    t_bcc[:, gg * 128:(gg + 1) * 128],
                                    t_mcols[:, gabs:gabs + 1].broadcast_to(
                                        [128, 128]))
                                inst = vector.tensor_copy(
                                    t_bcc[:, 256 + gg * 128:256 + (gg + 1) * 128],
                                    t_rcols[:, gabs:gabs + 1].broadcast_to(
                                        [128, 128]))
                            sV.inc(inst, 1)

                        gate_q = list(range(DCH)) if not last else []
                        mb, rb = mbc_ring[s % 2]

                        def emit_gate_evac(mc):
                            pb = banks[("gate", mc)]
                            slot = t_hs[:, (mc % 2) * SLAB:(mc % 2 + 1) * SLAB]
                            vector.wait_ge(sT.h, ev[("t_gate", mc)])
                            if mc >= 2:
                                vector.wait_ge(sC.h, ev[("c_sig", mc - 2)])
                            vector.scalar_tensor_tensor(
                                slot, mb[:],
                                t_cwg[:, mc:mc + 1],
                                pb[:, 0:SLAB], AL.mult, AL.subtract)
                            sV.inc(vector.tensor_mul(
                                slot, slot,
                                rb[:]), 1)

                        def g1(n=1):
                            for _i in range(n):
                                if gate_q:
                                    emit_gate_evac(gate_q.pop(0))

                        g1(2)
                        if pv is not None:
                            vector.wait_ge(sT.h, pv["t_stats"])
                            vector.tensor_copy(t_fa[:], pST[:, 0:256])
                            inst = vector.tensor_copy(t_mrs2[:],
                                                      pST[:, 256:512])
                            vdrain(vector, inst)
                            inst = vector.tensor_mul(t_fb[:], t_fa[:], t_fa[:])
                            vdrain(vector, inst)
                            sV.inc(vector.scalar_tensor_tensor(
                                t_fb[:], t_mrs2[:], EPS, t_fb[:],
                                AL.add, AL.subtract), 1)
                            vector.wait_ge(sC.h, ev["c_sqrt"])
                            inst = vector.reciprocal(t_rs2[:], t_mrs2[:])
                            vdrain(vector, inst)
                            inst = vector.tensor_mul(t_mrs2[:], t_rs2[:],
                                                     t_rs2[:])
                            vdrain(vector, inst)
                            inst = vector.scalar_tensor_tensor(
                                t_mrs2[:], t_fb[:], -0.5, t_mrs2[:],
                                AL.mult, AL.mult)
                            vdrain(vector, inst)
                            inst = vector.scalar_tensor_tensor(
                                t_rs2[:], t_mrs2[:], 1.5, t_rs2[:],
                                AL.add, AL.mult)
                            vdrain(vector, inst)
                            inst = vector.tensor_scalar_mul(
                                t_rs2[:], t_rs2[:], t_rsinv[:, 0:1])
                            vdrain(vector, inst)
                            sV.inc(vector.tensor_tensor(
                                t_mrs2[:], t_fa[:], t_rs2[:], AL.mult), 1)
                            rsb = t_rs2[:, 0:SLAB].unsqueeze(1).broadcast_to(
                                [128, DCH, SLAB])
                            mrsb = t_mrs2[:, 0:SLAB].unsqueeze(1).broadcast_to(
                                [128, DCH, SLAB])
                            x2v = t_x2[:].rearrange("p (m c) -> p m c", c=SLAB)
                            gyv = t_gy[:].rearrange("p (m c) -> p m c", c=SLAB)
                            inst = vector.tensor_tensor(gyv, x2v, rsb, AL.mult)
                            vdrain(vector, inst)
                            inst = vector.tensor_tensor(gyv, gyv, mrsb,
                                                        AL.subtract)
                            vdrain(vector, inst)
                            inst = vector.tensor_scalar(t_gy[:], t_gy[:],
                                                        127.0, -127.0,
                                                        AL.min, AL.max)
                            vdrain(vector, inst)
                            inst = vector.tensor_copy(t_qi[:], t_gy[:])
                            vdrain(vector, inst)
                            sV.inc(vector.tensor_tensor(
                                t_xq[:], t_gy[:], t_qi[:],
                                AL.subtract), 1)
                            vector.wait_ge(sD.h, sD.v)
                            g1(DCH)
                            inst = vector.scalar_tensor_tensor(
                                t_qi[:], t_xq[:], 0.5, t_qi[:],
                                AL.is_gt, AL.add)
                            vdrain(vector, inst)
                            inst = vector.scalar_tensor_tensor(
                                t_qi[:], t_xq[:], -0.5, t_qi[:],
                                AL.is_lt, AL.subtract)
                            vdrain(vector, inst)
                            sV.inc(vector.tensor_scalar_mul(
                                t_xq[:], t_qi[:], -1.0), 1)
                        g1(DCH)
                        if pv is not None:
                            for mc in range(DCH):
                                vector.wait_ge(sC.h, ev[("c_fevac", mc)])
                                fb = t_fa if mc % 2 == 0 else t_fb
                                sV.inc(vector.tensor_tensor(
                                    t_x2[:, mc * SLAB:(mc + 1) * SLAB],
                                    fb[:],
                                    t_x2[:, mc * SLAB:(mc + 1) * SLAB],
                                    AL.add), 1)
                        if last:
                            return
                        vector.wait_ge(sC.h, ev[("c_sig", DCH - 1)])
                        vector.wait_ge(sT.h, ev["t_lag"])
                        sV.inc(vector.tensor_copy(t_hs[:], pHS[:]), 1)
                        vector.wait_ge(sC.h, ev[("c_sig", DCH - 1)])
                        if DEBUG and s == 1:
                            vector.wait_ge(qG.h, G0 + 32)
                        for mc in range(DCH):
                            pb = banks[("cd", mc)]
                            vector.wait_ge(sT.h, ev[("t_cd", mc)])
                            sV.inc(vector.tensor_tensor(
                                t_gy[:, mc * SLAB:(mc + 1) * SLAB],
                                pb[:, 0:SLAB],
                                t_gy[:, mc * SLAB:(mc + 1) * SLAB],
                                AL.mult), 1)
                        vector.wait_ge(sC.h, ev["c_gycast"])
                        sV.inc(vector.tensor_tensor(
                            t_lo[:], t_gy[:], t_bf[:],
                            AL.subtract), 1)
                        if pv is not None:
                            vector.wait_ge(sO.h, O0 + 16)
                        for mc in range(DCH):
                            pb = banks[("wout", mc)]
                            vector.wait_ge(sT.h, ev[("t_wout", mc)])
                            sV.inc(vector.scalar_tensor_tensor(
                                t_x2[:, mc * SLAB:(mc + 1) * SLAB],
                                pb[:, 0:SLAB], t_bout[:, mc:mc + 1],
                                xf_ring[s % 2][:, mc * SLAB:(mc + 1) * SLAB],
                                AL.add, AL.add), 1)
                        if s + 1 < NSLAB:
                            vector.wait_ge(sT.h, ev["t_bcast"])
                            mb, rb = mbc_ring[(s + 1) % 2]
                            vector.tensor_copy(mb[:], pBC[:, 0:256])
                            sV.inc(vector.tensor_copy(rb[:], pBC[:, 256:512]), 1)

                    @blk.scalar
                    def _(scalar):
                        if not last:
                            if pv is not None:
                                scalar.wait_ge(sT.h, pv[("t_wout", DCH - 1)])
                            scalar.wait_ge(qS.h, S0 + 16)
                            sC.inc(scalar.activation(
                                t_bf[:], xf_ring[s % 2][:], ACT.Copy), 1)
                        if pv is not None:
                            scalar.wait_ge(sV.h, ev["v_ln2a"])
                            sC.inc(scalar.activation(t_mrs2[:], t_fb[:],
                                                     ACT.Sqrt), 1)
                        if not last:
                            if pv is not None:
                                scalar.wait_ge(sV.h, ev["v_qgy"])
                            for mc in range(DCH):
                                scalar.wait_ge(sV.h, ev[("v_gate", mc)])
                                sC.inc(scalar.activation(
                                    t_gy[:, mc * SLAB:(mc + 1) * SLAB],
                                    t_hs[:, (mc % 2) * SLAB:
                                         (mc % 2 + 1) * SLAB],
                                    ACT.Sigmoid, bias=t_bgate[:, mc:mc + 1]), 1)
                        if pv is not None:
                            for mc in range(DCH):
                                pb = banks[("ffn", mc)]
                                fb = t_fa if mc % 2 == 0 else t_fb
                                scalar.wait_ge(sT.h, ev[("t_ffn", mc)])
                                if mc >= 2:
                                    scalar.wait_ge(sV.h, ev[("v_fres", mc - 2)])
                                sC.inc(scalar.activation(
                                    fb[:],
                                    pb[:, 0:SLAB], ACT.Identity,
                                    bias=t_bffn[:, mc:mc + 1],
                                    scale=t_scol[:, 0:1]), 1)
                        if last:
                            return
                        scalar.wait_ge(sV.h, ev[("v_y", DCH - 1)])
                        sC.inc(scalar.activation(
                            t_bf[:], t_gy[:], ACT.Copy), 1)
                        scalar.wait_ge(sT.h, ev[("t_wout", DCH - 1)])
                        for mc in range(DCH):
                            scalar.wait_ge(sV.h, ev[("v_x2", mc)])
                            sC.inc(scalar.activation(
                                t_gy[:, mc * SLAB:(mc + 1) * SLAB],
                                t_x2[:, mc * SLAB:(mc + 1) * SLAB],
                                ACT.Square), 1)

                    @blk.tensor
                    def _(tensor):
                        if not last:
                            tensor.wait_ge(qS.h, S0 + 16)
                            tensor.wait_ge(sV.h, ev["v_xsplit"])
                            for mc in range(DCH):
                                pb = banks[("gate", mc)]
                                bwait(tensor, ("gate", mc))
                                for kc in range(DCH):
                                    tensor.matmul(
                                        pb[:, 0:SLAB],
                                        w_lhsT(t_wghi, D, kc, mc),
                                        t_bf[:, kc * SLAB:(kc + 1) * SLAB],
                                        start=(kc == 0), stop=False)
                                for kc in range(DCH):
                                    tensor.matmul(
                                        pb[:, 0:SLAB],
                                        w_lhsT(t_wglo, D, kc, mc),
                                        t_bf[:, kc * SLAB:(kc + 1) * SLAB],
                                        start=False, stop=False)
                                for kc in range(DCH):
                                    inst = tensor.matmul(
                                        pb[:, 0:SLAB],
                                        w_lhsT(t_wghi, D, kc, mc),
                                        t_lo[:, kc * SLAB:(kc + 1) * SLAB],
                                        start=False, stop=(kc == DCH - 1))
                                sT.inc(inst, 1)
                        if not last and s + 1 < NSLAB:
                            tensor.wait_ge(sV.h, ev["v_bcc"])
                            tensor.wait_ge(sV.h, pv["v_mbc"]
                                           if pv is not None else pre_mbc_cnt)
                            for gg in range(2):
                                tensor.matmul(
                                    pBC[:, gg * 128:(gg + 1) * 128],
                                    t_bcc[:, gg * 128:(gg + 1) * 128],
                                    t_id128[:], start=True, stop=True)
                                inst = tensor.matmul(
                                    pBC[:, 256 + gg * 128:256 + (gg + 1) * 128],
                                    t_bcc[:, 256 + gg * 128:
                                          256 + (gg + 1) * 128],
                                    t_id128[:], start=True, stop=True)
                            sT.inc(inst, 1)
                        if pv is not None:
                            tensor.wait_ge(sV.h, ev["v_quant"])
                            for mc in range(DCH):
                                pb = banks[("ffn", mc)]
                                bwait(tensor, ("ffn", mc))
                                for kc in range(DCH):
                                    inst = tensor.matmul(
                                        pb[:, 0:SLAB],
                                        w_lhsT(t_wffn, D, kc, mc),
                                        t_xq[:, kc * SLAB:(kc + 1) * SLAB],
                                        start=(kc == 0), stop=(kc == DCH - 1))
                                sT.inc(inst, 1)
                        if last:
                            return
                        if pv is not None:
                            tensor.wait_ge(sV.h, pv["v_hs"])
                        nch_s = SLAB // C1
                        for nco in range(SCH):
                            for l in range(C1):
                                for sc in range(SCH):
                                    uv = t_u[:, TLOC * sc + s * SLAB:
                                             TLOC * sc + (s + 1) * SLAB]
                                    uv = uv.rearrange("p (ch c) -> p c ch",
                                                      c=C1)[:, 0:C1 - l, :]
                                    ov = pHS[:, nco * 256 + l * nch_s:
                                             (nco + 1) * 256]
                                    st = (nco == 0 and l == 0 and sc == 0)
                                    inst = tensor.matmul(
                                        ov, m_lhsT(Q(l), sc, nco), uv,
                                        start=st, stop=False)
                            for j in range(C1):
                                for sc in range(SCH):
                                    ov = pHS[:, nco * 256 + j * nch_s:
                                             nco * 256 + (j + 1) * nch_s]
                                    bv = t_ball[:, sc * 256:(sc + 1) * 256]
                                    bv = bv.rearrange("p (j K) -> p K j",
                                                      K=NCH2)[:, 2 * s:2 * s + 2, :]
                                    inst = tensor.matmul(
                                        ov, m_lhsT(P(j + 1), sc, nco), bv,
                                        start=False,
                                        stop=(nco == SCH - 1 and j == C1 - 1
                                              and sc == SCH - 1))
                        sT.inc(inst, 1)
                        tensor.wait_ge(sV.h, ev["v_hs"])
                        for mc in range(DCH):
                            pb = banks[("cd", mc)]
                            bwait(tensor, ("cd", mc))
                            for sc in range(SCH):
                                hv = t_hs[:, sc * SLAB:(sc + 1) * SLAB]
                                hv = hv.rearrange("p (c ch) -> p ch c", c=C1)
                                tensor.matmul(
                                    pb[:, 0:SLAB], w_lhsT(t_ct, D, sc, mc),
                                    hv, start=(sc == 0), stop=False)
                            for sc in range(SCH):
                                inst = tensor.matmul(
                                    pb[:, 0:SLAB], w_lhsT(t_dt, D, sc, mc),
                                    t_u[:, TLOC * sc + s * SLAB:
                                        TLOC * sc + (s + 1) * SLAB],
                                    start=False, stop=(sc == SCH - 1))
                            sT.inc(inst, 1)
                        tensor.wait_ge(sV.h, ev["v_gysplit"])
                        for mc in range(DCH):
                            pb = banks[("wout", mc)]
                            bwait(tensor, ("wout", mc))
                            for kc in range(DCH):
                                tensor.matmul(
                                    pb[:, 0:SLAB], w_lhsT(t_wohi, D, kc, mc),
                                    t_bf[:, kc * SLAB:(kc + 1) * SLAB],
                                    start=(kc == 0), stop=False)
                            for kc in range(DCH):
                                tensor.matmul(
                                    pb[:, 0:SLAB], w_lhsT(t_wolo, D, kc, mc),
                                    t_bf[:, kc * SLAB:(kc + 1) * SLAB],
                                    start=False, stop=False)
                            for kc in range(DCH):
                                inst = tensor.matmul(
                                    pb[:, 0:SLAB], w_lhsT(t_wohi, D, kc, mc),
                                    t_lo[:, kc * SLAB:(kc + 1) * SLAB],
                                    start=False, stop=(kc == DCH - 1))
                            sT.inc(inst, 1)
                        if pv is not None:
                            tensor.wait_ge(sV.h, ev["v_ln2a"])
                        tensor.wait_ge(sV.h, ev[("v_x2", DCH - 1)])
                        tensor.wait_ge(sC.h, ev[("c_sq", DCH - 1)])
                        for kc in range(DCH):
                            tensor.matmul(
                                pST[:, 0:256], t_ones[:],
                                t_x2[:, kc * SLAB:(kc + 1) * SLAB],
                                start=(kc == 0), stop=(kc == DCH - 1))
                        for kc in range(DCH):
                            inst = tensor.matmul(
                                pST[:, 256:512], t_ones[:],
                                t_gy[:, kc * SLAB:(kc + 1) * SLAB],
                                start=(kc == 0), stop=(kc == DCH - 1))
                        sT.inc(inst, 1)

                    @blk.gpsimd
                    def _(gpsimd):
                        if pv is not None:
                            gpsimd.wait_ge(sV.h, ev[("v_fres", DCH - 1)])
                            dst = out_d.rearrange("(kc p) t -> p kc t", p=128)[
                                :, :, (s - 1) * SLAB:s * SLAB]
                            dma(gpsimd, sO, dst, t_x2[:])
                        if DEBUG and s == 1:
                            gpsimd.wait_ge(sC.h, ev[("c_sig", DCH - 1)])
                            dma(gpsimd, qG, dbg["d_g"][:], t_gy[:])
                            gpsimd.wait_ge(sV.h, ev["v_hs"])
                            dma(gpsimd, qG, dbg["d_hs"][:], t_hs[:])
                            gpsimd.wait_ge(sV.h, ev[("v_x2", DCH - 1)])
                            dma(gpsimd, qG, dbg["d_x2"][:], t_x2[:])

                    assert sT.v == ev["t_end"], (s, sT.v, ev["t_end"])
                    assert sV.v == ev["v_end"], (s, sV.v, ev["v_end"])
                    assert sC.v == ev["c_end"], (s, sC.v, ev["c_end"])

                    nxt = dict(ev)
                    nxt["t_bcast_prev_cnt"] = ev.get("t_bcast", sT.v)
                    nxt["prev_x2"] = (pv or {}).get("x2_cnt_for_ring", None)
                    nxt["x2_cnt_for_ring"] = ev.get(("v_x2", DCH - 1))
                    prev = nxt

            with nc.Block() as blk:
                @blk.gpsimd
                def _(gpsimd):
                    gpsimd.wait_ge(sO.h, sO.v)

    return nc


_NC_CACHE = None


def kernel(**inputs):
    global _NC_CACHE
    x = np.asarray(inputs["L0_attn_x"], dtype=np.float32)
    w_in = np.ascontiguousarray(inputs["L0_attn_W_in"], dtype=np.float32)
    b_in = np.asarray(inputs["L0_attn_b_in"], dtype=np.float32)
    w_gate = np.ascontiguousarray(inputs["L0_attn_W_gate"], dtype=np.float32)
    b_gate = np.asarray(inputs["L0_attn_b_gate"], dtype=np.float32)
    w_out = np.ascontiguousarray(inputs["L0_attn_W_out"], dtype=np.float32)
    b_out = np.asarray(inputs["L0_attn_b_out"], dtype=np.float32)
    a_m = np.asarray(inputs["L0_attn_A"], dtype=np.float32)
    bm_m = np.asarray(inputs["L0_attn_Bm"], dtype=np.float32)
    c_m = np.asarray(inputs["L0_attn_C"], dtype=np.float32)
    d_m = np.asarray(inputs["L0_attn_D"], dtype=np.float32)
    tern = np.asarray(inputs["L0_ffn_ternary_weight"], dtype=np.int32)
    b_ffn = np.asarray(inputs["L0_ffn_bias"], dtype=np.float32)
    act_scale = np.asarray(inputs["L0_ffn_act_scale"], dtype=np.float32)

    if _NC_CACHE is None:
        _NC_CACHE = build_nc()
    nc = _NC_CACHE

    shared = {
        "w_in": w_in,
        "w_gate": w_gate,
        "w_out": w_out,
        "ct": np.ascontiguousarray(c_m.T),
        "dtw": np.ascontiguousarray(d_m.T),
        "a_n": np.ascontiguousarray(a_m),
        "a_t": np.ascontiguousarray(a_m.T),
        "bm": np.ascontiguousarray(bm_m),
        "bm_t": np.ascontiguousarray(bm_m.T),
        "id128": np.eye(128, dtype=np.float32),
        "ternt": np.ascontiguousarray(tern.T),
        "b_in": np.ascontiguousarray(b_in.reshape(SCH, 128).T),
        "b_gate": np.ascontiguousarray(b_gate.reshape(DCH, 128).T),
        "b_out": np.ascontiguousarray(b_out.reshape(DCH, 128).T),
        "b_ffn": np.ascontiguousarray(b_ffn.reshape(DCH, 128).T),
        "scal": act_scale.reshape(1, 1),
    }
    zeros_f = np.zeros((D, TLOC), np.float32)
    zeros_t = np.zeros((TLOC, D), np.float32)
    in_maps = []
    for core in range(8):
        b = core // 2
        half = core % 2
        xs = x[b, half * TLOC:(half + 1) * TLOC, :]
        m = dict(shared)
        m["xf"] = np.ascontiguousarray(xs.T)
        m["xt"] = np.ascontiguousarray(xs)
        if half == 1:
            ps = x[b, 0:TLOC, :]
            m["pf"] = np.ascontiguousarray(ps.T)
            m["pt"] = np.ascontiguousarray(ps)
            m["pmask"] = np.ones((128, 1), np.float32)
        else:
            m["pf"] = zeros_f
            m["pt"] = zeros_t
            m["pmask"] = np.zeros((128, 1), np.float32)
        in_maps.append(m)

    res = run_bass_kernel_spmd(nc, in_maps, core_ids=list(range(8)))
    out = np.empty((B, T, D), np.float32)
    for core in range(8):
        b = core // 2
        half = core % 2
        out[b, half * TLOC:(half + 1) * TLOC, :] = res.results[core]["out"].T
    if DEBUG:
        kernel.last_results = res
    return out


# version 13
# speedup vs baseline: 1.0396x; 1.0140x over previous
from contextlib import ExitStack

import numpy as np
import concourse.bass as bass
import concourse.mybir as mybir
from concourse.bass_utils import run_bass_kernel_spmd

F32 = mybir.dt.float32
BF16 = mybir.dt.bfloat16
I32 = mybir.dt.int32
AL = mybir.AluOpType
ACT = mybir.ActivationFunctionType
AXX = mybir.AxisListType.X

B, T, D, DS = 4, 4096, 1024, 256
TLOC = 2048
DCH = D // 128
SCH = DS // 128
C1 = 8
NCH1 = TLOC // C1
C2 = 16
NCH2 = NCH1 // C2
SLAB = 256
NSLAB = TLOC // SLAB
NGRP = TLOC // 128
EPS = 1e-5
DEBUG = False


class Sem:
    def __init__(self, handle):
        self.h = handle
        self.v = 0

    def inc(self, inst, n=1):
        inst.then_inc(self.h, n)
        self.v += n
        return inst


def build_nc():
    nc = bass.Bass("TRN2", target_bir_lowering=False,
                   detect_race_conditions=False)

    def din(name, shape, dt=F32):
        return nc.dram_tensor(name, list(shape), dt, kind="ExternalInput").ap()

    xf = din("xf", (D, TLOC))
    xt = din("xt", (TLOC, D))
    pf = din("pf", (D, TLOC))
    pt = din("pt", (TLOC, D))
    pmask = din("pmask", (128, 1))
    w_in = din("w_in", (D, DS))
    w_gate = din("w_gate", (D, D))
    w_out = din("w_out", (D, D))
    ct_d = din("ct", (DS, D))
    dt_d = din("dtw", (DS, D))
    a_n = din("a_n", (DS, DS))
    a_t = din("a_t", (DS, DS))
    bm_d = din("bm", (DS, DS))
    bm_t = din("bm_t", (DS, DS))
    id128 = din("id128", (128, 128))
    ternt = din("ternt", (D, D), I32)
    b_in_d = din("b_in", (128, SCH))
    b_gate_d = din("b_gate", (128, DCH))
    b_out_d = din("b_out", (128, DCH))
    b_ffn_d = din("b_ffn", (128, DCH))
    scal_d = din("scal", (1, 1))

    out_d = nc.dram_tensor("out", [D, TLOC], F32, kind="ExternalOutput").ap()
    dbg = {}
    if DEBUG:
        for name, shape in [("d_u", (128, SCH * TLOC)), ("d_up", (128, SCH * TLOC)),
                            ("d_e1", (128, SCH * NCH1)), ("d_i2", (128, SCH * NCH1)),
                            ("d_ball", (128, SCH * NCH1)),
                            ("d_m", (128, 2 * NGRP)), ("d_r", (128, 2 * NGRP)),
                            ("d_hst", (128, SCH)), ("d_e2", (128, SCH * NCH2)),
                            ("d_b2", (128, SCH * (NCH2 + 1))),
                            ("d_hs", (128, SCH * SLAB)), ("d_g", (128, DCH * SLAB)),
                            ("d_x2", (128, DCH * SLAB)), ("d_p8", (128, 512)),
                            ("d_q3", (128, 512)), ("d_r16", (128, 512)),
                            ("d_xq", (128, DCH * SLAB)),
                            ("d_st0", (128, 4 * NGRP)), ("d_st1", (128, 4 * NGRP)),
                            ("d_rc0", (128, 2 * NGRP)), ("d_rc1", (128, 2 * NGRP)),
                            ("d_sd0", (128, 2 * NGRP)), ("d_sd1", (128, 2 * NGRP))]:
            dbg[name] = nc.dram_tensor(name, list(shape), F32,
                                       kind="ExternalOutput").ap()

    def m_lhsT(tile, kc, mc):
        return tile[:, 256 * kc + 128 * mc: 256 * kc + 128 * mc + 128]

    def m_rhs(tile, kc):
        return tile[:, 256 * kc: 256 * kc + 256]

    def w_lhsT(tile, dout, kc, mc):
        return tile[:, dout * kc + 128 * mc: dout * kc + 128 * mc + 128]

    _es = ExitStack()
    with _es:
        t_wghi = _es.enter_context(nc.sbuf_tensor("t_wghi", [128, DCH * D], BF16))
        t_wglo = _es.enter_context(nc.sbuf_tensor("t_wglo", [128, DCH * D], BF16))
        t_wffn = _es.enter_context(nc.sbuf_tensor("t_wffn", [128, DCH * D], BF16))
        t_onesbf = _es.enter_context(nc.sbuf_tensor("t_onesbf", [128, 1], BF16))
        t_u = _es.enter_context(nc.sbuf_tensor("t_u", [128, SCH * TLOC], F32))
        t_bm = _es.enter_context(nc.sbuf_tensor("t_bm", [128, 512], F32))
        t_id128 = _es.enter_context(nc.sbuf_tensor("t_id128", [128, 128], F32))
        t_p = _es.enter_context(nc.sbuf_tensor("t_p", [128, 8 * 512], F32))
        t_q = _es.enter_context(nc.sbuf_tensor("t_q", [128, 7 * 512], F32))
        t_ball = _es.enter_context(nc.sbuf_tensor("t_ball", [128, SCH * NCH1], F32))
        t_hst = _es.enter_context(nc.sbuf_tensor("t_hst", [128, SCH], F32))
        t_mcols = _es.enter_context(nc.sbuf_tensor("t_mcols", [128, 2 * NGRP], F32))
        t_rcols = _es.enter_context(nc.sbuf_tensor("t_rcols", [128, 2 * NGRP], F32))
        t_scol = _es.enter_context(nc.sbuf_tensor("t_scol", [128, 1], F32))
        t_rsinv = _es.enter_context(nc.sbuf_tensor("t_rsinv", [128, 1], F32))
        t_pmask = _es.enter_context(nc.sbuf_tensor("t_pmask", [128, 1], F32))
        t_cwin = _es.enter_context(nc.sbuf_tensor("t_cwin", [128, SCH], F32))
        t_cwg = _es.enter_context(nc.sbuf_tensor("t_cwg", [128, DCH], F32))
        t_bin = _es.enter_context(nc.sbuf_tensor("t_bin", [128, SCH], F32))
        t_bgate = _es.enter_context(nc.sbuf_tensor("t_bgate", [128, DCH], F32))
        t_bout = _es.enter_context(nc.sbuf_tensor("t_bout", [128, DCH], F32))
        t_bffn = _es.enter_context(nc.sbuf_tensor("t_bffn", [128, DCH], F32))
        t_ones = _es.enter_context(nc.sbuf_tensor("t_ones", [128, 128], F32))
        t_onescol = _es.enter_context(nc.sbuf_tensor("t_onescol", [128, 1], F32))
        t_xf0 = _es.enter_context(nc.sbuf_tensor("t_xf0", [128, DCH * SLAB], F32))
        t_xf1 = _es.enter_context(nc.sbuf_tensor("t_xf1", [128, DCH * SLAB], F32))
        t_mbc0 = _es.enter_context(nc.sbuf_tensor("t_mbc0", [128, SLAB], F32))
        t_rbc0 = _es.enter_context(nc.sbuf_tensor("t_rbc0", [128, SLAB], F32))
        t_mbc1 = _es.enter_context(nc.sbuf_tensor("t_mbc1", [128, SLAB], F32))
        t_rbc1 = _es.enter_context(nc.sbuf_tensor("t_rbc1", [128, SLAB], F32))
        t_bcc = _es.enter_context(nc.sbuf_tensor("t_bcc", [128, 512], F32))
        t_rs2 = _es.enter_context(nc.sbuf_tensor("t_rs2", [128, 256], F32))
        t_mrs2 = _es.enter_context(nc.sbuf_tensor("t_mrs2", [128, 256], F32))
        t_qi = _es.enter_context(nc.sbuf_tensor("t_qi", [128, DCH * SLAB], mybir.dt.int8))
        t_fa = _es.enter_context(nc.sbuf_tensor("t_fa", [128, SLAB], F32))
        t_fb = _es.enter_context(nc.sbuf_tensor("t_fb", [128, SLAB], F32))
        pA = _es.enter_context(nc.psum_tensor("pA", [128, 512], F32))
        pB = _es.enter_context(nc.psum_tensor("pB", [128, 512], F32))
        pHS = _es.enter_context(nc.psum_tensor("pHS", [128, 512], F32))
        pST = _es.enter_context(nc.psum_tensor("pST", [128, 512], F32))
        pBC = _es.enter_context(nc.psum_tensor("pBC", [128, 512], F32))
        pE = _es.enter_context(nc.psum_tensor("pE", [128, 512], F32))
        pEb = _es.enter_context(nc.psum_tensor("pEb", [128, 512], F32))
        pSP = _es.enter_context(nc.psum_tensor("pSP", [128, 512], F32))
        _qS = _es.enter_context(nc.semaphore("qS"))
        _qG = _es.enter_context(nc.semaphore("qG"))
        _sT = _es.enter_context(nc.semaphore("sT"))
        _sV = _es.enter_context(nc.semaphore("sV"))
        _sC = _es.enter_context(nc.semaphore("sC"))
        _sO = _es.enter_context(nc.semaphore("sO"))
        _sD = _es.enter_context(nc.semaphore("sD"))
        qS, qG = Sem(_qS), Sem(_qG)
        sT, sV, sC, sO = Sem(_sT), Sem(_sV), Sem(_sC), Sem(_sO)
        sD = Sem(_sD)

        def vdrain(vector, inst):
            sD.inc(inst, 1)
            vector.wait_ge(sD.h, sD.v)
        xf_ring = [t_xf0, t_xf1]
        mbc_ring = [(t_mbc0, t_rbc0), (t_mbc1, t_rbc1)]

        def P(l):
            return t_p[:, (l - 1) * 512: l * 512]

        def Q(l):
            return t_bm if l == 0 else t_q[:, (l - 1) * 512: l * 512]

        def dma(eng, q, out, in_):
            q.inc(eng.dma_start(out=out, in_=in_), 16)

        with nc.sbuf_tensor("t_uprior", [128, SCH * TLOC], F32) as t_uprior:
            with (nc.sbuf_tensor("t_win", [128, DCH * DS], F32) as t_win,
                  nc.sbuf_tensor("t_tok0", [128, D], F32) as t_tok0,
                  nc.sbuf_tensor("t_tok1", [128, D], F32) as t_tok1,
                  nc.sbuf_tensor("t_tok2", [128, D], F32) as t_tok2,
                  nc.sbuf_tensor("t_tok3", [128, D], F32) as t_tok3,
                  nc.sbuf_tensor("t_toksq", [128, D], F32) as t_toksq,
                  nc.sbuf_tensor("t_st", [128, 4 * NGRP], F32) as t_st,
                  nc.sbuf_tensor("t_sa", [128, 256], F32) as t_sa,
                  nc.sbuf_tensor("t_sb", [128, 256], F32) as t_sb):
                tok_ring = [t_tok0, t_tok1, t_tok2, t_tok3]
                with nc.sbuf_tensor("t_tern", [128, 4 * D], I32) as t_tern, \
                        nc.sbuf_tensor("t_wgtmp", [128, 2 * D], F32) as t_wgtmp, \
                        nc.Block() as blk:
                    S0, G0, T0, V0, C0 = qS.v, qG.v, sT.v, sV.v, sC.v

                    @blk.sync
                    def _(sync):
                        for kc in range(DCH):
                            dma(sync, qS, t_win[:, kc * DS:(kc + 1) * DS],
                                w_in[kc * 128:(kc + 1) * 128, :])
                        dma(sync, qS, t_bm[:, 0:256], bm_d[0:128, :])
                        dma(sync, qS, t_bm[:, 256:512], bm_d[128:256, :])
                        dma(sync, qS, t_id128[:], id128[:])
                        dma(sync, qS, t_bin[:], b_in_d[:])
                        dma(sync, qS, t_bgate[:], b_gate_d[:])
                        dma(sync, qS, t_bout[:], b_out_d[:])
                        dma(sync, qS, t_bffn[:], b_ffn_d[:])
                        dma(sync, qS, t_pmask[:], pmask[:])
                        dma(sync, qS, t_scol[0:1, 0:1], scal_d[:])
                        for kc in range(DCH):
                            if kc >= 2:
                                sync.wait_ge(sV.h, V0 + 1 + kc)
                            dma(sync, qS,
                                t_wgtmp[:, (kc % 2) * D:(kc % 2 + 1) * D],
                                w_gate[kc * 128:(kc + 1) * 128, :])

                    @blk.gpsimd
                    def _(gpsimd):
                        for kc in range(4):
                            dma(gpsimd, qG, t_tern[:, kc * D:(kc + 1) * D],
                                ternt[kc * 128:(kc + 1) * 128, :])
                        gpsimd.wait_ge(sV.h, V0 + 14)
                        for kc in range(4):
                            dma(gpsimd, qG, t_tern[:, kc * D:(kc + 1) * D],
                                ternt[(4 + kc) * 128:(4 + kc + 1) * 128, :])

                    @blk.vector
                    def _(vector):
                        vector.memset(t_ones[:], 1.0 / 1024.0)
                        sV.inc(vector.memset(t_onescol[:], 1.0), 1)
                        vector.memset(t_onesbf[:], 1.0)
                        sV.inc(vector.memset(t_bcc[0:1, 0:128], 1.0), 1)
                        for kc in range(DCH):
                            vector.wait_ge(qS.h, S0 + 16 * (18 + kc))
                            wsrc = t_wgtmp[:, (kc % 2) * D:(kc % 2 + 1) * D]
                            hi = t_wghi[:, kc * D:(kc + 1) * D]
                            inst = vector.tensor_copy(hi, wsrc)
                            vdrain(vector, inst)
                            sV.inc(vector.tensor_tensor(
                                t_wglo[:, kc * D:(kc + 1) * D], wsrc, hi,
                                AL.subtract), 1)
                        vector.wait_ge(qG.h, G0 + 64)
                        for kc in range(4):
                            sV.inc(vector.tensor_scalar_add(
                                t_wffn[:, kc * D:(kc + 1) * D],
                                t_tern[:, kc * D:(kc + 1) * D], -1.0), 1)
                        vector.wait_ge(qG.h, G0 + 128)
                        for kc in range(4):
                            sV.inc(vector.tensor_scalar_add(
                                t_wffn[:, (4 + kc) * D:(5 + kc) * D],
                                t_tern[:, kc * D:(kc + 1) * D], -1.0), 1)
                        vector.wait_ge(sT.h, T0 + 2)
                        inst = vector.tensor_scalar(
                            t_scol[:, 0:1], pSP[:, 2:3], EPS, None, AL.max)
                        sV.inc(inst, 1)
                        vector.wait_ge(sV.h, sV.v)
                        sV.inc(vector.reciprocal(
                            t_rsinv[:, 0:1], t_scol[:, 0:1]), 1)
                        vector.wait_ge(sT.h, T0 + 2)
                        sV.inc(vector.tensor_copy(
                            t_cwin[:, :], pSP[:, 4:4 + SCH]), 1)

                    @blk.tensor
                    def _(tensor):
                        tensor.wait_ge(qS.h, S0 + 16 * 17)
                        tensor.wait_ge(sV.h, V0 + 2)
                        sT.inc(tensor.matmul(pSP[:, 2:3], t_bcc[0:1, 0:128],
                                             t_scol[0:1, 0:1],
                                             start=True, stop=True), 1)
                        for mc in range(SCH):
                            for kc in range(DCH):
                                inst = tensor.matmul(
                                    pSP[:, 4 + mc:5 + mc],
                                    w_lhsT(t_win, DS, kc, mc),
                                    t_onescol[:, 0:1],
                                    start=(kc == 0), stop=(kc == DCH - 1))
                        sT.inc(inst, 1)

                wg_split_done = V0 + 10

                for src in range(2):
                    xtok = pt if src == 0 else xt
                    xfeat = pf if src == 0 else xf
                    u_t = t_uprior if src == 0 else t_u
                    cbase = NGRP if src == 0 else 0

                    with nc.Block() as blk:
                        G0, S0st, C0, V0 = qG.v, qS.v, sC.v, sV.v

                        def _tok_wait(g):
                            return (max(C0, C0 + g - 3),
                                    max(V0, V0 + g - 3))

                        @blk.gpsimd
                        def _(gpsimd):
                            for g in range(0, NGRP, 2):
                                wc, wv = _tok_wait(g)
                                gpsimd.wait_ge(sC.h, wc)
                                gpsimd.wait_ge(sV.h, wv)
                                dma(gpsimd, qG, tok_ring[g % 4][:],
                                    xtok[g * 128:(g + 1) * 128, :])

                        @blk.sync
                        def _(sync):
                            for g in range(1, NGRP, 2):
                                wc, wv = _tok_wait(g)
                                sync.wait_ge(sC.h, wc)
                                sync.wait_ge(sV.h, wv)
                                dma(sync, qS, tok_ring[g % 4][:],
                                    xtok[g * 128:(g + 1) * 128, :])

                        def _tok_dma_wait(eng, g):
                            if g % 2 == 0:
                                eng.wait_ge(qG.h, G0 + 16 * (g // 2 + 1))
                            else:
                                eng.wait_ge(qS.h, S0st + 16 * ((g + 1) // 2))

                        @blk.scalar
                        def _(scalar):
                            scalar.wait_ge(sV.h, V0)
                            for g in range(NGRP):
                                _tok_dma_wait(scalar, g)
                                sC.inc(scalar.activation(
                                    t_toksq[:], tok_ring[g % 4][:], ACT.Square,
                                    accum_out=t_st[:, NGRP + g:NGRP + g + 1]), 1)

                        @blk.vector
                        def _(vector):
                            for g in range(NGRP):
                                _tok_dma_wait(vector, g)
                                sV.inc(vector.reduce_sum(
                                    t_st[:, g:g + 1], tok_ring[g % 4][:],
                                    AXX), 1)

                    with nc.Block() as blk:
                        V0, C0 = sV.v, sC.v
                        G0d = qG.v

                        if DEBUG:
                            @blk.gpsimd
                            def _(gpsimd):
                                gpsimd.wait_ge(sC.h, C0)
                                gpsimd.wait_ge(sV.h, V0)
                                dma(gpsimd, qG, dbg["d_st%d" % src][:, 0:2 * NGRP],
                                    t_st[:, 0:2 * NGRP])

                        @blk.vector
                        def _(vector):
                            vector.wait_ge(sC.h, C0)
                            vector.wait_ge(sT.h, sT.v)
                            vector.wait_ge(qS.h, qS.v)
                            vector.wait_ge(qG.h, qG.v)
                            if DEBUG:
                                vector.wait_ge(qG.h, G0d + 16)
                            mC = t_mcols[:, cbase:cbase + NGRP]
                            tA = t_st[:, 2 * NGRP:3 * NGRP]
                            tB = t_st[:, 3 * NGRP:4 * NGRP]
                            sV.inc(vector.tensor_scalar_mul(
                                mC, t_st[:, 0:NGRP], 1.0 / D), 1)
                            sV.inc(vector.tensor_scalar_mul(
                                tA, t_st[:, NGRP:2 * NGRP], 1.0 / D), 1)
                            vector.wait_ge(sV.h, sV.v)
                            sV.inc(vector.tensor_mul(tB, mC, mC), 1)
                            vector.wait_ge(sV.h, sV.v)
                            sV.inc(vector.scalar_tensor_tensor(
                                tB, tA, EPS, tB, AL.add, AL.subtract), 1)

                        @blk.scalar
                        def _(scalar):
                            scalar.wait_ge(sV.h, V0 + 4)
                            sC.inc(scalar.activation(
                                t_st[:, 0:NGRP], t_st[:, 3 * NGRP:4 * NGRP],
                                ACT.Sqrt), 1)

                    with nc.Block() as blk:
                        C0 = sC.v
                        V0p2 = sV.v
                        G0d2 = qG.v

                        if DEBUG:
                            @blk.gpsimd
                            def _(gpsimd):
                                gpsimd.wait_ge(sC.h, C0)
                                dma(gpsimd, qG, dbg["d_sd%d" % src][:, 0:NGRP],
                                    t_st[:, 0:NGRP])
                                dma(gpsimd, qG, dbg["d_sd%d" % src][:, NGRP:2 * NGRP],
                                    t_st[:, 3 * NGRP:4 * NGRP])

                        @blk.vector
                        def _(vector):
                            vector.wait_ge(sC.h, C0)
                            if DEBUG:
                                vector.wait_ge(qG.h, G0d2 + 32)
                            rC = t_rcols[:, cbase:cbase + NGRP]
                            sd = t_st[:, 0:NGRP]
                            ve = t_st[:, 3 * NGRP:4 * NGRP]
                            w_ = t_st[:, NGRP:2 * NGRP]
                            sV.inc(vector.reciprocal(rC, sd), 1)
                            vector.wait_ge(sV.h, sV.v)
                            sV.inc(vector.tensor_mul(w_, rC, rC), 1)
                            vector.wait_ge(sV.h, sV.v)
                            sV.inc(vector.scalar_tensor_tensor(
                                w_, ve, -0.5, w_, AL.mult, AL.mult), 1)
                            vector.wait_ge(sV.h, sV.v)
                            sV.inc(vector.scalar_tensor_tensor(
                                rC, w_, 1.5, rC, AL.add, AL.mult), 1)
                            vector.wait_ge(sV.h, sV.v)
                            sV.inc(vector.tensor_scalar_mul(rC, rC, -1.0), 1)

                        if DEBUG:
                            @blk.gpsimd
                            def _(gpsimd):
                                gpsimd.wait_ge(sV.h, V0p2 + 5)
                                dma(gpsimd, qG,
                                    dbg["d_rc%d" % src][:, cbase:cbase + NGRP],
                                    t_rcols[:, cbase:cbase + NGRP])

                    for s in range(NSLAB):
                        with nc.Block() as blk:
                            S0, T0, V0, C0 = qS.v, sT.v, sV.v, sC.v

                            @blk.sync
                            def _(sync):
                                sync.wait_ge(sT.h, max(0, T0 - 6))
                                src_ap = xfeat.rearrange(
                                    "(kc p) t -> p kc t", p=128)[
                                    :, :, s * SLAB:(s + 1) * SLAB]
                                dma(sync, qS, xf_ring[s % 2][:], src_ap)

                            @blk.vector
                            def _(vector):
                                vector.wait_ge(sT.h, max(0, T0 - 2))
                                for gg in range(2):
                                    gabs = cbase + 2 * s + gg
                                    vector.tensor_copy(
                                        t_bcc[:, gg * 128:(gg + 1) * 128],
                                        t_mcols[:, gabs:gabs + 1].broadcast_to(
                                            [128, 128]))
                                    inst = vector.tensor_copy(
                                        t_bcc[:, 256 + gg * 128:
                                              256 + (gg + 1) * 128],
                                        t_rcols[:, gabs:gabs + 1].broadcast_to(
                                            [128, 128]))
                                sV.inc(inst, 1)
                                vector.wait_ge(sT.h, T0 + 1)
                                mb, rb = mbc_ring[s % 2]
                                vector.tensor_copy(mb[:], pBC[:, 0:256])
                                sV.inc(vector.tensor_copy(
                                    rb[:], pBC[:, 256:512]), 1)
                                for mc in range(SCH):
                                    pb = pA if mc == 0 else pB
                                    vector.wait_ge(sT.h, T0 + 2 + mc)
                                    sc = t_sa if mc == 0 else t_sb
                                    vector.wait_ge(sC.h, max(0, C0 - 1 + mc))
                                    vector.scalar_tensor_tensor(
                                        sc[:], mb[:], t_cwin[:, mc:mc + 1],
                                        pb[:, 0:SLAB], AL.mult, AL.subtract)
                                    sV.inc(vector.tensor_mul(
                                        sc[:], sc[:], rb[:]), 1)

                            @blk.tensor
                            def _(tensor):
                                tensor.wait_ge(sV.h, V0 + 1)
                                for gg in range(2):
                                    tensor.matmul(
                                        pBC[:, gg * 128:(gg + 1) * 128],
                                        t_bcc[:, gg * 128:(gg + 1) * 128],
                                        t_id128[:], start=True, stop=True)
                                    inst = tensor.matmul(
                                        pBC[:, 256 + gg * 128:
                                            256 + (gg + 1) * 128],
                                        t_bcc[:, 256 + gg * 128:
                                              256 + (gg + 1) * 128],
                                        t_id128[:], start=True, stop=True)
                                sT.inc(inst, 1)
                                tensor.wait_ge(qS.h, S0 + 16)
                                tensor.wait_ge(sV.h, V0 + 2)
                                for mc in range(SCH):
                                    pb = pA if mc == 0 else pB
                                    if mc >= 1:
                                        tensor.wait_ge(sV.h, V0 + 2 + mc)
                                    for kc in range(DCH):
                                        inst = tensor.matmul(
                                            pb[:, 0:SLAB],
                                            w_lhsT(t_win, DS, kc, mc),
                                            xf_ring[s % 2][
                                                :, kc * SLAB:(kc + 1) * SLAB],
                                            start=(kc == 0),
                                            stop=(kc == DCH - 1))
                                    sT.inc(inst, 1)

                            @blk.scalar
                            def _(scalar):
                                for mc in range(SCH):
                                    scalar.wait_ge(sV.h, V0 + 3 + mc)
                                    sc = t_sa if mc == 0 else t_sb
                                    sC.inc(scalar.activation(
                                        u_t[:, TLOC * mc + s * SLAB:
                                            TLOC * mc + (s + 1) * SLAB],
                                        sc[:], ACT.Identity,
                                        bias=t_bin[:, mc:mc + 1]), 1)

            with (nc.sbuf_tensor("t_r", [128, 15 * 512], F32) as t_r,
                  nc.sbuf_tensor("t_rt0", [128, 512], F32) as t_rt0,
                  nc.sbuf_tensor("t_rt1", [128, 512], F32) as t_rt1,
                  nc.sbuf_tensor("t_at", [128, 512], F32) as t_at,
                  nc.sbuf_tensor("t_pt8", [128, 512], F32) as t_pt8,
                  nc.sbuf_tensor("t_bmt", [128, 512], F32) as t_bmt,
                  nc.sbuf_tensor("t_e1", [128, SCH * NCH1], F32) as t_e1,
                  nc.sbuf_tensor("t_i2", [128, SCH * NCH1], F32) as t_i2,
                  nc.sbuf_tensor("t_e2", [128, SCH * NCH2], F32) as t_e2,
                  nc.sbuf_tensor("t_b2", [128, SCH * (NCH2 + 1)], F32) as t_b2):

                def R(i):
                    return P(8) if i == 1 else t_r[:, (i - 2) * 512:(i - 1) * 512]

                jobs = []
                _prod = {}

                def J(dst, dk, lh, lk, rh, rk):
                    dep = max(_prod.get(lk, -1), _prod.get(rk, -1),
                              len(jobs) - 2)
                    jobs.append((dst, lh, rh, dep))
                    _prod[dk] = len(jobs) - 1
                for l in range(1, 8):
                    J(P(l + 1), f"P{l+1}", t_at, None, P(l), f"P{l}")
                    J(Q(l), f"Q{l}", t_bmt, None, P(l), f"P{l}")
                J(t_pt8, "pt8", P(7), "P7", t_at, None)
                J(R(2), "R2", t_pt8, "pt8", R(1), "P8")
                J(t_rt0, "rt0", R(1), "P8", t_pt8, "pt8")
                J(R(3), "R3", t_rt0, "rt0", R(1), "P8")
                J(R(4), "R4", t_rt0, "rt0", R(2), "R2")
                J(t_rt1, "rt1", R(2), "R2", t_rt0, "rt0")
                J(R(5), "R5", t_rt1, "rt1", R(1), "P8")
                J(R(6), "R6", t_rt1, "rt1", R(2), "R2")
                J(R(7), "R7", t_rt1, "rt1", R(3), "R3")
                J(R(8), "R8", t_rt1, "rt1", R(4), "R4")
                J(t_rt0, "rt0", R(4), "R4", t_rt1, "rt1")
                for i in range(1, 9):
                    J(R(8 + i), f"R{8+i}", t_rt0, "rt0",
                      R(i), "P8" if i == 1 else f"R{i}")

                with nc.Block() as blk:
                    S0, T0, V0 = qS.v, sT.v, sV.v

                    @blk.sync
                    def _(sync):
                        dma(sync, qS, P(1)[:, 0:256], a_n[0:128, :])
                        dma(sync, qS, P(1)[:, 256:512], a_n[128:256, :])
                        dma(sync, qS, t_at[:, 0:256], a_t[0:128, :])
                        dma(sync, qS, t_at[:, 256:512], a_t[128:256, :])
                        dma(sync, qS, t_bmt[:, 0:256], bm_t[0:128, :])
                        dma(sync, qS, t_bmt[:, 256:512], bm_t[128:256, :])

                    @blk.tensor
                    def _(tensor):
                        tensor.wait_ge(qS.h, S0 + 96)
                        for idx, (dst, lh, rh, dep) in enumerate(jobs):
                            bank = pE if idx % 2 == 0 else pEb
                            if dep >= 0:
                                tensor.wait_ge(sV.h, V0 + dep + 1)
                            for mc in range(SCH):
                                for kc in range(SCH):
                                    inst = tensor.matmul(
                                        bank[:, mc * 256:(mc + 1) * 256],
                                        m_lhsT(lh, kc, mc), m_rhs(rh, kc),
                                        start=(kc == 0), stop=(kc == SCH - 1))
                            sT.inc(inst, 1)

                    @blk.vector
                    def _(vector):
                        for idx, (dst, lh, rh, dep) in enumerate(jobs):
                            bank = pE if idx % 2 == 0 else pEb
                            vector.wait_ge(sT.h, T0 + idx + 1)
                            sV.inc(vector.tensor_copy(dst[:], bank[:]), 1)

                with nc.Block() as blk:
                    T0c = sT.v

                    @blk.tensor
                    def _(tensor):
                        tensor.wait_ge(sV.h, wg_split_done)
                        for mc in range(DCH):
                            for kc in range(DCH):
                                tensor.matmul(
                                    pSP[:, 8 + mc:9 + mc],
                                    w_lhsT(t_wghi, D, kc, mc),
                                    t_onesbf[:, 0:1],
                                    start=(kc == 0), stop=False)
                            for kc in range(DCH):
                                inst = tensor.matmul(
                                    pSP[:, 8 + mc:9 + mc],
                                    w_lhsT(t_wglo, D, kc, mc),
                                    t_onesbf[:, 0:1],
                                    start=False, stop=(kc == DCH - 1))
                        sT.inc(inst, 1)

                    @blk.vector
                    def _(vector):
                        vector.wait_ge(sT.h, T0c + 1)
                        sV.inc(vector.tensor_copy(
                            t_cwg[:, :], pSP[:, 8:8 + DCH]), 1)

                for src in range(2):
                    u_t = t_uprior if src == 0 else t_u

                    with nc.Block() as blk:
                        T0, V0 = sT.v, sV.v

                        @blk.tensor
                        def _(tensor):
                            tensor.wait_ge(sV.h, V0)
                            tensor.wait_ge(sC.h, sC.v)
                            for nco in range(SCH):
                                for i in range(C1):
                                    for sc in range(SCH):
                                        uv = u_t[:, TLOC * sc:TLOC * (sc + 1)]
                                        uv = uv.rearrange(
                                            "p (K j c) -> p j K c",
                                            K=NCH2, j=C2, c=C1)[:, :, :, C1 - 1 - i]
                                        inst = tensor.matmul(
                                            pE[:, nco * 256:(nco + 1) * 256],
                                            m_lhsT(Q(i), sc, nco), uv,
                                            start=(nco == 0 and i == 0 and sc == 0),
                                            stop=(nco == SCH - 1 and i == C1 - 1
                                                  and sc == SCH - 1))
                            sT.inc(inst, 1)
                            tensor.wait_ge(sV.h, V0 + 2)
                            for j in range(2, C2):
                                for nco in range(SCH):
                                    for kc in range(SCH):
                                        n_el = (C2 - j) * NCH2
                                        ev = t_e1[:, kc * 256:kc * 256 + n_el]
                                        ov = pEb[:, nco * 256 + j * NCH2:
                                                 (nco + 1) * 256]
                                        inst = tensor.matmul(
                                            ov, m_lhsT(R(j - 1), kc, nco), ev,
                                            start=False, stop=(kc == SCH - 1),
                                            skip_group_check=True)
                            sT.inc(inst, 1)
                            tensor.wait_ge(sV.h, V0 + 4)
                            for nco in range(SCH):
                                for kc in range(SCH):
                                    iv = t_i2[:, kc * 256 + (C2 - 1) * NCH2:
                                              kc * 256 + 256]
                                    inst = tensor.matmul(
                                        pSP[:, nco * 256:nco * 256 + NCH2],
                                        m_lhsT(R(1), kc, nco), iv,
                                        start=(kc == 0), stop=(kc == SCH - 1))
                            sT.inc(inst, 1)
                            for K in range(NCH2):
                                tensor.wait_ge(sV.h, V0 + 6 + K)
                                for nco in range(SCH):
                                    for kc in range(SCH):
                                        inst = tensor.matmul(
                                            pSP[:, nco * 256 + 32:nco * 256 + 33],
                                            m_lhsT(R(16), kc, nco),
                                            t_b2[:, kc * (NCH2 + 1) + K:
                                                 kc * (NCH2 + 1) + K + 1],
                                            start=(kc == 0), stop=(kc == SCH - 1))
                                sT.inc(inst, 1)
                            if src == 1:
                                tensor.wait_ge(sV.h, V0 + 6 + NCH2)
                                for j in range(1, C2):
                                    for nco in range(SCH):
                                        for kc in range(SCH):
                                            ov = pE[:, nco * 256 + j * NCH2:
                                                    nco * 256 + (j + 1) * NCH2]
                                            inst = tensor.matmul(
                                                ov, m_lhsT(R(j), kc, nco),
                                                t_b2[:, kc * (NCH2 + 1):
                                                     kc * (NCH2 + 1) + NCH2],
                                                start=False, stop=(kc == SCH - 1),
                                                skip_group_check=True)
                                sT.inc(inst, 1)

                        @blk.vector
                        def _(vector):
                            vector.wait_ge(sT.h, T0 + 1)
                            sV.inc(vector.tensor_copy(t_e1[:], pE[:]), 1)
                            sV.inc(vector.memset(pEb[:], 0.0), 1)
                            vector.wait_ge(sT.h, T0 + 2)
                            vector.tensor_copy(t_i2[:], pEb[:])
                            for nco in range(SCH):
                                ov = t_i2[:, nco * 256 + NCH2:(nco + 1) * 256]
                                ev = t_e1[:, nco * 256:
                                          nco * 256 + (C2 - 1) * NCH2]
                                inst = vector.tensor_tensor(ov, ov, ev, AL.add)
                            sV.inc(inst, 1)
                            sV.inc(vector.memset(pE[:], 0.0), 1)
                            vector.wait_ge(sT.h, T0 + 3)
                            for nco in range(SCH):
                                ev = t_e1[:, nco * 256 + (C2 - 1) * NCH2:
                                          nco * 256 + 256]
                                inst = vector.tensor_tensor(
                                    t_e2[:, nco * NCH2:(nco + 1) * NCH2],
                                    pSP[:, nco * 256:nco * 256 + NCH2],
                                    ev, AL.add)
                            sV.inc(inst, 1)
                            if src == 0:
                                for sc in range(SCH):
                                    inst = vector.memset(
                                        t_b2[:, sc * (NCH2 + 1):
                                             sc * (NCH2 + 1) + 1], 0.0)
                            else:
                                for sc in range(SCH):
                                    inst = vector.tensor_copy(
                                        t_b2[:, sc * (NCH2 + 1):
                                             sc * (NCH2 + 1) + 1],
                                        t_hst[:, sc:sc + 1])
                            sV.inc(inst, 1)
                            for K in range(NCH2):
                                vector.wait_ge(sT.h, T0 + 4 + K)
                                for nco in range(SCH):
                                    inst = vector.tensor_tensor(
                                        t_b2[:, nco * (NCH2 + 1) + K + 1:
                                             nco * (NCH2 + 1) + K + 2],
                                        pSP[:, nco * 256 + 32:nco * 256 + 33],
                                        t_e2[:, nco * NCH2 + K:
                                             nco * NCH2 + K + 1], AL.add)
                                sV.inc(inst, 1)
                            if src == 0:
                                for sc in range(SCH):
                                    inst = vector.tensor_scalar_mul(
                                        t_hst[:, sc:sc + 1],
                                        t_b2[:, sc * (NCH2 + 1) + NCH2:
                                             sc * (NCH2 + 1) + NCH2 + 1],
                                        t_pmask[:, 0:1])
                                sV.inc(inst, 1)
                            else:
                                vector.wait_ge(sT.h, T0 + 4 + NCH2)
                                inst = vector.tensor_tensor(
                                    t_ball[:], pE[:], t_i2[:], AL.add)
                                for nco in range(SCH):
                                    bv = t_ball[:, nco * 256:nco * 256 + NCH2]
                                    inst = vector.tensor_tensor(
                                        bv, bv,
                                        t_b2[:, nco * (NCH2 + 1):
                                             nco * (NCH2 + 1) + NCH2], AL.add)
                                sV.inc(inst, 1)

                if DEBUG:
                    with nc.Block() as blk:
                        @blk.gpsimd
                        def _(gpsimd):
                            gpsimd.wait_ge(sV.h, sV.v)
                            gpsimd.wait_ge(sC.h, sC.v)
                            for name, tile in [("d_u", t_u), ("d_up", t_uprior),
                                               ("d_e1", t_e1), ("d_i2", t_i2),
                                               ("d_ball", t_ball),
                                               ("d_m", t_mcols), ("d_r", t_rcols),
                                               ("d_hst", t_hst), ("d_e2", t_e2),
                                               ("d_b2", t_b2), ("d_p8", P(8)),
                                               ("d_q3", Q(3)), ("d_r16", R(16))]:
                                dma(gpsimd, qG, dbg[name][:], tile[:])

        with (nc.sbuf_tensor("t_wohi", [128, DCH * D], BF16) as t_wohi,
              nc.sbuf_tensor("t_wolo", [128, DCH * D], BF16) as t_wolo,
              nc.sbuf_tensor("t_bf", [128, DCH * SLAB], BF16) as t_bf,
              nc.sbuf_tensor("t_lo", [128, DCH * SLAB], BF16) as t_lo,
              nc.sbuf_tensor("t_ct", [128, SCH * D], F32) as t_ct,
              nc.sbuf_tensor("t_dt", [128, SCH * D], F32) as t_dt,
              nc.sbuf_tensor("t_gy", [128, DCH * SLAB], F32) as t_gy,
              nc.sbuf_tensor("t_hs", [128, SCH * SLAB], F32) as t_hs,
              nc.sbuf_tensor("t_x2", [128, DCH * SLAB], F32) as t_x2,
              nc.sbuf_tensor("t_xq", [128, DCH * SLAB], BF16) as t_xq):

            with nc.Block() as blk:
                S0, V0 = qS.v, sV.v

                @blk.sync
                def _(sync):
                    for kc in range(SCH):
                        dma(sync, qS, t_ct[:, kc * D:(kc + 1) * D],
                            ct_d[kc * 128:(kc + 1) * 128, :])
                    for kc in range(SCH):
                        dma(sync, qS, t_dt[:, kc * D:(kc + 1) * D],
                            dt_d[kc * 128:(kc + 1) * 128, :])
                    for kc in range(DCH):
                        if kc >= 2:
                            sync.wait_ge(sV.h, V0 + kc - 1)
                        dma(sync, qS,
                            t_x2[:, (kc % 2) * D:(kc % 2 + 1) * D],
                            w_out[kc * 128:(kc + 1) * 128, :])

                @blk.vector
                def _(vector):
                    for kc in range(DCH):
                        vector.wait_ge(qS.h, S0 + 16 * (5 + kc))
                        wsrc = t_x2[:, (kc % 2) * D:(kc % 2 + 1) * D]
                        hi = t_wohi[:, kc * D:(kc + 1) * D]
                        inst = vector.tensor_copy(hi, wsrc)
                        vdrain(vector, inst)
                        sV.inc(vector.tensor_tensor(
                            t_wolo[:, kc * D:(kc + 1) * D], wsrc, hi,
                            AL.subtract), 1)

            with nc.Block() as blk:
                P0V, P0T = sV.v, sT.v

                @blk.vector
                def _(vector):
                    vector.wait_ge(sT.h, P0T)
                    for gg in range(2):
                        vector.tensor_copy(
                            t_bcc[:, gg * 128:(gg + 1) * 128],
                            t_mcols[:, gg:gg + 1].broadcast_to([128, 128]))
                        inst = vector.tensor_copy(
                            t_bcc[:, 256 + gg * 128:256 + (gg + 1) * 128],
                            t_rcols[:, gg:gg + 1].broadcast_to([128, 128]))
                    sV.inc(inst, 1)
                    vector.wait_ge(sT.h, P0T + 1)
                    mb, rb = mbc_ring[0]
                    vector.tensor_copy(mb[:], pBC[:, 0:256])
                    sV.inc(vector.tensor_copy(rb[:], pBC[:, 256:512]), 1)

                @blk.tensor
                def _(tensor):
                    tensor.wait_ge(sV.h, P0V + 1)
                    for gg in range(2):
                        tensor.matmul(
                            pBC[:, gg * 128:(gg + 1) * 128],
                            t_bcc[:, gg * 128:(gg + 1) * 128],
                            t_id128[:], start=True, stop=True)
                        inst = tensor.matmul(
                            pBC[:, 256 + gg * 128:256 + (gg + 1) * 128],
                            t_bcc[:, 256 + gg * 128:256 + (gg + 1) * 128],
                            t_id128[:], start=True, stop=True)
                    sT.inc(inst, 1)

            pre_bcast_cnt = sT.v
            pre_mbc_cnt = sV.v
            rot_banks = [pA, pB, pE, pEb, pSP]
            bank_hist = [("sV", sV.v)] * 5
            gbank = [5]
            prev = None
            for s in range(NSLAB + 1):
                with nc.Block() as blk:
                    S0, G0, T0, V0, C0, O0 = qS.v, qG.v, sT.v, sV.v, sC.v, sO.v
                    last = s == NSLAB
                    pv = prev
                    ev = {}

                    t = T0
                    if not last:
                        for mc in range(DCH):
                            t += 1
                            ev[("t_gate", mc)] = t
                        if s + 1 < NSLAB:
                            t += 1
                            ev["t_bcast"] = t
                    if pv is not None:
                        for mc in range(DCH):
                            t += 1
                            ev[("t_ffn", mc)] = t
                    if not last:
                        t += 1
                        ev["t_lag"] = t
                        for mc in range(DCH):
                            t += 1
                            ev[("t_cd", mc)] = t
                        for mc in range(DCH):
                            t += 1
                            ev[("t_wout", mc)] = t
                        t += 1
                        ev["t_stats"] = t
                    ev["t_end"] = t

                    v = V0
                    _gq = list(range(DCH)) if not last else []

                    def _pop(n, v):
                        for _ in range(n):
                            if _gq:
                                v += 1
                                ev[("v_gate", _gq.pop(0))] = v
                        return v
                    if not last:
                        v += 1
                        ev["v_xsplit"] = v
                    if not last and s + 1 < NSLAB:
                        v += 1
                        ev["v_bcc"] = v
                    v = _pop(2, v)
                    if pv is not None:
                        v += 1
                        ev["v_ln2a"] = v
                        v += 1
                        ev["v_ln2b"] = v
                        v += 1
                        ev["v_qgy"] = v
                        v += 1
                        ev["v_quant"] = v
                        v = _pop(DCH, v)
                    v = _pop(DCH, v)
                    if pv is not None:
                        for mc in range(DCH):
                            v += 1
                            ev[("v_fres", mc)] = v
                    if not last:
                        v += 1
                        ev["v_hs"] = v
                        for mc in range(DCH):
                            v += 1
                            ev[("v_y", mc)] = v
                        v += 1
                        ev["v_gysplit"] = v
                        for mc in range(DCH):
                            v += 1
                            ev[("v_x2", mc)] = v
                        if s + 1 < NSLAB:
                            v += 1
                            ev["v_mbc"] = v
                    ev["v_end"] = v

                    c = C0
                    if not last:
                        c += 1
                        ev["c_xcast"] = c
                    if pv is not None:
                        c += 1
                        ev["c_sqrt"] = c
                    if not last:
                        for mc in range(DCH):
                            c += 1
                            ev[("c_sig", mc)] = c
                    if pv is not None:
                        for mc in range(DCH):
                            c += 1
                            ev[("c_fevac", mc)] = c
                    if not last:
                        c += 1
                        ev["c_gycast"] = c
                        for mc in range(DCH):
                            c += 1
                            ev[("c_sq", mc)] = c
                    ev["c_end"] = c

                    banks = {}
                    bank_wait = {}
                    plan = []
                    if not last:
                        plan += [("gate", mc, ("sV", ev[("v_gate", mc)]))
                                 for mc in range(DCH)]
                    if pv is not None:
                        gi = len([p for p in plan])
                        plan = (plan[:DCH if not last else 0]
                                if False else plan)
                    plan2 = []
                    if not last:
                        plan2 += [("gate", mc, ("sV", ev[("v_gate", mc)]))
                                  for mc in range(DCH)]
                    if pv is not None:
                        plan2 += [("ffn", mc, ("sC", ev[("c_fevac", mc)]))
                                  for mc in range(DCH)]
                    if not last:
                        plan2 += [("cd", mc, ("sV", ev[("v_y", mc)]))
                                  for mc in range(DCH)]
                        plan2 += [("wout", mc, ("sV", ev[("v_x2", mc)]))
                                  for mc in range(DCH)]
                    for stage, mc, consumer in plan2:
                        i = gbank[0]
                        banks[(stage, mc)] = rot_banks[i % 5]
                        bank_wait[(stage, mc)] = (
                            bank_hist[i - 5] if i >= 5 else None)
                        bank_hist.append(consumer)
                        gbank[0] = i + 1

                    def bwait(eng, key):
                        w = bank_wait[key]
                        if w is not None:
                            nm, cnt = w
                            eng.wait_ge(sV.h if nm == "sV" else sC.h, cnt)

                    @blk.sync
                    def _(sync):
                        if not last:
                            if pv is not None and pv.get("prev_x2") is not None:
                                sync.wait_ge(sV.h, pv["prev_x2"])
                            src_ap = xf.rearrange(
                                "(kc p) t -> p kc t", p=128)[
                                :, :, s * SLAB:(s + 1) * SLAB]
                            dma(sync, qS, xf_ring[s % 2][:], src_ap)

                    @blk.vector
                    def _(vector):
                        if not last:
                            vector.wait_ge(sC.h, ev["c_xcast"])
                            sV.inc(vector.tensor_tensor(
                                t_lo[:], xf_ring[s % 2][:], t_bf[:],
                                AL.subtract), 1)
                        if not last and s + 1 < NSLAB:
                            vector.wait_ge(sT.h, pv["t_bcast_prev_cnt"]
                                           if s >= 1 else pre_bcast_cnt)
                            for gg in range(2):
                                gabs = 2 * (s + 1) + gg
                                vector.tensor_copy(
                                    t_bcc[:, gg * 128:(gg + 1) * 128],
                                    t_mcols[:, gabs:gabs + 1].broadcast_to(
                                        [128, 128]))
                                inst = vector.tensor_copy(
                                    t_bcc[:, 256 + gg * 128:256 + (gg + 1) * 128],
                                    t_rcols[:, gabs:gabs + 1].broadcast_to(
                                        [128, 128]))
                            sV.inc(inst, 1)

                        gate_q = list(range(DCH)) if not last else []
                        mb, rb = mbc_ring[s % 2]

                        def emit_gate_evac(mc):
                            pb = banks[("gate", mc)]
                            slot = t_hs[:, (mc % 2) * SLAB:(mc % 2 + 1) * SLAB]
                            vector.wait_ge(sT.h, ev[("t_gate", mc)])
                            if mc >= 2:
                                vector.wait_ge(sC.h, ev[("c_sig", mc - 2)])
                            vector.scalar_tensor_tensor(
                                slot, mb[:],
                                t_cwg[:, mc:mc + 1],
                                pb[:, 0:SLAB], AL.mult, AL.subtract)
                            sV.inc(vector.tensor_mul(
                                slot, slot,
                                rb[:]), 1)

                        def g1(n=1):
                            for _i in range(n):
                                if gate_q:
                                    emit_gate_evac(gate_q.pop(0))

                        g1(2)
                        if pv is not None:
                            vector.wait_ge(sT.h, pv["t_stats"])
                            vector.tensor_copy(t_fa[:], pST[:, 0:256])
                            inst = vector.tensor_copy(t_mrs2[:],
                                                      pST[:, 256:512])
                            vdrain(vector, inst)
                            inst = vector.tensor_mul(t_fb[:], t_fa[:], t_fa[:])
                            vdrain(vector, inst)
                            sV.inc(vector.scalar_tensor_tensor(
                                t_fb[:], t_mrs2[:], EPS, t_fb[:],
                                AL.add, AL.subtract), 1)
                            vector.wait_ge(sC.h, ev["c_sqrt"])
                            inst = vector.reciprocal(t_rs2[:], t_mrs2[:])
                            vdrain(vector, inst)
                            inst = vector.tensor_mul(t_mrs2[:], t_rs2[:],
                                                     t_rs2[:])
                            vdrain(vector, inst)
                            inst = vector.scalar_tensor_tensor(
                                t_mrs2[:], t_fb[:], -0.5, t_mrs2[:],
                                AL.mult, AL.mult)
                            vdrain(vector, inst)
                            inst = vector.scalar_tensor_tensor(
                                t_rs2[:], t_mrs2[:], 1.5, t_rs2[:],
                                AL.add, AL.mult)
                            vdrain(vector, inst)
                            inst = vector.tensor_scalar_mul(
                                t_rs2[:], t_rs2[:], t_rsinv[:, 0:1])
                            vdrain(vector, inst)
                            sV.inc(vector.tensor_tensor(
                                t_mrs2[:], t_fa[:], t_rs2[:], AL.mult), 1)
                            rsb = t_rs2[:, 0:SLAB].unsqueeze(1).broadcast_to(
                                [128, DCH, SLAB])
                            mrsb = t_mrs2[:, 0:SLAB].unsqueeze(1).broadcast_to(
                                [128, DCH, SLAB])
                            x2v = t_x2[:].rearrange("p (m c) -> p m c", c=SLAB)
                            gyv = t_gy[:].rearrange("p (m c) -> p m c", c=SLAB)
                            inst = vector.tensor_tensor(gyv, x2v, rsb, AL.mult)
                            vdrain(vector, inst)
                            inst = vector.tensor_tensor(gyv, gyv, mrsb,
                                                        AL.subtract)
                            vdrain(vector, inst)
                            inst = vector.tensor_scalar(t_gy[:], t_gy[:],
                                                        127.0, -127.0,
                                                        AL.min, AL.max)
                            vdrain(vector, inst)
                            inst = vector.tensor_copy(t_qi[:], t_gy[:])
                            vdrain(vector, inst)
                            inst = vector.tensor_tensor(
                                t_gy[:], t_gy[:], t_qi[:], AL.subtract)
                            vdrain(vector, inst)
                            inst = vector.scalar_tensor_tensor(
                                t_qi[:], t_gy[:], 0.5, t_qi[:],
                                AL.is_gt, AL.add)
                            vdrain(vector, inst)
                            sV.inc(vector.scalar_tensor_tensor(
                                t_qi[:], t_gy[:], -0.5, t_qi[:],
                                AL.is_lt, AL.subtract), 1)
                            vector.wait_ge(sV.h, sV.v)
                            sV.inc(vector.tensor_scalar_mul(
                                t_xq[:], t_qi[:], -1.0), 1)
                            g1(DCH)
                        g1(DCH)
                        if pv is not None:
                            for mc in range(DCH):
                                vector.wait_ge(sC.h, ev[("c_fevac", mc)])
                                fb = t_fa if mc % 2 == 0 else t_fb
                                sV.inc(vector.tensor_tensor(
                                    t_x2[:, mc * SLAB:(mc + 1) * SLAB],
                                    fb[:],
                                    t_x2[:, mc * SLAB:(mc + 1) * SLAB],
                                    AL.add), 1)
                        if last:
                            return
                        vector.wait_ge(sC.h, ev[("c_sig", DCH - 1)])
                        vector.wait_ge(sT.h, ev["t_lag"])
                        sV.inc(vector.tensor_copy(t_hs[:], pHS[:]), 1)
                        vector.wait_ge(sC.h, ev[("c_sig", DCH - 1)])
                        if DEBUG and s == 1:
                            vector.wait_ge(qG.h, G0 + 32)
                        for mc in range(DCH):
                            pb = banks[("cd", mc)]
                            vector.wait_ge(sT.h, ev[("t_cd", mc)])
                            sV.inc(vector.tensor_tensor(
                                t_gy[:, mc * SLAB:(mc + 1) * SLAB],
                                pb[:, 0:SLAB],
                                t_gy[:, mc * SLAB:(mc + 1) * SLAB],
                                AL.mult), 1)
                        vector.wait_ge(sC.h, ev["c_gycast"])
                        sV.inc(vector.tensor_tensor(
                            t_lo[:], t_gy[:], t_bf[:],
                            AL.subtract), 1)
                        if pv is not None:
                            vector.wait_ge(sO.h, O0 + 16)
                        for mc in range(DCH):
                            pb = banks[("wout", mc)]
                            vector.wait_ge(sT.h, ev[("t_wout", mc)])
                            sV.inc(vector.scalar_tensor_tensor(
                                t_x2[:, mc * SLAB:(mc + 1) * SLAB],
                                pb[:, 0:SLAB], t_bout[:, mc:mc + 1],
                                xf_ring[s % 2][:, mc * SLAB:(mc + 1) * SLAB],
                                AL.add, AL.add), 1)
                        if s + 1 < NSLAB:
                            vector.wait_ge(sT.h, ev["t_bcast"])
                            mb, rb = mbc_ring[(s + 1) % 2]
                            vector.tensor_copy(mb[:], pBC[:, 0:256])
                            sV.inc(vector.tensor_copy(rb[:], pBC[:, 256:512]), 1)

                    @blk.scalar
                    def _(scalar):
                        if not last:
                            if pv is not None:
                                scalar.wait_ge(sT.h, pv[("t_wout", DCH - 1)])
                            scalar.wait_ge(qS.h, S0 + 16)
                            sC.inc(scalar.activation(
                                t_bf[:], xf_ring[s % 2][:], ACT.Copy), 1)
                        if pv is not None:
                            scalar.wait_ge(sV.h, ev["v_ln2a"])
                            sC.inc(scalar.activation(t_mrs2[:], t_fb[:],
                                                     ACT.Sqrt), 1)
                        if not last:
                            if pv is not None:
                                scalar.wait_ge(sV.h, ev["v_qgy"])
                            for mc in range(DCH):
                                scalar.wait_ge(sV.h, ev[("v_gate", mc)])
                                sC.inc(scalar.activation(
                                    t_gy[:, mc * SLAB:(mc + 1) * SLAB],
                                    t_hs[:, (mc % 2) * SLAB:
                                         (mc % 2 + 1) * SLAB],
                                    ACT.Sigmoid, bias=t_bgate[:, mc:mc + 1]), 1)
                        if pv is not None:
                            for mc in range(DCH):
                                pb = banks[("ffn", mc)]
                                fb = t_fa if mc % 2 == 0 else t_fb
                                scalar.wait_ge(sT.h, ev[("t_ffn", mc)])
                                if mc >= 2:
                                    scalar.wait_ge(sV.h, ev[("v_fres", mc - 2)])
                                sC.inc(scalar.activation(
                                    fb[:],
                                    pb[:, 0:SLAB], ACT.Identity,
                                    bias=t_bffn[:, mc:mc + 1],
                                    scale=t_scol[:, 0:1]), 1)
                        if last:
                            return
                        scalar.wait_ge(sV.h, ev[("v_y", DCH - 1)])
                        sC.inc(scalar.activation(
                            t_bf[:], t_gy[:], ACT.Copy), 1)
                        scalar.wait_ge(sT.h, ev[("t_wout", DCH - 1)])
                        for mc in range(DCH):
                            scalar.wait_ge(sV.h, ev[("v_x2", mc)])
                            sC.inc(scalar.activation(
                                t_gy[:, mc * SLAB:(mc + 1) * SLAB],
                                t_x2[:, mc * SLAB:(mc + 1) * SLAB],
                                ACT.Square), 1)

                    @blk.tensor
                    def _(tensor):
                        if not last:
                            tensor.wait_ge(qS.h, S0 + 16)
                            tensor.wait_ge(sV.h, ev["v_xsplit"])
                            for mc in range(DCH):
                                pb = banks[("gate", mc)]
                                bwait(tensor, ("gate", mc))
                                for kc in range(DCH):
                                    tensor.matmul(
                                        pb[:, 0:SLAB],
                                        w_lhsT(t_wghi, D, kc, mc),
                                        t_bf[:, kc * SLAB:(kc + 1) * SLAB],
                                        start=(kc == 0), stop=False)
                                for kc in range(DCH):
                                    tensor.matmul(
                                        pb[:, 0:SLAB],
                                        w_lhsT(t_wglo, D, kc, mc),
                                        t_bf[:, kc * SLAB:(kc + 1) * SLAB],
                                        start=False, stop=False)
                                for kc in range(DCH):
                                    inst = tensor.matmul(
                                        pb[:, 0:SLAB],
                                        w_lhsT(t_wghi, D, kc, mc),
                                        t_lo[:, kc * SLAB:(kc + 1) * SLAB],
                                        start=False, stop=(kc == DCH - 1))
                                sT.inc(inst, 1)
                        if not last and s + 1 < NSLAB:
                            tensor.wait_ge(sV.h, ev["v_bcc"])
                            tensor.wait_ge(sV.h, pv["v_mbc"]
                                           if pv is not None else pre_mbc_cnt)
                            for gg in range(2):
                                tensor.matmul(
                                    pBC[:, gg * 128:(gg + 1) * 128],
                                    t_bcc[:, gg * 128:(gg + 1) * 128],
                                    t_id128[:], start=True, stop=True)
                                inst = tensor.matmul(
                                    pBC[:, 256 + gg * 128:256 + (gg + 1) * 128],
                                    t_bcc[:, 256 + gg * 128:
                                          256 + (gg + 1) * 128],
                                    t_id128[:], start=True, stop=True)
                            sT.inc(inst, 1)
                        if pv is not None:
                            tensor.wait_ge(sV.h, ev["v_quant"])
                            for mc in range(DCH):
                                pb = banks[("ffn", mc)]
                                bwait(tensor, ("ffn", mc))
                                for kc in range(DCH):
                                    inst = tensor.matmul(
                                        pb[:, 0:SLAB],
                                        w_lhsT(t_wffn, D, kc, mc),
                                        t_xq[:, kc * SLAB:(kc + 1) * SLAB],
                                        start=(kc == 0), stop=(kc == DCH - 1))
                                sT.inc(inst, 1)
                        if last:
                            return
                        if pv is not None:
                            tensor.wait_ge(sV.h, pv["v_hs"])
                        nch_s = SLAB // C1
                        for nco in range(SCH):
                            for l in range(C1):
                                for sc in range(SCH):
                                    uv = t_u[:, TLOC * sc + s * SLAB:
                                             TLOC * sc + (s + 1) * SLAB]
                                    uv = uv.rearrange("p (ch c) -> p c ch",
                                                      c=C1)[:, 0:C1 - l, :]
                                    ov = pHS[:, nco * 256 + l * nch_s:
                                             (nco + 1) * 256]
                                    st = (nco == 0 and l == 0 and sc == 0)
                                    inst = tensor.matmul(
                                        ov, m_lhsT(Q(l), sc, nco), uv,
                                        start=st, stop=False)
                            for j in range(C1):
                                for sc in range(SCH):
                                    ov = pHS[:, nco * 256 + j * nch_s:
                                             nco * 256 + (j + 1) * nch_s]
                                    bv = t_ball[:, sc * 256:(sc + 1) * 256]
                                    bv = bv.rearrange("p (j K) -> p K j",
                                                      K=NCH2)[:, 2 * s:2 * s + 2, :]
                                    inst = tensor.matmul(
                                        ov, m_lhsT(P(j + 1), sc, nco), bv,
                                        start=False,
                                        stop=(nco == SCH - 1 and j == C1 - 1
                                              and sc == SCH - 1))
                        sT.inc(inst, 1)
                        tensor.wait_ge(sV.h, ev["v_hs"])
                        for mc in range(DCH):
                            pb = banks[("cd", mc)]
                            bwait(tensor, ("cd", mc))
                            for sc in range(SCH):
                                hv = t_hs[:, sc * SLAB:(sc + 1) * SLAB]
                                hv = hv.rearrange("p (c ch) -> p ch c", c=C1)
                                tensor.matmul(
                                    pb[:, 0:SLAB], w_lhsT(t_ct, D, sc, mc),
                                    hv, start=(sc == 0), stop=False)
                            for sc in range(SCH):
                                inst = tensor.matmul(
                                    pb[:, 0:SLAB], w_lhsT(t_dt, D, sc, mc),
                                    t_u[:, TLOC * sc + s * SLAB:
                                        TLOC * sc + (s + 1) * SLAB],
                                    start=False, stop=(sc == SCH - 1))
                            sT.inc(inst, 1)
                        tensor.wait_ge(sV.h, ev["v_gysplit"])
                        for mc in range(DCH):
                            pb = banks[("wout", mc)]
                            bwait(tensor, ("wout", mc))
                            for kc in range(DCH):
                                tensor.matmul(
                                    pb[:, 0:SLAB], w_lhsT(t_wohi, D, kc, mc),
                                    t_bf[:, kc * SLAB:(kc + 1) * SLAB],
                                    start=(kc == 0), stop=False)
                            for kc in range(DCH):
                                tensor.matmul(
                                    pb[:, 0:SLAB], w_lhsT(t_wolo, D, kc, mc),
                                    t_bf[:, kc * SLAB:(kc + 1) * SLAB],
                                    start=False, stop=False)
                            for kc in range(DCH):
                                inst = tensor.matmul(
                                    pb[:, 0:SLAB], w_lhsT(t_wohi, D, kc, mc),
                                    t_lo[:, kc * SLAB:(kc + 1) * SLAB],
                                    start=False, stop=(kc == DCH - 1))
                            sT.inc(inst, 1)
                        if pv is not None:
                            tensor.wait_ge(sV.h, ev["v_ln2a"])
                        tensor.wait_ge(sV.h, ev[("v_x2", DCH - 1)])
                        tensor.wait_ge(sC.h, ev[("c_sq", DCH - 1)])
                        for kc in range(DCH):
                            tensor.matmul(
                                pST[:, 0:256], t_ones[:],
                                t_x2[:, kc * SLAB:(kc + 1) * SLAB],
                                start=(kc == 0), stop=(kc == DCH - 1))
                        for kc in range(DCH):
                            inst = tensor.matmul(
                                pST[:, 256:512], t_ones[:],
                                t_gy[:, kc * SLAB:(kc + 1) * SLAB],
                                start=(kc == 0), stop=(kc == DCH - 1))
                        sT.inc(inst, 1)

                    @blk.gpsimd
                    def _(gpsimd):
                        if pv is not None:
                            gpsimd.wait_ge(sV.h, ev[("v_fres", DCH - 1)])
                            dst = out_d.rearrange("(kc p) t -> p kc t", p=128)[
                                :, :, (s - 1) * SLAB:s * SLAB]
                            dma(gpsimd, sO, dst, t_x2[:])
                        if DEBUG and s == 1:
                            gpsimd.wait_ge(sC.h, ev[("c_sig", DCH - 1)])
                            dma(gpsimd, qG, dbg["d_g"][:], t_gy[:])
                            gpsimd.wait_ge(sV.h, ev["v_hs"])
                            dma(gpsimd, qG, dbg["d_hs"][:], t_hs[:])
                            gpsimd.wait_ge(sV.h, ev[("v_x2", DCH - 1)])
                            dma(gpsimd, qG, dbg["d_x2"][:], t_x2[:])

                    assert sT.v == ev["t_end"], (s, sT.v, ev["t_end"])
                    assert sV.v == ev["v_end"], (s, sV.v, ev["v_end"])
                    assert sC.v == ev["c_end"], (s, sC.v, ev["c_end"])

                    nxt = dict(ev)
                    nxt["t_bcast_prev_cnt"] = ev.get("t_bcast", sT.v)
                    nxt["prev_x2"] = (pv or {}).get("x2_cnt_for_ring", None)
                    nxt["x2_cnt_for_ring"] = ev.get(("v_x2", DCH - 1))
                    prev = nxt

            with nc.Block() as blk:
                @blk.gpsimd
                def _(gpsimd):
                    gpsimd.wait_ge(sO.h, sO.v)

    return nc


_NC_CACHE = None


def kernel(**inputs):
    global _NC_CACHE
    x = np.asarray(inputs["L0_attn_x"], dtype=np.float32)
    w_in = np.ascontiguousarray(inputs["L0_attn_W_in"], dtype=np.float32)
    b_in = np.asarray(inputs["L0_attn_b_in"], dtype=np.float32)
    w_gate = np.ascontiguousarray(inputs["L0_attn_W_gate"], dtype=np.float32)
    b_gate = np.asarray(inputs["L0_attn_b_gate"], dtype=np.float32)
    w_out = np.ascontiguousarray(inputs["L0_attn_W_out"], dtype=np.float32)
    b_out = np.asarray(inputs["L0_attn_b_out"], dtype=np.float32)
    a_m = np.asarray(inputs["L0_attn_A"], dtype=np.float32)
    bm_m = np.asarray(inputs["L0_attn_Bm"], dtype=np.float32)
    c_m = np.asarray(inputs["L0_attn_C"], dtype=np.float32)
    d_m = np.asarray(inputs["L0_attn_D"], dtype=np.float32)
    tern = np.asarray(inputs["L0_ffn_ternary_weight"], dtype=np.int32)
    b_ffn = np.asarray(inputs["L0_ffn_bias"], dtype=np.float32)
    act_scale = np.asarray(inputs["L0_ffn_act_scale"], dtype=np.float32)

    if _NC_CACHE is None:
        _NC_CACHE = build_nc()
    nc = _NC_CACHE

    shared = {
        "w_in": w_in,
        "w_gate": w_gate,
        "w_out": w_out,
        "ct": np.ascontiguousarray(c_m.T),
        "dtw": np.ascontiguousarray(d_m.T),
        "a_n": np.ascontiguousarray(a_m),
        "a_t": np.ascontiguousarray(a_m.T),
        "bm": np.ascontiguousarray(bm_m),
        "bm_t": np.ascontiguousarray(bm_m.T),
        "id128": np.eye(128, dtype=np.float32),
        "ternt": np.ascontiguousarray(tern.T),
        "b_in": np.ascontiguousarray(b_in.reshape(SCH, 128).T),
        "b_gate": np.ascontiguousarray(b_gate.reshape(DCH, 128).T),
        "b_out": np.ascontiguousarray(b_out.reshape(DCH, 128).T),
        "b_ffn": np.ascontiguousarray(b_ffn.reshape(DCH, 128).T),
        "scal": act_scale.reshape(1, 1),
    }
    zeros_f = np.zeros((D, TLOC), np.float32)
    zeros_t = np.zeros((TLOC, D), np.float32)
    in_maps = []
    for core in range(8):
        b = core // 2
        half = core % 2
        xs = x[b, half * TLOC:(half + 1) * TLOC, :]
        m = dict(shared)
        m["xf"] = np.ascontiguousarray(xs.T)
        m["xt"] = np.ascontiguousarray(xs)
        if half == 1:
            ps = x[b, 0:TLOC, :]
            m["pf"] = np.ascontiguousarray(ps.T)
            m["pt"] = np.ascontiguousarray(ps)
            m["pmask"] = np.ones((128, 1), np.float32)
        else:
            m["pf"] = zeros_f
            m["pt"] = zeros_t
            m["pmask"] = np.zeros((128, 1), np.float32)
        in_maps.append(m)

    res = run_bass_kernel_spmd(nc, in_maps, core_ids=list(range(8)))
    out = np.empty((B, T, D), np.float32)
    for core in range(8):
        b = core // 2
        half = core % 2
        out[b, half * TLOC:(half + 1) * TLOC, :] = res.results[core]["out"].T
    if DEBUG:
        kernel.last_results = res
    return out
